# version 05
# speedup vs baseline: 1.0288x; 1.0022x over previous
import numpy as np
import ml_dtypes
import concourse.bass as bass
import concourse.mybir as mybir
from concourse.bass_utils import run_bass_kernel_spmd
from concourse.alu_op_type import AluOpType as OP
from contextlib import ExitStack

F32 = mybir.dt.float32
BF16 = mybir.dt.bfloat16
F16 = mybir.dt.float16
U32 = mybir.dt.uint32
AF = mybir.ActivationFunctionType
AX = mybir.AxisListType

D = 1024
NCH = 8
T = 2048
NEG = -1.0e38
SC_ATTN0 = 128.0 ** -0.5
ARENA_BYTES = 114 * 1024

DT_BYTES = {F32: 4, BF16: 2, F16: 2, U32: 4}


def build_program(taps=(), gru_T=T, blocks=("l0a", "l0f", "l1a", "l1f")):
    nc = bass.Bass("TRN2", target_bir_lowering=False)
    dram = {}

    def din(name, shape, dt=F32):
        dram[name] = nc.dram_tensor(name, shape, dt, kind="ExternalInput").ap()

    def dout(name, shape, dt=F32):
        dram[name] = nc.dram_tensor(name, shape, dt, kind="ExternalOutput").ap()

    din("x0", [128, NCH, T])
    din("l0a_wv", [128, NCH, 1024])
    din("l0a_wqk", [128, 16, NCH, 128])
    din("l0a_gamma", [128, NCH])
    din("l0a_wout", [128, NCH, D])
    din("l0f_ctxw", [128, NCH, D])
    din("l0f_ctxb", [128, NCH])
    din("l0f_wih", [128, NCH, 3072])
    din("l0f_whh", [128, NCH, 3072])
    din("l0f_expr", [128, NCH, D])
    din("l0f_gate", [9, 4, 128, NCH, 512])
    din("l0f_up", [9, 4, 128, NCH, 512])
    din("l0f_down", [9, 4, 128, 4, D])
    din("sel8", [128, NCH, 36])
    din("l1f_wr", [128, NCH, D])
    din("l1f_wk", [128, NCH, D])
    din("l1f_wv", [128, NCH, D])
    din("l1f_mur", [128, NCH])
    din("l1f_muk", [128, NCH])
    din("l1a_wqk", [128, NCH, D])
    din("l1a_wv", [128, NCH, D])
    din("l1a_wout", [128, NCH, D])
    din("l1a_rot", [128, 128])
    din("l1a_cols", [128, 6, NCH])
    din("l1a_bff1", [128, 32])
    din("l1a_wff1", [128, NCH, 4096])
    din("l1a_wff2", [128, 32, D])
    nsl = gru_T // 128
    dram["s_xw"] = nc.dram_tensor("s_xw", [nsl, 128, 3072], F32, kind="Internal").ap()
    dram["s_hs"] = nc.dram_tensor("s_hs", [nsl, 128, NCH, 128], F32, kind="Internal").ap()
    dram["s_ob"] = nc.dram_tensor("s_ob", [128, NCH, T], BF16, kind="Internal").ap()
    for _hs in range(12):
        dram[f"s_sort{_hs}"] = nc.dram_tensor(f"s_sort{_hs}", [8768, 256], F16,
                                              kind="Internal").ap()
        dram[f"s_so{_hs}"] = nc.dram_tensor(f"s_so{_hs}", [8320, 72], F32,
                                            kind="Internal").ap()
    dram["s_h"] = nc.dram_tensor("s_h", [32, 128, T], BF16, kind="Internal").ap()
    dout("y", [128, NCH, T])
    tapset = set(taps)
    if "l0a" in tapset:
        dout("dbg_l0a", [128, NCH, T])
    if "l0f" in tapset:
        dout("dbg_l0f", [128, NCH, T])
    if "hs" in tapset:
        pass
    if "xa" in tapset or "x1" in tapset:
        dout("dbg_xa", [128, NCH, T], BF16)
    if "obuf" in tapset:
        dout("dbg_obuf", [128, NCH, T], BF16)
    if "rk" in tapset:
        dout("dbg_rkc", [128, 16, 4], U32)
        dout("dbg_bt", [128, 16, 4])
        dout("dbg_tie", [128, 16, 4])
        dout("dbg_rows", [1, 2 * T])
        dout("dbg_sa", [128, T], F16)
        dout("dbg_sb", [128, T], F16)
    if "vtok" in tapset:
        dout("dbg_vtok", [128, 16, D], BF16)

    with ExitStack() as ctx:
        B = Builder(nc, ctx, dram, tapset)
        B.body(gru_T, blocks)
    return nc


class Ctr:
    def __init__(self, sem):
        self.sem = sem
        self.v = 0

    def inc(self, n=1):
        self.v += n
        return self.v


class Builder:
    def __init__(self, nc, ctx, dram, taps):
        self.nc, self.ctx, self.dram, self.taps = nc, ctx, dram, taps
        self.blk = ctx.enter_context(nc.Block())
        self._arn_off = 0

    def sbt(self, name, shape, dt=F32):
        return self.ctx.enter_context(self.nc.sbuf_tensor(name, shape, dt))

    def ps_(self, name, shape, dt=F32):
        return self.ctx.enter_context(self.nc.psum_tensor(name, shape, dt))

    def sem(self, name):
        return Ctr(self.ctx.enter_context(self.nc.semaphore(name)))

    def av(self, off_kb, shape, dt=F32):
        nbytes = int(np.prod(shape)) * DT_BYTES[dt]
        off = int(off_kb * 1024)
        assert off + nbytes <= ARENA_BYTES, (off_kb, shape, dt)
        v = self.arena[:, off:off + nbytes].bitcast(dt)
        if len(shape) == 1:
            return v
        pat = "p (" + " ".join(f"d{i}" for i in range(len(shape))) + ") -> p " + \
              " ".join(f"d{i}" for i in range(len(shape)))
        kw = {f"d{i}": s for i, s in enumerate(shape)}
        return v.rearrange(pat, **kw)

    def te(self, fn):
        self.blk.tensor(fn)

    def ve(self, fn):
        self.blk.vector(fn)

    def ae(self, fn):
        self.blk.scalar(fn)

    def ge(self, fn):
        self.blk.gpsimd(fn)

    def sy(self, fn):
        self.blk.sync(fn)

    ROWP = [0, 32, 64, 0, 32, 64]

    def R(self, i, sl=None):
        p = self.ROWP[i]
        base = (i // 3) * T
        r = self.rows[p:p + 1, base:base + T]
        return r if sl is None else r[:, sl]

    def Rp(self, i):
        return self.ROWP[i]

    def body(self, gru_T, blocks):
        self.x = x = self.sbt("x", [128, NCH, T])
        self.rows = self.sbt("rows", [65, 2 * T])
        self.tmp = self.sbt("tmpf", [128, T])
        self.ident = self.sbt("ident", [128, 128])
        self.identb = self.sbt("identb", [128, 128], BF16)
        self.ones1 = self.sbt("ones1", [65, 128])
        self.onesb = self.sbt("onesb", [128, 2], BF16)
        self.rcol = self.sbt("rcol", [128, 8])
        self.cb = self.sbt("cb", [128, 2])
        self.iota8 = self.sbt("iota8", [128, 8])
        self.rsel = self.sbt("rsel", [8, 1024], BF16)
        self.mask = self.sbt("maskd", [128, 128])
        self.iotap = self.sbt("iotap", [128, 1])
        self.iotap128 = self.sbt("iotap128", [128, 1])
        self.iotap2 = self.sbt("iotap2", [128, 1])
        self.dmp4 = self.sbt("dmp4", [128, 4])
        self.e16 = self.sbt("e16", [128, 16], F16)
        self.arena = self.sbt("arena", [128, ARENA_BYTES], mybir.dt.uint8)

        self.psA = self.ps_("psA", [128, 512])
        self.psB = self.ps_("psB", [128, 512])
        self.psW = self.ps_("psW", [128, 2048])
        self.psT = self.ps_("psT", [128, 128], BF16)
        self.psO = self.ps_("psO", [128, 512])

        self.s_ld = self.sem("s_ld")
        self.s_w = self.sem("s_w")
        self.s_pe = self.sem("s_pe")
        self.s_v = self.sem("s_v")
        self.s_a = self.sem("s_a")
        self.s_g = self.sem("s_g")
        self.s_out = self.sem("s_out")
        self.s_wq = [self.sem("s_wq0"), self.sem("s_wq1")]
        s_ld, s_v, s_g = self.s_ld, self.s_v, self.s_g

        @self.sy
        def _(sy):
            sy.dma_start(out=x[:], in_=self.dram["x0"][:]).then_inc(s_ld.sem, 16)
        s_ld.inc(16)

        @self.ge
        def _(g):
            g.iota(self.ident[:], pattern=[[-1, 128]], base=0, channel_multiplier=1,
                   allow_small_or_imprecise_dtypes=True)
            g.drain()
            g.iota(self.mask[:], pattern=[[1, 128]], base=0, channel_multiplier=-1,
                   allow_small_or_imprecise_dtypes=True)
            g.drain()
            g.iota(self.iota8[:], pattern=[[1, 8]], base=0, channel_multiplier=0,
                   allow_small_or_imprecise_dtypes=True)
            g.drain()
            g.iota(self.rsel[:].rearrange("p (a b) -> p a b", a=8),
                   pattern=[[1, 8], [0, 128]], base=0, channel_multiplier=-1,
                   allow_small_or_imprecise_dtypes=True)
            g.drain()
            g.iota(self.iotap[:], pattern=[[0, 1]], base=0, channel_multiplier=1,
                   allow_small_or_imprecise_dtypes=True).then_inc(s_g.sem, 1)
        s_g.inc()

        @self.ve
        def _(v):
            v.memset(self.ones1[:], 1.0)
            v.memset(self.onesb[:], 1.0)
            v.memset(self.cb[:, 0:1], 1e-5)
            v.memset(self.cb[:, 1:2], 1.0)
            v.wait_ge(s_g.sem, s_g.v)
            v.tensor_scalar(out=self.ident[:], in0=self.ident[:], scalar1=0.0,
                            scalar2=None, op0=OP.is_equal, op1=OP.bypass)
            v.drain()
            v.tensor_copy(out=self.identb[:], in_=self.ident[:])
            v.tensor_scalar(out=self.rsel[:], in0=self.rsel[:], scalar1=0.0,
                            scalar2=None, op0=OP.is_equal, op1=OP.bypass)
            v.tensor_scalar(out=self.mask[:], in0=self.mask[:], scalar1=0.0,
                            scalar2=NEG, op0=OP.is_gt, op1=OP.mult)
            v.drain()
            v.tensor_scalar(out=self.iotap128[:], in0=self.iotap[:], scalar1=128.0,
                            scalar2=None, op0=OP.add, op1=OP.bypass)
            v.tensor_scalar(out=self.iotap2[:], in0=self.iotap[:], scalar1=8256.0,
                            scalar2=None, op0=OP.add, op1=OP.bypass)
            v.drain()
            v.tensor_scalar(out=self.dmp4[:], in0=self.iota8[:, 0:4],
                            scalar1=128.0, scalar2=self.iotap2[:, 0:1],
                            op0=OP.mult, op1=OP.add)
            v.memset(self.e16[:], 0.0)
            v.drain()
            for r in range(4):
                v.memset(self.e16[:, 5 * r:5 * r + 1], 1.0)
            v.drain()
            v.sem_inc(s_v.sem, 1)
        s_v.inc()

        if "l0a" in blocks:
            self.l0_attn()
        if "l0f" in blocks:
            self.l0_ffn(gru_T)
        if "l1a" in blocks:
            self.l1_attn()
        if "l1f" in blocks:
            self.l1_ffn()

        @self.sy
        def _(sy):
            sy.wait_ge(self.s_v.sem, self.s_v.v)
            sy.dma_start(out=self.dram["y"][:], in_=x[:]).then_inc(self.s_out.sem, 16)
            sy.wait_ge(self.s_out.sem, self.s_out.inc(16))

    def ln_rows(self, xb, xsq, r_mean=3, r_A=0, r_B=1, l2mode=False, src=None):
        x = self.x if src is None else src
        s_v, s_a, s_pe = self.s_v, self.s_a, self.s_pe
        psS = self.psW
        R = self.R

        @self.ve
        def _(v):
            v.wait_ge(self.s_ld.sem, self.s_ld.v)
            v.wait_ge(s_v.sem, s_v.v)
            for c in range(NCH):
                v.tensor_copy(out=xb[:, c, :], in_=x[:, c, :])
            v.drain()
            v.sem_inc(s_v.sem, 1)
        s_v.inc()

        @self.ae
        def _(a):
            a.wait_ge(s_v.sem, s_v.v)
            for c in range(NCH):
                a.activation(out=xsq[:, c, :], in_=x[:, c, :], func=AF.Square)
            a.drain()
            a.sem_inc(s_a.sem, 1)
        s_a.inc()

        @self.te
        def _(te):
            te.wait_ge(s_a.sem, s_a.v)
            te.wait_ge(s_pe.sem, s_pe.v)
            for tg in range(4):
                sl = slice(512 * tg, 512 * (tg + 1))
                for c in range(NCH):
                    te.matmul(psS[0:1, sl], self.onesb[:, 0:1], xb[:, c, sl],
                              start=(c == 0), stop=(c == NCH - 1))
                for c in range(NCH):
                    mm = te.matmul(psS[32:33, sl], self.onesb[:, 0:1], xsq[:, c, sl],
                                   start=(c == 0), stop=(c == NCH - 1))
            mm.then_inc(s_pe.sem, 1)
        s_pe.inc()

        self.rows_finish(r_mean, r_A, r_B, l2mode)

    def rows_finish(self, r_mean=3, r_A=0, r_B=1, l2mode=False):
        s_v, s_a, s_pe = self.s_v, self.s_a, self.s_pe
        psS = self.psW
        R = self.R
        rv, rt = 4, 5

        @self.ve
        def _(v):
            v.wait_ge(s_pe.sem, s_pe.v)
            v.tensor_scalar(out=R(r_mean), in0=psS[0:1, :], scalar1=1.0 / D,
                            scalar2=None, op0=OP.mult, op1=OP.bypass)
            v.drain()
            v.tensor_tensor(out=R(rv), in0=R(r_mean), in1=R(r_mean), op=OP.mult)
            v.drain()
            v.scalar_tensor_tensor(out=R(rv), in0=psS[32:33, :], scalar=1.0 / D,
                                   in1=R(rv), op0=OP.mult, op1=OP.subtract)
            v.drain()
            v.tensor_scalar(out=R(rv), in0=R(rv), scalar1=0.0, scalar2=None,
                            op0=OP.max, op1=OP.bypass).then_inc(s_v.sem, 1)
        s_v.inc()

        @self.ae
        def _(a):
            a.wait_ge(s_v.sem, s_v.v)
            a.activation(out=R(rt), in_=R(rv), func=AF.Sqrt,
                         bias=self.cb[32:33, 0:1]).then_inc(s_a.sem, 1)
        s_a.inc()

        @self.ve
        def _(v):
            v.wait_ge(s_a.sem, s_a.v)
            v.reciprocal(out=R(r_A), in_=R(rt)).then_inc(s_v.sem, 1)
        s_v.inc()

        if l2mode:
            t0 = self.tmp[0:1, 0:T]

            @self.ae
            def _(a):
                a.wait_ge(s_v.sem, s_v.v)
                a.activation(out=t0, in_=R(rv), func=AF.Sqrt).then_inc(s_a.sem, 1)
            s_a.inc()

            @self.ve
            def _(v):
                v.wait_ge(s_a.sem, s_a.v)
                v.tensor_tensor(out=t0, in0=t0, in1=R(r_A), op=OP.mult)
                v.drain()
                v.tensor_scalar(out=t0, in0=t0, scalar1=32.0, scalar2=1e-12,
                                op0=OP.mult, op1=OP.max)
                v.drain()
                v.reciprocal(out=t0, in_=t0)
                v.drain()
                v.tensor_tensor(out=R(r_A), in0=R(r_A), in1=t0, op=OP.mult)
                v.drain()
                v.tensor_scalar(out=R(r_A), in0=R(r_A), scalar1=32.0, scalar2=None,
                                op0=OP.mult, op1=OP.bypass).then_inc(s_v.sem, 1)
            s_v.inc()

        @self.ve
        def _(v):
            v.wait_ge(s_v.sem, s_v.v)
            v.tensor_tensor(out=R(r_B), in0=R(r_mean), in1=R(r_A), op=OP.mult)
            v.drain()
            v.tensor_scalar(out=R(r_B), in0=R(r_B), scalar1=-1.0, scalar2=None,
                            op0=OP.mult, op1=OP.bypass).then_inc(s_v.sem, 1)
        s_v.inc()

    def apply_affine(self, dst_bf, r_A, r_B, percol=None):
        x, s_v, s_pe = self.x, self.s_v, self.s_pe
        for tg in range(4):
            sl = slice(512 * tg, 512 * (tg + 1))

            @self.te
            def _(te, tg=tg, sl=sl):
                te.wait_ge(s_v.sem, s_v.v)
                pa, pb = self.Rp(r_A), self.Rp(r_B)
                te.matmul(self.psA[:], self.ones1[pa:pa + 1, :], self.R(r_A, sl),
                          start=True, stop=True)
                te.matmul(self.psB[:], self.ones1[pb:pb + 1, :], self.R(r_B, sl),
                          start=True, stop=True).then_inc(s_pe.sem, 1)
            s_pe.inc()

            @self.ve
            def _(v, tg=tg, sl=sl):
                v.wait_ge(s_pe.sem, s_pe.v)
                for c in range(NCH):
                    v.tensor_tensor(out=self.tmp[:, 0:512], in0=x[:, c, sl],
                                    in1=self.psA[:], op=OP.mult)
                    v.drain()
                    if percol is None:
                        v.tensor_tensor(out=dst_bf[:, c, sl], in0=self.tmp[:, 0:512],
                                        in1=self.psB[:], op=OP.add)
                    else:
                        v.tensor_tensor(out=self.tmp[:, 0:512], in0=self.tmp[:, 0:512],
                                        in1=self.psB[:], op=OP.add)
                        v.drain()
                        v.tensor_scalar(out=dst_bf[:, c, sl], in0=self.tmp[:, 0:512],
                                        scalar1=percol[:, c:c + 1], scalar2=None,
                                        op0=OP.mult, op1=OP.bypass)
                    v.drain()
                v.sem_inc(s_v.sem, 1)
            s_v.inc()

    def l0_attn(self):
        x, dram = self.x, self.dram
        s_ld, s_w, s_pe, s_v, s_a = (self.s_ld, self.s_w, self.s_pe, self.s_v,
                                     self.s_a)
        xa = self.av(0, [NCH, T], BF16)
        vtok = self.av(32, [16, D], BF16)
        wv = self.av(64, [NCH, D], BF16)
        wqk = self.av(64, [2, 2, NCH, 128], BF16)
        qh = self.av(72, [T], BF16)
        kh = self.av(76, [T], BF16)
        pbuf = self.av(80, [T], BF16)
        ptb = self.av(84, [128], BF16)
        gam = self.av(84.5, [NCH], F32)
        obuf = self.av(85, [T], BF16)
        oprj = self.av(89, [NCH, 512], BF16)
        of32 = self.tmp
        rrow = self.R(2)
        rcol = self.rcol

        @self.ge
        def _(g):
            g.dma_start(out=wv[:], in_=dram["l0a_wv"][:]).then_inc(s_w.sem, 16)
            g.dma_start(out=gam[:], in_=dram["l0a_gamma"][:]).then_inc(s_w.sem, 16)
        s_w.inc(32)

        self.ln_rows(xb=self.av(32, [NCH, T], BF16), xsq=xa, l2mode=True)

        @self.ve
        def _(v):
            v.wait_ge(s_w.sem, s_w.v)
            v.drain()
            v.sem_inc(s_v.sem, 1)
        s_v.inc()
        self.apply_affine(xa, 0, 1, percol=gam)

        if "xa" in self.taps:
            s_dx = self.sem("s_dxa")

            @self.sy
            def _(sy):
                sy.wait_ge(s_v.sem, s_v.v)
                sy.dma_start(out=dram["dbg_xa"][:], in_=xa[:]).then_inc(s_dx.sem, 16)
                sy.wait_ge(s_dx.sem, 16)

        for tt in range(16):
            @self.te
            def _(te, tt=tt):
                te.wait_ge(s_v.sem, s_v.v)
                for c in range(NCH):
                    te.matmul(self.psA[:], xa[:, c, 128 * tt:128 * (tt + 1)],
                              wv[:, c, 0:512], start=(c == 0), stop=(c == NCH - 1))
                for c in range(NCH):
                    mm = te.matmul(self.psB[:], xa[:, c, 128 * tt:128 * (tt + 1)],
                                   wv[:, c, 512:1024], start=(c == 0),
                                   stop=(c == NCH - 1))
                mm.then_inc(s_pe.sem, 1)
            s_pe.inc()

            @self.ve
            def _(v, tt=tt):
                v.wait_ge(s_pe.sem, s_pe.v)
                v.tensor_copy(out=vtok[:, tt, 0:512], in_=self.psA[:])
                v.tensor_copy(out=vtok[:, tt, 512:1024], in_=self.psB[:]).then_inc(s_v.sem, 1)
            s_v.inc()
        vtok_done = s_v.v

        for h in range(8):
            par = h % 2

            s_wq = self.s_wq[par]

            @self.ge
            def _(g, h=h, par=par, vtok_done=vtok_done, s_wq=s_wq):
                if h >= 2:
                    g.wait_ge(s_v.sem, self._qk_used[h - 2])
                else:
                    g.wait_ge(s_v.sem, vtok_done)
                g.dma_start(out=wqk[:, par],
                            in_=dram["l0a_wqk"][:, 2 * h:2 * h + 2]
                            ).then_inc(s_wq.sem, 16)
            s_wq.inc(16)
            qk_ld = s_wq.v

            for tg in range(4):
                @self.te
                def _(te, h=h, tg=tg, par=par, qk_ld=qk_ld, s_wq=s_wq):
                    te.wait_ge(s_v.sem, s_v.v)
                    te.wait_ge(s_wq.sem, qk_ld)
                    slq = slice(512 * tg, 512 * (tg + 1))
                    for c in range(NCH):
                        te.matmul(self.psA[:], wqk[:, par, 0, c], xa[:, c, slq],
                                  start=(c == 0), stop=(c == NCH - 1))
                    for c in range(NCH):
                        mm = te.matmul(self.psB[:], wqk[:, par, 1, c], xa[:, c, slq],
                                       start=(c == 0), stop=(c == NCH - 1))
                    mm.then_inc(s_pe.sem, 1)
                s_pe.inc()

                @self.ve
                def _(v, tg=tg):
                    v.wait_ge(s_pe.sem, s_pe.v)
                    v.tensor_copy(out=qh[:, 512 * tg:512 * (tg + 1)], in_=self.psA[:])
                    v.tensor_copy(out=kh[:, 512 * tg:512 * (tg + 1)], in_=self.psB[:]).then_inc(s_v.sem, 1)
                s_v.inc()
            if not hasattr(self, "_qk_used"):
                self._qk_used = {}
            self._qk_used[h] = s_v.v

            for it in range(16):
                jend = 128 * (it + 1)
                ng = (jend + 511) // 512

                @self.te
                def _(te, it=it, jend=jend, ng=ng):
                    te.wait_ge(s_v.sem, s_v.v)
                    te.wait_ge(s_a.sem, s_a.v)
                    for g in range(ng):
                        n = min(512, jend - 512 * g)
                        mm = te.matmul(self.psW[:, 512 * g:512 * g + n],
                                       qh[:, 128 * it:128 * (it + 1)],
                                       kh[:, 512 * g:512 * g + n], start=True,
                                       stop=True)
                    mm.then_inc(s_pe.sem, 1)
                s_pe.inc()

                @self.ve
                def _(v, it=it, jend=jend):
                    v.wait_ge(s_pe.sem, s_pe.v)
                    js = 128 * it
                    v.tensor_tensor(out=self.psW[:, js:js + 128],
                                    in0=self.psW[:, js:js + 128],
                                    in1=self.mask[:], op=OP.add)
                    v.drain()
                    v.reduce_max(out=rcol[:, 0:1], in_=self.psW[:, 0:jend],
                                 axis=AX.X, negate=True)
                    v.drain()
                    v.tensor_scalar(out=rcol[:, 1:2], in0=rcol[:, 0:1],
                                    scalar1=SC_ATTN0, scalar2=None, op0=OP.mult,
                                    op1=OP.bypass).then_inc(s_v.sem, 1)
                s_v.inc()

                @self.ae
                def _(a, it=it, jend=jend):
                    a.wait_ge(s_v.sem, s_v.v)
                    a.activation(out=pbuf[:, 0:jend], in_=self.psW[:, 0:jend],
                                 func=AF.Exp, bias=rcol[:, 1:2], scale=SC_ATTN0,
                                 accum_out=rcol[:, 2:3]).then_inc(s_a.sem, 1)
                s_a.inc()

                @self.ve
                def _(v):
                    v.wait_ge(s_a.sem, s_a.v)
                    v.reciprocal(out=rcol[:, 3:4], in_=rcol[:, 2:3]).then_inc(s_v.sem, 1)
                s_v.inc()

                @self.te
                def _(te, it=it):
                    te.wait_ge(s_v.sem, s_v.v)
                    te.matmul(self.psO[0:1, 0:128], rcol[:, 3:4], self.ident[:],
                              start=True, stop=True).then_inc(s_pe.sem, 1)
                s_pe.inc()

                @self.ve
                def _(v, it=it):
                    v.wait_ge(s_pe.sem, s_pe.v)
                    v.tensor_copy(out=rrow[:, 128 * it:128 * (it + 1)],
                                  in_=self.psO[0:1, 0:128]).then_inc(s_v.sem, 1)
                s_v.inc()

                for jt in range(it + 1):
                    @self.te
                    def _(te, jt=jt):
                        te.wait_ge(s_a.sem, s_a.v)
                        if jt == 0:
                            te.wait_ge(s_v.sem, s_v.v)
                        te.transpose(self.psT[:], pbuf[:, 128 * jt:128 * (jt + 1)],
                                     self.identb[:]).then_inc(s_pe.sem, 1)
                    s_pe.inc()

                    @self.ve
                    def _(v, jt=jt):
                        v.wait_ge(s_pe.sem, s_pe.v)
                        v.tensor_copy(out=ptb[:], in_=self.psT[:]
                                      ).then_inc(s_v.sem, 1)
                    s_v.inc()

                    @self.te
                    def _(te, h=h, it=it, jt=jt):
                        te.wait_ge(s_v.sem, s_v.v)
                        te.matmul(self.psO[:, 0:128],
                                  vtok[:, jt, 128 * h:128 * (h + 1)],
                                  ptb[:], start=(jt == 0), stop=(jt == it)
                                  ).then_inc(s_pe.sem, 1)
                    s_pe.inc()

                @self.ve
                def _(v, it=it):
                    v.wait_ge(s_pe.sem, s_pe.v)
                    v.tensor_copy(out=of32[:, 128 * it:128 * (it + 1)],
                                  in_=self.psO[:, 0:128]).then_inc(s_v.sem, 1)
                s_v.inc()

            for tg in range(4):
                @self.te
                def _(te, tg=tg, h=h):
                    te.wait_ge(s_v.sem, s_v.v)
                    if h > 0 and tg == 0:
                        te.wait_ge(self.s_ld.sem, self._ob_st[h - 1])
                    te.matmul(self.psA[:], self.ones1[64:65, :],
                              rrow[:, 512 * tg:512 * (tg + 1)], start=True,
                              stop=True).then_inc(s_pe.sem, 1)
                s_pe.inc()

                @self.ve
                def _(v, h=h, tg=tg):
                    v.wait_ge(s_pe.sem, s_pe.v)
                    v.tensor_tensor(out=obuf[:, 512 * tg:512 * (tg + 1)],
                                    in0=of32[:, 512 * tg:512 * (tg + 1)],
                                    in1=self.psA[:], op=OP.mult).then_inc(s_v.sem, 1)
                s_v.inc()

            if not hasattr(self, "_ob_st"):
                self._ob_st = {}

            @self.sy
            def _(sy, h=h):
                sy.wait_ge(s_v.sem, s_v.v)
                sy.dma_start(out=dram["s_ob"][:, h, :], in_=obuf[:]
                             ).then_inc(self.s_ld.sem, 16)
            self.s_ld.inc(16)
            self._ob_st[h] = self.s_ld.v

        if "obuf" in self.taps:
            s_do = self.sem("s_dob")

            @self.sy
            def _(sy):
                sy.wait_ge(s_v.sem, s_v.v)
                sy.dma_start(out=dram["dbg_obuf"][:], in_=obuf[:]).then_inc(s_do.sem, 16)
                sy.wait_ge(s_do.sem, 16)

        s_dv = None
        if "vtok" in self.taps:
            s_dv = self.sem("s_dvt")

            @self.sy
            def _(sy, vtok_done=vtok_done):
                sy.wait_ge(s_v.sem, vtok_done)
                sy.dma_start(out=dram["dbg_vtok"][:], in_=vtok[:]).then_inc(s_dv.sem, 16)
                sy.wait_ge(s_dv.sem, 16)

        wout = self.av(32, [NCH, D], BF16)

        @self.ge
        def _(g, s_dv=s_dv):
            g.wait_ge(s_v.sem, s_v.v)
            if s_dv is not None:
                g.wait_ge(s_dv.sem, 16)
            g.dma_start(out=wout[:], in_=dram["l0a_wout"][:]).then_inc(s_w.sem, 16)
        s_w.inc(16)

        s_op = self.sem("s_oprj")
        for tg in range(4):
            sl = slice(512 * tg, 512 * (tg + 1))

            @self.sy
            def _(sy, tg=tg, sl=sl):
                sy.wait_ge(self.s_ld.sem, self._ob_st[7])
                if tg > 0:
                    sy.wait_ge(s_v.sem, self._op_used)
                sy.dma_start(out=oprj[:], in_=dram["s_ob"][:, :, sl]
                             ).then_inc(s_op.sem, 16)
            s_op.inc(16)

            for o in range(NCH):
                @self.te
                def _(te, tg=tg, o=o, sl=sl):
                    te.wait_ge(s_v.sem, s_v.v)
                    te.wait_ge(s_w.sem, s_w.v)
                    te.wait_ge(s_op.sem, s_op.v)
                    pd = self.psA if o % 2 == 0 else self.psB
                    for c in range(NCH):
                        mm = te.matmul(pd[:], wout[:, c, 128 * o:128 * (o + 1)],
                                       oprj[:, c, :], start=(c == 0),
                                       stop=(c == NCH - 1))
                    mm.then_inc(s_pe.sem, 1)
                s_pe.inc()

                @self.ve
                def _(v, tg=tg, o=o, sl=sl):
                    v.wait_ge(s_pe.sem, s_pe.v)
                    pd = self.psA if o % 2 == 0 else self.psB
                    v.tensor_tensor(out=x[:, o, sl], in0=x[:, o, sl], in1=pd[:],
                                    op=OP.add).then_inc(s_v.sem, 1)
                s_v.inc()
                self._op_used = s_v.v

        if "l0a" in self.taps:
            s_dbg = self.sem("s_dbg")

            @self.sy
            def _(sy):
                sy.wait_ge(s_v.sem, s_v.v)
                sy.dma_start(out=dram["dbg_l0a"][:], in_=x[:]).then_inc(s_dbg.sem, 16)
                sy.wait_ge(s_dbg.sem, 16)


    def l0_ffn(self, gru_T):
        x, dram = self.x, self.dram
        s_ld, s_w, s_pe, s_v, s_a, s_g = (self.s_ld, self.s_w, self.s_pe,
                                          self.s_v, self.s_a, self.s_g)
        NSL = gru_T // 128
        hid = self.av(0, [NCH, T], BF16)
        self.ln_rows(xb=self.av(32, [NCH, T], BF16),
                     xsq=self.av(64, [NCH, T], BF16))
        self.apply_affine(hid, 0, 1)

        ctxw = self.av(32, [NCH, D], BF16)
        ctxb = self.av(112.0, [NCH], F32)
        mvec = self.av(112.125, [NCH], BF16)
        h_fp = self.av(112.25, [NCH], F32)
        h_bf = self.av(112.375, [NCH], BF16)

        @self.ge
        def _(g):
            g.wait_ge(s_v.sem, s_v.v)
            g.dma_start(out=ctxw[:], in_=dram["l0f_ctxw"][:]).then_inc(s_w.sem, 16)
        s_w.inc(16)

        @self.sy
        def _(sy):
            sy.wait_ge(s_v.sem, s_v.v)
            sy.dma_start(out=ctxb[:], in_=dram["l0f_ctxb"][:]).then_inc(s_ld.sem, 16)
        s_ld.inc(16)

        @self.ve
        def _(v):
            v.wait_ge(s_v.sem, s_v.v)
            for c in range(NCH):
                v.tensor_reduce(out=self.rcol[:, 0:1], in_=hid[:, c, :], axis=AX.X,
                                op=OP.add)
                v.drain()
                v.tensor_scalar(out=mvec[:, c:c + 1], in0=self.rcol[:, 0:1],
                                scalar1=1.0 / T, scalar2=None, op0=OP.mult,
                                op1=OP.bypass)
                v.drain()
            v.sem_inc(s_v.sem, 1)
        s_v.inc()

        @self.te
        def _(te):
            te.wait_ge(s_v.sem, s_v.v)
            te.wait_ge(s_w.sem, s_w.v)
            for o in range(NCH):
                for c in range(NCH):
                    mm = te.matmul(self.psO[:, o:o + 1],
                                   ctxw[:, c, 128 * o:128 * (o + 1)],
                                   mvec[:, c:c + 1], start=(c == 0),
                                   stop=(c == NCH - 1))
            mm.then_inc(s_pe.sem, 1)
        s_pe.inc()

        @self.ve
        def _(v):
            v.wait_ge(s_pe.sem, s_pe.v)
            v.wait_ge(self.s_ld.sem, self.s_ld.v)
            v.tensor_tensor(out=h_fp[:], in0=self.psO[:, 0:NCH], in1=ctxb[:],
                            op=OP.add)
            v.drain()
            v.tensor_copy(out=h_bf[:], in_=h_fp[:]).then_inc(s_v.sem, 1)
        s_v.inc()

        wih = self.av(32, [NCH, 3072], BF16)
        xst = [self.av(80, [24, 128], F32), self.av(92, [24, 128], F32)]
        s_xs = [self.sem("s_xs0"), self.sem("s_xs1")]

        @self.ge
        def _(g):
            g.wait_ge(s_v.sem, s_v.v)
            g.dma_start(out=wih[:], in_=dram["l0f_wih"][:]).then_inc(s_w.sem, 16)
        s_w.inc(16)
        self._xst_free = {}

        for sl_i in range(NSL):
            par = sl_i % 2
            tsl = slice(128 * sl_i, 128 * (sl_i + 1))
            for ph, (o0, o1) in enumerate([(0, 16), (16, 24)]):
                @self.te
                def _(te, sl_i=sl_i, par=par, tsl=tsl, ph=ph, o0=o0, o1=o1):
                    te.wait_ge(s_w.sem, s_w.v)
                    te.wait_ge(s_v.sem, s_v.v)
                    if ph == 0 and sl_i >= 2:
                        te.wait_ge(s_xs[par].sem, self._xst_free[sl_i - 2])
                    for o in range(o0, o1):
                        for c in range(NCH):
                            mm = te.matmul(
                                self.psW[:, 128 * (o - o0):128 * (o - o0) + 128],
                                wih[:, c, 128 * o:128 * (o + 1)],
                                hid[:, c, tsl], start=(c == 0),
                                stop=(c == NCH - 1))
                    mm.then_inc(s_pe.sem, 1)
                s_pe.inc()

                @self.ve
                def _(v, par=par, o0=o0, o1=o1):
                    v.wait_ge(s_pe.sem, s_pe.v)
                    v.tensor_copy(out=xst[par][:, o0:o1, :],
                                  in_=self.psW[:, 0:128 * (o1 - o0)].rearrange(
                                      "p (a b) -> p a b", a=o1 - o0)).then_inc(s_v.sem, 1)
                s_v.inc()

            @self.sy
            def _(sy, sl_i=sl_i, par=par):
                sy.wait_ge(s_v.sem, s_v.v)
                sy.dma_start(out=dram["s_xw"][sl_i],
                             in_=xst[par][:].rearrange("p a b -> p (a b)")
                             ).then_inc(s_xs[par].sem, 16)
            s_xs[par].inc(16)
            self._xst_free[sl_i] = s_xs[par].v

        whh = self.av(32, [NCH, 3072], BF16)
        xin = [self.av(80, [24, 128], F32), self.av(92, [24, 128], F32)]
        hst = [self.av(104, [NCH, 128], F32), self.av(108, [NCH, 128], F32)]
        trz = self.av(112.5, [16], F32)
        tnn = self.av(112.625, [8], F32)
        grz = self.av(112.75, [16], F32)
        nwt = self.av(112.875, [8], F32)
        dlt = self.av(113.0, [8], F32)
        zbc = self.av(113.125, [8], F32)
        s_xi = [self.sem("s_xi0"), self.sem("s_xi1")]
        s_ho = [self.sem("s_ho0"), self.sem("s_ho1")]
        s_sh = self.sem("s_sh")
        s_srz = self.sem("s_srz")
        s_sn = self.sem("s_sn")
        s_sv1 = self.sem("s_sv1")
        s_sa1 = self.sem("s_sa1")
        s_sv2 = self.sem("s_sv2")
        s_sa2 = self.sem("s_sa2")
        s_hof = self.sem("s_hof")

        @self.ge
        def _(g):
            g.wait_ge(s_xs[0].sem, s_xs[0].v)
            g.wait_ge(s_xs[1].sem, s_xs[1].v)
            g.dma_start(out=whh[:], in_=dram["l0f_whh"][:]).then_inc(s_w.sem, 16)
        s_w.inc(16)
        whh_ld = s_w.v

        xs_done = [s_xs[0].v, s_xs[1].v]

        def emit_store(sy, sl_i):
            hpar = sl_i % 2
            sy.wait_ge(s_hof.sem, sl_i + 1)
            sy.dma_start(out=dram["s_hs"][sl_i],
                         in_=hst[hpar][:].rearrange("p a b -> p (a b)")
                         ).then_inc(s_ho[hpar].sem, 16)
            s_ho[hpar].inc(16)

        @self.sy
        def _(sy):
            for sl_i in range(NSL):
                par = sl_i % 2
                if sl_i < 2:
                    sy.wait_ge(s_xs[par].sem, xs_done[par])
                else:
                    sy.wait_ge(s_sv2.sem, 128 * (sl_i - 1))
                sy.dma_start(out=xin[par][:], in_=dram["s_xw"][sl_i]
                             ).then_inc(s_xi[par].sem, 16)
                if sl_i >= 2:
                    emit_store(sy, sl_i - 2)
            for sl_i in range(max(NSL - 2, 0), NSL):
                emit_store(sy, sl_i)

        @self.ve
        def _(v):
            v.wait_ge(s_v.sem, s_v.v)
            v.sem_inc(s_sh.sem, 1)
        s_sh.inc()

        @self.te
        def _(te):
            te.wait_ge(s_w.sem, whh_ld)
            for t in range(gru_T):
                te.wait_ge(s_sh.sem, t + 1)
                for o in range(24):
                    out = (self.psA[:, o:o + 1] if o < 16
                           else self.psB[:, o - 16:o - 15])
                    for c in range(NCH):
                        mm = te.matmul(out, whh[:, c, 128 * o:128 * (o + 1)],
                                       h_bf[:, c:c + 1], start=(c == 0),
                                       stop=(c == NCH - 1))
                    if o == 15:
                        mm.then_inc(s_srz.sem, 1)
                    elif o == 23:
                        mm.then_inc(s_sn.sem, 1)

        @self.ve
        def _(v):
            for t in range(gru_T):
                sl_i = t // 128
                trel = t % 128
                par = sl_i % 2
                xs = xin[par]
                hpar = sl_i % 2
                v.wait_ge(s_srz.sem, t + 1)
                if trel == 0:
                    v.wait_ge(s_xi[par].sem, 16 * (sl_i // 2 + 1))
                    if sl_i >= 2:
                        v.wait_ge(s_ho[hpar].sem, 16 * (sl_i // 2))
                v.tensor_tensor(out=trz[:], in0=self.psA[:, 0:16],
                                in1=xs[:, 0:16, trel], op=OP.add
                                ).then_inc(s_sv1.sem, 1)
                v.wait_ge(s_sa1.sem, t + 1)
                v.wait_ge(s_sn.sem, t + 1)
                v.tensor_tensor(out=tnn[:], in0=self.psB[:, 0:8], in1=grz[:, 0:8],
                                op=OP.mult)
                v.drain()
                v.tensor_tensor(out=tnn[:], in0=tnn[:], in1=xs[:, 16:24, trel],
                                op=OP.add).then_inc(s_sv2.sem, 1)
                v.tensor_tensor(out=dlt[:], in0=h_fp[:], in1=grz[:, 8:16],
                                op=OP.mult)
                v.tensor_scalar(out=zbc[:], in0=grz[:, 8:16], scalar1=-1.0,
                                scalar2=1.0, op0=OP.mult, op1=OP.add)
                v.drain()
                v.wait_ge(s_sa2.sem, t + 1)
                v.tensor_tensor(out=zbc[:], in0=nwt[:], in1=zbc[:], op=OP.mult)
                v.drain()
                v.tensor_tensor(out=h_fp[:], in0=zbc[:], in1=dlt[:], op=OP.add)
                v.drain()
                cp = v.tensor_copy(out=hst[hpar][:, :, trel], in_=h_fp[:])
                if trel == 127:
                    cp.then_inc(s_hof.sem, 1)
                v.tensor_copy(out=h_bf[:], in_=h_fp[:]).then_inc(s_sh.sem, 1)
        s_sh.inc(gru_T)
        s_srz.inc(gru_T)
        s_sn.inc(gru_T)
        s_sv1.inc(gru_T)
        s_sv2.inc(gru_T)
        s_hof.inc(NSL)

        @self.ae
        def _(a):
            for t in range(gru_T):
                a.wait_ge(s_sv1.sem, t + 1)
                a.activation(out=grz[:], in_=trz[:], func=AF.Sigmoid
                             ).then_inc(s_sa1.sem, 1)
                a.wait_ge(s_sv2.sem, t + 1)
                a.activation(out=nwt[:], in_=tnn[:], func=AF.Tanh
                             ).then_inc(s_sa2.sem, 1)
        s_sa1.inc(gru_T)
        s_sa2.inc(gru_T)

        @self.ve
        def _(v):
            v.wait_ge(s_ho[0].sem, s_ho[0].v)
            v.wait_ge(s_ho[1].sem, s_ho[1].v)
            v.sem_inc(s_v.sem, 1)
        s_v.inc()

        self.routing(NSL)
        self.experts()

    def routing(self, NSL):
        x, dram = self.x, self.dram
        s_ld, s_w, s_pe, s_v, s_a = (self.s_ld, self.s_w, self.s_pe, self.s_v,
                                     self.s_a)
        hid = self.av(0, [NCH, T], BF16)
        exw = self.av(32, [NCH, D], BF16)
        hsf = self.av(48, [NCH, 128], F32)
        hstk = self.av(52, [NCH, 128], F32)
        expf = self.av(56, [NCH, 128], F32)
        extk = self.av(60, [NCH, 128], F32)
        gsq = self.av(64, [80], F32)
        nrm = self.av(65, [16], F32)
        sel = self.av(66, [NCH, 36], F32)
        sco = self.av(68, [16, NCH], F32)
        cwt = self.av(70, [16, NCH], F32)
        pen = self.av(72, [16], F32)
        cwfm = self.rows[0:65, T:T + 1024].bitcast(BF16).rearrange(
            "p (a b) -> p a b", a=16)
        s_hr = self.sem("s_hr")
        self._rt_done = None

        @self.ge
        def _(g):
            g.wait_ge(s_v.sem, s_v.v)
            g.dma_start(out=exw[:], in_=dram["l0f_expr"][:]).then_inc(s_w.sem, 16)
        s_w.inc(16)

        @self.sy
        def _(sy):
            sy.dma_start(out=sel[:], in_=dram["sel8"][:]).then_inc(s_ld.sem, 16)
        s_ld.inc(16)

        for tt in range(16):
            @self.sy
            def _(sy, tt=tt):
                if tt > 0:
                    sy.wait_ge(s_v.sem, self._rt_done)
                sy.dma_start(out=hsf[:].rearrange("p a b -> p (a b)"),
                             in_=dram["s_hs"][tt]).then_inc(s_hr.sem, 16)
            s_hr.inc(16)

            for c in range(NCH):
                @self.te
                def _(te, tt=tt, c=c):
                    if c == 0:
                        te.wait_ge(s_hr.sem, s_hr.v)
                        te.wait_ge(s_v.sem, s_v.v)
                        te.wait_ge(s_w.sem, s_w.v)
                    te.wait_ge(s_a.sem, s_a.v)
                    te.transpose(self.psO[:, 0:128], hsf[:, c, :], self.ident[:]
                                 ).then_inc(s_pe.sem, 1)
                s_pe.inc()

                @self.ae
                def _(a_, c=c):
                    a_.wait_ge(s_pe.sem, s_pe.v)
                    a_.copy(out=hstk[:, c, :], in_=self.psO[:, 0:128])
                    a_.drain()
                    a_.sem_inc(s_a.sem, 1)
                s_a.inc()

            @self.te
            def _(te, tt=tt):
                for o in range(NCH):
                    for c in range(NCH):
                        mm = te.matmul(self.psW[:, 128 * o:128 * (o + 1)],
                                       exw[:, c, 128 * o:128 * (o + 1)],
                                       hid[:, c, 128 * tt:128 * (tt + 1)],
                                       start=(c == 0), stop=(c == NCH - 1))
                mm.then_inc(s_pe.sem, 1)
            s_pe.inc()

            @self.ve
            def _(v):
                v.wait_ge(s_pe.sem, s_pe.v)
                v.tensor_copy(out=expf[:], in_=self.psW[:, 0:1024].rearrange(
                    "p (a b) -> p a b", a=NCH)).then_inc(s_v.sem, 1)
            s_v.inc()

            for c in range(NCH):
                @self.te
                def _(te, c=c):
                    if c == 0:
                        te.wait_ge(s_v.sem, s_v.v)
                    te.wait_ge(s_a.sem, s_a.v)
                    te.transpose(self.psO[:, 0:128], expf[:, c, :], self.ident[:]
                                 ).then_inc(s_pe.sem, 1)
                s_pe.inc()

                @self.ae
                def _(a_, c=c):
                    a_.wait_ge(s_pe.sem, s_pe.v)
                    a_.copy(out=extk[:, c, :], in_=self.psO[:, 0:128])
                    a_.drain()
                    a_.sem_inc(s_a.sem, 1)
                s_a.inc()

            @self.ae
            def _(a, tt=tt):
                a.wait_ge(s_a.sem, s_a.v)
                for e in range(NCH):
                    a.activation(out=self.tmp[:, 0:128], in_=hstk[:, e, :],
                                 func=AF.Square, accum_out=nrm[:, e:e + 1])
                    a.drain()
                for e in range(NCH):
                    a.activation(out=self.tmp[:, 0:128], in_=extk[:, e, :],
                                 func=AF.Square, accum_out=nrm[:, 8 + e:9 + e])
                    a.drain()
                a.activation(out=nrm[:, 0:16], in_=nrm[:, 0:16], func=AF.Sqrt).then_inc(s_a.sem, 1)
            s_a.inc()

            @self.ve
            def _(v, tt=tt):
                v.wait_ge(s_a.sem, s_a.v)
                for e in range(NCH):
                    v.tensor_scalar(out=self.rcol[:, 0:1], in0=nrm[:, e:e + 1],
                                    scalar1=1e-12, scalar2=None, op0=OP.max,
                                    op1=OP.bypass)
                    v.drain()
                    v.reciprocal(out=self.rcol[:, 1:2], in_=self.rcol[:, 0:1])
                    v.drain()
                    v.tensor_scalar(out=hstk[:, e, :], in0=hstk[:, e, :],
                                    scalar1=self.rcol[:, 1:2], scalar2=None,
                                    op0=OP.mult, op1=OP.bypass)
                    v.drain()
                    v.tensor_tensor(out=nrm[:, e:e + 1], in0=nrm[:, e:e + 1],
                                    in1=self.rcol[:, 1:2], op=OP.mult)
                    v.drain()
                v.sem_inc(s_v.sem, 1)
            s_v.inc()

            pairs = [(e, e) for e in range(NCH)] + \
                    [(e, f) for e in range(NCH) for f in range(e + 1, NCH)]

            @self.ve
            def _(v, tt=tt, pairs=pairs):
                v.wait_ge(s_v.sem, s_v.v)
                for j, (e, f) in enumerate(pairs):
                    v.tensor_tensor(out=self.tmp[:, 0:128], in0=hstk[:, e, :],
                                    in1=hstk[:, f, :], op=OP.mult)
                    v.drain()
                    v.tensor_reduce(out=gsq[:, j:j + 1], in_=self.tmp[:, 0:128],
                                    axis=AX.X, op=OP.add)
                    v.drain()
                for e in range(NCH):
                    v.tensor_tensor(out=self.tmp[:, 0:128], in0=extk[:, e, :],
                                    in1=hstk[:, e, :], op=OP.mult)
                    v.drain()
                    v.tensor_reduce(out=gsq[:, 40 + e:41 + e],
                                    in_=self.tmp[:, 0:128], axis=AX.X, op=OP.add)
                    v.drain()
                v.tensor_scalar(out=gsq[:, 0:8], in0=gsq[:, 0:8], scalar1=-1.0,
                                scalar2=None, op0=OP.add, op1=OP.bypass).then_inc(s_v.sem, 1)
            s_v.inc()

            @self.ae
            def _(a, tt=tt):
                a.wait_ge(s_v.sem, s_v.v)
                a.activation(out=gsq[:, 0:36], in_=gsq[:, 0:36], func=AF.Square).then_inc(s_a.sem, 1)
            s_a.inc()

            @self.ve
            def _(v, tt=tt):
                v.wait_ge(s_a.sem, s_a.v)
                v.wait_ge(s_ld.sem, s_ld.v)
                for e in range(NCH):
                    v.tensor_tensor(out=self.tmp[:, 0:36], in0=gsq[:, 0:36],
                                    in1=sel[:, e, :], op=OP.mult)
                    v.drain()
                    v.tensor_reduce(out=gsq[:, 50 + e:51 + e],
                                    in_=self.tmp[:, 0:36], axis=AX.X, op=OP.add)
                    v.drain()
                v.sem_inc(s_v.sem, 1)
            s_v.inc()

            @self.ae
            def _(a, tt=tt):
                a.wait_ge(s_v.sem, s_v.v)
                a.activation(out=gsq[:, 58:66], in_=gsq[:, 50:58], func=AF.Sqrt).then_inc(s_a.sem, 1)
            s_a.inc()

            @self.ve
            def _(v, tt=tt):
                v.wait_ge(s_a.sem, s_a.v)
                v.tensor_scalar(out=self.tmp[:, 0:8], in0=gsq[:, 58:66],
                                scalar1=1e-12, scalar2=None, op0=OP.max,
                                op1=OP.bypass)
                v.drain()
                v.reciprocal(out=self.tmp[:, 8:16], in_=self.tmp[:, 0:8])
                v.drain()
                v.tensor_tensor(out=self.tmp[:, 8:16], in0=self.tmp[:, 8:16],
                                in1=self.tmp[:, 8:16], op=OP.mult)
                v.drain()
                v.tensor_tensor(out=self.tmp[:, 0:8], in0=gsq[:, 50:58],
                                in1=self.tmp[:, 8:16], op=OP.mult)
                v.drain()
                v.tensor_reduce(out=pen[:, tt:tt + 1], in_=self.tmp[:, 0:8],
                                axis=AX.X, op=OP.add)
                v.drain()
                v.tensor_scalar(out=self.tmp[:, 16:24], in0=nrm[:, 8:16],
                                scalar1=1e-8, scalar2=None, op0=OP.max,
                                op1=OP.bypass)
                v.drain()
                v.tensor_scalar(out=self.tmp[:, 24:32], in0=nrm[:, 0:8],
                                scalar1=1e-8, scalar2=None, op0=OP.max,
                                op1=OP.bypass)
                v.drain()
                v.tensor_tensor(out=self.tmp[:, 16:24], in0=self.tmp[:, 16:24],
                                in1=self.tmp[:, 24:32], op=OP.mult)
                v.drain()
                v.reciprocal(out=self.tmp[:, 16:24], in_=self.tmp[:, 16:24])
                v.drain()
                v.tensor_tensor(out=self.tmp[:, 16:24], in0=gsq[:, 40:48],
                                in1=self.tmp[:, 16:24], op=OP.mult)
                v.drain()
                v.tensor_scalar(out=sco[:, tt, :], in0=self.tmp[:, 16:24],
                                scalar1=-1.0, scalar2=1.0, op0=OP.mult, op1=OP.add).then_inc(s_v.sem, 1)
            s_v.inc()
            self._rt_done = s_v.v

        @self.ve
        def _(v):
            v.wait_ge(s_v.sem, s_v.v)
            v.tensor_reduce(out=self.rcol[:, 0:1], in_=pen[:, 0:16], axis=AX.X,
                            op=OP.add).then_inc(s_v.sem, 1)
        s_v.inc()

        @self.te
        def _(te):
            te.wait_ge(s_v.sem, s_v.v)
            te.matmul(self.psO[0:1, 0:1], self.cb[:, 1:2], self.rcol[:, 0:1],
                      start=True, stop=True).then_inc(s_pe.sem, 1)
        s_pe.inc()

        @self.ve
        def _(v):
            v.wait_ge(s_pe.sem, s_pe.v)
            v.tensor_scalar(out=self.rcol[0:1, 2:3], in0=self.psO[0:1, 0:1],
                            scalar1=1.0 / T, scalar2=1.0, op0=OP.mult, op1=OP.add).then_inc(s_v.sem, 1)
        s_v.inc()

        @self.te
        def _(te):
            te.wait_ge(s_v.sem, s_v.v)
            te.matmul(self.psO[:, 0:1], self.ones1[0:1, :], self.rcol[0:1, 2:3],
                      start=True, stop=True).then_inc(s_pe.sem, 1)
        s_pe.inc()

        @self.ve
        def _(v):
            v.wait_ge(s_pe.sem, s_pe.v)
            v.tensor_copy(out=self.rcol[:, 3:4], in_=self.psO[:, 0:1])
            v.drain()
            for tt in range(16):
                v.tensor_scalar(out=sco[:, tt, :], in0=sco[:, tt, :],
                                scalar1=self.rcol[:, 3:4], scalar2=None,
                                op0=OP.mult, op1=OP.bypass)
                v.drain()
            v.sem_inc(s_v.sem, 1)
        s_v.inc()

        for tt in range(16):
            @self.ve
            def _(v, tt=tt):
                v.wait_ge(s_v.sem, s_v.v)
                v.wait_ge(s_a.sem, s_a.v)
                v.max(out=self.tmp[:, 24:32], in_=sco[:, tt, :])
                v.drain()
                v.max_index(out=self.tmp[:, 32:40].bitcast(U32),
                            in_max=self.tmp[:, 24:32], in_values=sco[:, tt, :])
                v.drain()
                v.tensor_copy(out=self.rcol[:, 0:1], in_=self.tmp[:, 24:25])
                v.tensor_copy(out=self.rcol[:, 4:5], in_=self.tmp[:, 25:26])
                v.drain()
                v.tensor_copy(out=self.rcol[:, 2:3],
                              in_=self.tmp[:, 32:33].bitcast(U32))
                v.tensor_copy(out=self.rcol[:, 6:7],
                              in_=self.tmp[:, 33:34].bitcast(U32))
                v.drain()
                v.tensor_tensor(out=self.rcol[:, 7:8], in0=self.rcol[:, 4:5],
                                in1=self.rcol[:, 0:1], op=OP.subtract
                                ).then_inc(s_v.sem, 1)
            s_v.inc()
            v_now = s_v.v

            @self.ae
            def _(a, v_now=v_now):
                a.wait_ge(s_v.sem, v_now)
                a.activation(out=self.rcol[:, 7:8], in_=self.rcol[:, 7:8],
                             func=AF.Sigmoid).then_inc(s_a.sem, 1)
            s_a.inc()

            @self.ve
            def _(v, tt=tt):
                v.wait_ge(s_a.sem, s_a.v)
                v.tensor_scalar(out=self.tmp[:, 0:8], in0=self.iota8[:, 0:8],
                                scalar1=self.rcol[:, 2:3], scalar2=None,
                                op0=OP.is_equal, op1=OP.bypass)
                v.drain()
                v.tensor_scalar(out=self.tmp[:, 8:16], in0=self.iota8[:, 0:8],
                                scalar1=self.rcol[:, 6:7], scalar2=None,
                                op0=OP.is_equal, op1=OP.bypass)
                v.drain()
                v.tensor_scalar(out=self.tmp[:, 16:17], in0=self.rcol[:, 7:8],
                                scalar1=-1.0, scalar2=1.0, op0=OP.mult,
                                op1=OP.add)
                v.drain()
                v.tensor_scalar(out=self.tmp[:, 0:8], in0=self.tmp[:, 0:8],
                                scalar1=self.tmp[:, 16:17], scalar2=None,
                                op0=OP.mult, op1=OP.bypass)
                v.drain()
                v.tensor_scalar(out=self.tmp[:, 8:16], in0=self.tmp[:, 8:16],
                                scalar1=self.rcol[:, 7:8], scalar2=None,
                                op0=OP.mult, op1=OP.bypass)
                v.drain()
                v.tensor_tensor(out=cwt[:, tt, :], in0=self.tmp[:, 0:8],
                                in1=self.tmp[:, 8:16], op=OP.add).then_inc(s_v.sem, 1)
            s_v.inc()

        for tt in range(16):
            @self.te
            def _(te, tt=tt):
                te.wait_ge(s_v.sem, s_v.v)
                te.transpose(self.psO[0:NCH, 0:128], cwt[:, tt, :], self.ident[:]
                             ).then_inc(s_pe.sem, 1)
            s_pe.inc()

            @self.ve
            def _(v, tt=tt):
                v.wait_ge(s_pe.sem, s_pe.v)
                v.tensor_copy(out=cwfm[0:NCH, tt, :], in_=self.psO[0:NCH, 0:128]).then_inc(s_v.sem, 1)
            s_v.inc()
        self.cwfm = cwfm

    def experts(self):
        x, dram = self.x, self.dram
        s_ld, s_w, s_pe, s_v, s_a = (self.s_ld, self.s_w, self.s_pe, self.s_v,
                                     self.s_a)
        hid = self.av(0, [NCH, T], BF16)
        wg = [self.av(32, [NCH, 512], BF16), self.av(40, [NCH, 512], BF16)]
        wu = [self.av(48, [NCH, 512], BF16), self.av(56, [NCH, 512], BF16)]
        wd4 = self.av(64, [4, 4, D], BF16)
        hbuf = self.av(96, [16, 512], BF16)
        silu = self.av(112, [512], BF16)
        cwb8 = self.tmp[:, 1024:1536]
        cwfm = self.cwfm
        s_wx = [self.sem("s_wx0"), self.sem("s_wx1")]
        gup_done = {}
        dend = {}
        v_start = s_v.v

        for q in range(4):
            for e in range(9):
                eg = q * 9 + e
                if e < 8:
                    for ti in range(4):
                        tti = 4 * q + ti

                        @self.te
                        def _(te, e=e, tti=tti):
                            te.wait_ge(s_v.sem, s_v.v)
                            te.matmul(self.psO[:, 0:128],
                                      self.rsel[:, 128 * e:128 * (e + 1)],
                                      cwfm[0:NCH, tti, :],
                                      start=True, stop=True).then_inc(s_pe.sem, 1)
                        s_pe.inc()

                        @self.ve
                        def _(v, ti=ti):
                            v.wait_ge(s_pe.sem, s_pe.v)
                            v.tensor_copy(out=cwb8[:, 128 * ti:128 * (ti + 1)],
                                          in_=self.psO[:, 0:128]).then_inc(s_v.sem, 1)
                        s_v.inc()

                for fs in range(4):
                    par = fs % 2

                    @self.ge
                    def _(g, e=e, fs=fs, eg=eg, par=par):
                        if fs >= 2:
                            g.wait_ge(s_pe.sem, gup_done[(eg, fs - 2)])
                        elif eg > 0:
                            g.wait_ge(s_pe.sem, dend[eg - 1])
                        else:
                            g.wait_ge(s_v.sem, v_start)
                        g.dma_start(out=wg[par][:], in_=dram["l0f_gate"][e, fs]
                                    ).then_inc(s_wx[par].sem, 16)
                        g.dma_start(out=wu[par][:], in_=dram["l0f_up"][e, fs]
                                    ).then_inc(s_wx[par].sem, 16)
                        g.dma_start(out=wd4[:, fs],
                                    in_=dram["l0f_down"][e, fs]
                                    ).then_inc(s_wx[par].sem, 16)
                    s_wx[par].inc(48)
                    wld = s_wx[par].v

                    for tg in range(4):
                        tsl = slice(512 * q + 128 * tg, 512 * q + 128 * (tg + 1))

                        @self.te
                        def _(te, tg=tg, par=par, wld=wld, tsl=tsl):
                            if tg == 0:
                                te.wait_ge(s_wx[par].sem, wld)
                            te.wait_ge(s_v.sem, s_v.v)
                            for ft in range(4):
                                fsl = slice(128 * ft, 128 * (ft + 1))
                                for c in range(NCH):
                                    te.matmul(self.psA[:, fsl],
                                              wg[par][:, c, fsl],
                                              hid[:, c, tsl], start=(c == 0),
                                              stop=(c == NCH - 1))
                            for ft in range(4):
                                fsl = slice(128 * ft, 128 * (ft + 1))
                                for c in range(NCH):
                                    mm = te.matmul(self.psB[:, fsl],
                                                   wu[par][:, c, fsl],
                                                   hid[:, c, tsl],
                                                   start=(c == 0),
                                                   stop=(c == NCH - 1))
                            mm.then_inc(s_pe.sem, 1)
                        s_pe.inc()
                        if tg == 3:
                            gup_done[(eg, fs)] = s_pe.v

                        @self.ae
                        def _(a):
                            a.wait_ge(s_pe.sem, s_pe.v)
                            a.activation(out=silu[:], in_=self.psA[:, 0:512],
                                         func=AF.Silu).then_inc(s_a.sem, 1)
                        s_a.inc()

                        @self.ve
                        def _(v, e=e, fs=fs, tg=tg):
                            v.wait_ge(s_a.sem, s_a.v)
                            v.tensor_tensor(out=self.tmp[:, 0:512],
                                            in0=self.psB[:], in1=silu[:],
                                            op=OP.mult)
                            v.drain()
                            dst = hbuf[:, 4 * fs:4 * fs + 4,
                                       128 * tg:128 * (tg + 1)]
                            if e < 8:
                                cw1 = cwb8[:, 128 * tg:128 * (tg + 1)]
                                for fc in range(4):
                                    v.tensor_tensor(
                                        out=dst[:, fc, :],
                                        in0=self.tmp[:, 128 * fc:128 * (fc + 1)],
                                        in1=cw1, op=OP.mult)
                                v.drain()
                            else:
                                v.tensor_copy(out=dst,
                                              in_=self.tmp[:, 0:512].rearrange(
                                                  "p (a b) -> p a b", a=4))
                                v.drain()
                            v.sem_inc(s_v.sem, 1)
                        s_v.inc()

                for co in range(NCH):
                    bank = 512 * (co % 2)

                    @self.te
                    def _(te, e=e, co=co, bank=bank):
                        te.wait_ge(s_v.sem, s_v.v)
                        kk = 0
                        for fs in range(4):
                            for fc in range(4):
                                mm = te.matmul(
                                    self.psW[:, bank:bank + 512],
                                    wd4[:, fs, fc, 128 * co:128 * (co + 1)],
                                    hbuf[:, 4 * fs + fc, :], start=(kk == 0),
                                    stop=(kk == 15))
                                kk += 1
                        mm.then_inc(s_pe.sem, 1)
                    s_pe.inc()

                    @self.ve
                    def _(v, q=q, co=co, bank=bank):
                        v.wait_ge(s_pe.sem, s_pe.v)
                        qsl = slice(512 * q, 512 * (q + 1))
                        v.tensor_tensor(out=x[:, co, qsl], in0=x[:, co, qsl],
                                        in1=self.psW[:, bank:bank + 512],
                                        op=OP.add).then_inc(s_v.sem, 1)
                    s_v.inc()
                dend[eg] = s_pe.v

        if "l0f" in self.taps:
            s_df = self.sem("s_dl0f")

            @self.sy
            def _(sy):
                sy.wait_ge(s_v.sem, s_v.v)
                sy.dma_start(out=dram["dbg_l0f"][:], in_=x[:]).then_inc(
                    s_df.sem, 16)
                sy.wait_ge(s_df.sem, 16)

    def l1_attn(self):
        x, dram = self.x, self.dram
        s_w, s_pe, s_v = self.s_w, self.s_pe, self.s_v
        self.tf16 = self.tmp[:].bitcast(F16)
        self.tbf = self.tmp[:].bitcast(BF16)
        self.qkf = self.av(32, [NCH, T], BF16)
        self.vf = self.av(64, [NCH, T], BF16)
        xbv = self.av(0, [NCH, T], BF16)
        self.obuf = self.av(0, [NCH, T], BF16)

        self.ln_rows(xb=self.av(32, [NCH, T], BF16),
                     xsq=self.av(64, [NCH, T], BF16))
        self.apply_affine(xbv, 0, 1)

        wbig = self.av(97, [NCH, D], BF16)
        for (wname, dst) in (("l1a_wqk", self.qkf), ("l1a_wv", self.vf)):
            @self.ge
            def _(g, wname=wname):
                g.wait_ge(s_pe.sem, s_pe.v)
                g.dma_start(out=wbig[:], in_=dram[wname][:]).then_inc(s_w.sem, 16)
            s_w.inc(16)

            for tg in range(4):
                sl = slice(512 * tg, 512 * (tg + 1))
                for op_ in range(0, NCH, 2):
                    @self.te
                    def _(te, sl=sl, op_=op_):
                        te.wait_ge(s_v.sem, s_v.v)
                        te.wait_ge(s_w.sem, s_w.v)
                        for c in range(NCH):
                            te.matmul(self.psA[:],
                                      wbig[:, c, 128 * op_:128 * op_ + 128],
                                      xbv[:, c, sl], start=(c == 0),
                                      stop=(c == NCH - 1))
                        for c in range(NCH):
                            mm = te.matmul(self.psB[:],
                                           wbig[:, c, 128 * op_ + 128:128 * op_ + 256],
                                           xbv[:, c, sl], start=(c == 0),
                                           stop=(c == NCH - 1))
                        mm.then_inc(s_pe.sem, 1)
                    s_pe.inc()

                    @self.ve
                    def _(v, sl=sl, op_=op_, dst=dst):
                        v.wait_ge(s_pe.sem, s_pe.v)
                        v.tensor_copy(out=dst[:, op_, sl], in_=self.psA[:])
                        v.tensor_copy(out=dst[:, op_ + 1, sl], in_=self.psB[:]).then_inc(s_v.sem, 1)
                    s_v.inc()

        @self.ve
        def _(v):
            v.wait_ge(s_v.sem, s_v.v)
            v.memset(self.av(0, [NCH * T], BF16), 0.0)
            v.drain()
            v.sem_inc(s_v.sem, 1)
        s_v.inc()

        self._l1a_local()
        self._l1a_lsh()
        if "rk" in self.taps:
            return

        if "obuf" in self.taps:
            s_do = self.sem("s_dob1")

            @self.sy
            def _(sy):
                sy.wait_ge(s_v.sem, s_v.v)
                sy.wait_ge(self.s_a.sem, self.s_a.v)
                sy.dma_start(out=dram["dbg_obuf"][:], in_=self.obuf[:]
                             ).then_inc(s_do.sem, 16)
                sy.wait_ge(s_do.sem, s_do.inc(16))

        self._l1a_tail()

    def _l1a_local(self):
        s_pe, s_v, s_a, s_g = self.s_pe, self.s_v, self.s_a, self.s_g
        qkf, vf, obuf, tbf = self.qkf, self.vf, self.obuf, self.tbf
        NEGB = -1.0e30
        SHV = 50000.0
        NROW = self.R(3)
        knl = self.av(109, [T], BF16)
        vtok = self.av(97, [16, 64], BF16)
        m_neq = self.tmp[:, 1280:1536]
        m_gtn = self.tmp[:, 1536:1792]
        m_padn = self.tmp[:, 1792:2048]
        rcol = self.rcol

        @self.ge
        def _(g):
            g.wait_ge(s_v.sem, s_v.v)
            g.iota(m_neq[:], pattern=[[1, 256]], base=0, channel_multiplier=-1,
                   allow_small_or_imprecise_dtypes=True)
            g.drain()
            g.iota(m_padn[:], pattern=[[1, 256]], base=0, channel_multiplier=0,
                   allow_small_or_imprecise_dtypes=True).then_inc(s_g.sem, 1)
        s_g.inc()

        @self.ve
        def _(v):
            v.wait_ge(s_g.sem, s_g.v)
            v.tensor_scalar(out=m_gtn[:], in0=m_neq[:], scalar1=128.0,
                            scalar2=NEGB, op0=OP.is_gt, op1=OP.mult)
            v.drain()
            v.tensor_scalar(out=m_neq[:], in0=m_neq[:], scalar1=128.0,
                            scalar2=None, op0=OP.not_equal, op1=OP.bypass)
            v.drain()
            v.tensor_scalar(out=m_padn[:], in0=m_padn[:], scalar1=127.5,
                            scalar2=None, op0=OP.is_gt, op1=OP.bypass)
            v.drain()
            v.tensor_scalar(out=m_padn[:], in0=m_padn[:], scalar1=-NEGB,
                            scalar2=NEGB, op0=OP.mult, op1=OP.add).then_inc(s_v.sem, 1)
        s_v.inc()

        for h in range(4):
            hb, ch = 64 * (h % 2), h // 2
            hsl = slice(hb, hb + 64)

            @self.ae
            def _(a, hsl=hsl, ch=ch):
                a.wait_ge(s_v.sem, s_v.v)
                a.wait_ge(s_pe.sem, s_pe.v)
                for tg in range(4):
                    sl = slice(512 * tg, 512 * (tg + 1))
                    a.activation(out=tbf[hsl, sl], in_=qkf[hsl, ch, sl],
                                 func=AF.Square)
                a.drain()
                a.sem_inc(s_a.sem, 1)
            s_a.inc()

            @self.te
            def _(te, hsl=hsl, hb=hb):
                te.wait_ge(s_a.sem, s_a.v)
                for tg in range(4):
                    sl = slice(512 * tg, 512 * (tg + 1))
                    mm = te.matmul(self.psW[0:1, sl], self.onesb[hsl, 0:1],
                                   tbf[hsl, sl], start=True, stop=True,
                                   tile_position=(hb, 0))
                mm.then_inc(s_pe.sem, 1)
            s_pe.inc()

            @self.ae
            def _(a):
                a.wait_ge(s_pe.sem, s_pe.v)
                a.activation(out=NROW, in_=self.psW[0:1, :], func=AF.Sqrt).then_inc(s_a.sem, 1)
            s_a.inc()

            @self.ve
            def _(v):
                v.wait_ge(s_a.sem, s_a.v)
                v.reciprocal(out=NROW, in_=NROW)
                v.drain()
                v.tensor_scalar(out=NROW, in0=NROW, scalar1=0.125, scalar2=None,
                                op0=OP.mult, op1=OP.bypass).then_inc(s_v.sem, 1)
            s_v.inc()

            for tg in range(4):
                sl = slice(512 * tg, 512 * (tg + 1))

                @self.te
                def _(te, sl=sl, hb=hb):
                    te.wait_ge(s_v.sem, s_v.v)
                    te.matmul(self.psO[hb:hb + 64, :], self.ones1[0:1, 0:64],
                              NROW[:, sl], start=True, stop=True,
                              tile_position=(0, hb)).then_inc(s_pe.sem, 1)
                s_pe.inc()

                @self.ve
                def _(v, sl=sl, hsl=hsl, ch=ch):
                    v.wait_ge(s_pe.sem, s_pe.v)
                    v.tensor_tensor(out=knl[hsl, sl], in0=qkf[hsl, ch, sl],
                                    in1=self.psO[hsl, :], op=OP.mult).then_inc(s_v.sem, 1)
                s_v.inc()

            for tt in range(16):
                @self.te
                def _(te, tt=tt, hsl=hsl, ch=ch, hb=hb):
                    te.wait_ge(s_v.sem, s_v.v)
                    te.wait_ge(s_a.sem, s_a.v)
                    te.transpose(self.psT[:, 0:64],
                                 vf[hsl, ch, 128 * tt:128 * (tt + 1)],
                                 self.identb[hsl, hsl],
                                 tile_position=(hb, 0)).then_inc(s_pe.sem, 1)
                s_pe.inc()

                @self.ae
                def _(a, tt=tt):
                    a.wait_ge(s_pe.sem, s_pe.v)
                    a.copy(out=vtok[:, tt, :], in_=self.psT[:, 0:64]).then_inc(s_a.sem, 1)
                s_a.inc()

            for w in range(16):
                wsl = slice(128 * w, 128 * (w + 1))
                ksl = slice(max(0, 128 * (w - 1)), 128 * (w + 1))
                nk = 256 if w > 0 else 128
                msl = slice(0, 256) if w > 0 else slice(128, 256)

                @self.te
                def _(te, wsl=wsl, ksl=ksl, nk=nk, hsl=hsl, ch=ch, hb=hb):
                    te.wait_ge(s_v.sem, s_v.v)
                    te.matmul(self.psA[:, 0:nk], qkf[hsl, ch, wsl],
                              knl[hsl, ksl], start=True, stop=True,
                              tile_position=(hb, 0)).then_inc(s_pe.sem, 1)
                s_pe.inc()

                @self.ve
                def _(v, w=w, nk=nk, msl=msl):
                    v.wait_ge(s_pe.sem, s_pe.v)
                    d = self.tmp[:, 0:nk]
                    v.tensor_tensor(out=d, in0=self.psA[:, 0:nk],
                                    in1=m_neq[:, msl], op=OP.mult)
                    v.drain()
                    v.scalar_tensor_tensor(out=d, in0=m_neq[:, msl], scalar=SHV,
                                           in1=d, op0=OP.mult, op1=OP.add)
                    v.drain()
                    v.tensor_tensor(out=d, in0=d, in1=m_gtn[:, msl], op=OP.add)
                    v.drain()
                    if w == 0:
                        v.tensor_tensor(out=d, in0=d, in1=m_padn[:, msl],
                                        op=OP.add)
                        v.drain()
                    v.reduce_max(out=rcol[:, 0:1], in_=d, axis=AX.X, negate=True).then_inc(s_v.sem, 1)
                s_v.inc()

                @self.ae
                def _(a, nk=nk):
                    a.wait_ge(s_v.sem, s_v.v)
                    a.activation(out=tbf[:, 512:512 + nk], in_=self.tmp[:, 0:nk],
                                 func=AF.Exp, bias=rcol[:, 0:1],
                                 accum_out=rcol[:, 1:2]).then_inc(s_a.sem, 1)
                s_a.inc()

                nh = 2 if w > 0 else 1
                for half in range(nh):
                    @self.te
                    def _(te, half=half, nh=nh):
                        te.wait_ge(s_a.sem, s_a.v)
                        te.transpose(self.psT[:],
                                     tbf[:, 512 + 128 * half:640 + 128 * half],
                                     self.identb[:]).then_inc(s_pe.sem, 1)
                    s_pe.inc()

                    @self.ae
                    def _(a):
                        a.wait_ge(s_pe.sem, s_pe.v)
                        a.copy(out=tbf[:, 768:896], in_=self.psT[:]).then_inc(s_a.sem, 1)
                    s_a.inc()

                    @self.te
                    def _(te, w=w, half=half, nh=nh):
                        te.wait_ge(s_a.sem, s_a.v)
                        vt = w - 1 + half if nh == 2 else w
                        te.matmul(self.psO[:, 0:64], tbf[:, 768:896],
                                  vtok[:, vt, :], start=(half == 0),
                                  stop=(half == nh - 1)).then_inc(s_pe.sem, 1)
                    s_pe.inc()

                @self.ve
                def _(v):
                    v.wait_ge(s_pe.sem, s_pe.v)
                    v.reciprocal(out=rcol[:, 2:3], in_=rcol[:, 1:2])
                    v.drain()
                    v.tensor_scalar(out=self.tmp[:, 256:320],
                                    in0=self.psO[:, 0:64],
                                    scalar1=rcol[:, 2:3], scalar2=None,
                                    op0=OP.mult, op1=OP.bypass).then_inc(s_v.sem, 1)
                s_v.inc()

                @self.te
                def _(te):
                    te.wait_ge(s_v.sem, s_v.v)
                    te.transpose(self.psB[0:64, 0:128],
                                 self.tmp[:, 256:320],
                                 self.ident[:]).then_inc(s_pe.sem, 1)
                s_pe.inc()

                @self.ae
                def _(a, hsl=hsl, ch=ch, wsl=wsl):
                    a.wait_ge(s_pe.sem, s_pe.v)
                    a.copy(out=obuf[hsl, ch, wsl], in_=self.psB[0:64, 0:128]).then_inc(s_a.sem, 1)
                s_a.inc()


    def _l1a_lsh(self):
        s_pe, s_v, s_a, s_w = self.s_pe, self.s_v, self.s_a, self.s_w
        qkf, vf, obuf, tf16, tbf = self.qkf, self.vf, self.obuf, self.tf16, self.tbf
        dram, rows = self.dram, self.rows
        NEGB = -1.0e30
        SHV = 50000.0
        rot2 = self.av(96, [128], BF16)
        SA = self.av(97, [T], F16)
        SB = self.av(101, [T], F16)
        MG = self.av(105, [T], F16)
        SCR = self.av(109, [T], F16)
        rotf = self.av(97, [T])
        rkc = self.av(113.25, [16, 4], U32)
        rk2 = self.av(113.5, [16, 4], U32)
        knf = self.av(97, [8320], F16)
        rowstg = tf16[:, 768:1024]
        winb = [tf16[:, 3328:3584], tf16[:, 3584:3840]]
        btokv = self.tmp[:, 1664:1728].rearrange("p (a b) -> p a b", a=16)
        tiev = self.tmp[:, 1728:1792].rearrange("p (a b) -> p a b", a=16)
        catv = self.tmp[:, 128:384].rearrange("p (a b) -> p a b", a=4)
        BROW = self.R(3)
        TOT = rows[0:1, T:T + 160]
        RWA = rows[0:1, T + 256:T + 384]
        RWB = rows[0:1, T + 384:T + 512]
        SCN = rows[0:1, T + 512:T + 672]
        STA = rows[0:1, T + 672:T + 833]
        STROW = rows[0:1, T:T + 128]
        sostg = self.tmp[64:128, 512:1664]
        ugb = self.tmp[:, 1664:1952].rearrange("p (r f) -> p r f", r=4)
        rcol = self.rcol
        s_sc = self.sem("s_sc")
        s_so = self.sem("s_so")
        s_ug = self.sem("s_ug")
        s_kn = self.sem("s_kn")
        s_wd = [self.sem("s_wd0"), self.sem("s_wd1")]

        @self.ge
        def _(g):
            g.wait_ge(s_v.sem, s_v.v)
            g.dma_start(out=rot2[:], in_=dram["l1a_rot"][:]).then_inc(s_w.sem, 16)
        s_w.inc(16)

        BAND = [slice(0, 64), slice(32, 96), slice(64, 128), slice(96, 128)]
        CSTRIP = [SA, SB, SA, SB]

        def rsplit(sl):
            a, b = sl.start, sl.stop
            if a < 64 < b:
                return [slice(a, 64), slice(64, b)]
            return [sl]

        def emit_brow(r):
            for tp in range(8):
                @self.te
                def _(te, r=r, tp=tp):
                    te.wait_ge(s_v.sem, s_v.v)
                    te.transpose(self.psA[0:1, 0:128], btokv[:, 2 * tp, r:r + 1],
                                 self.ident[:])
                    te.transpose(self.psB[0:1, 0:128],
                                 btokv[:, 2 * tp + 1, r:r + 1],
                                 self.ident[:]).then_inc(s_pe.sem, 1)
                s_pe.inc()

                @self.ve
                def _(v, r=r, tp=tp):
                    v.wait_ge(s_pe.sem, s_pe.v)
                    v.tensor_scalar(out=BROW[:, 256 * tp:256 * tp + 128],
                                    in0=self.psA[0:1, 0:128],
                                    scalar1=float(32 * r), scalar2=None,
                                    op0=OP.add, op1=OP.bypass)
                    v.tensor_scalar(out=BROW[:, 256 * tp + 128:256 * tp + 256],
                                    in0=self.psB[0:1, 0:128],
                                    scalar1=float(32 * r), scalar2=None,
                                    op0=OP.add, op1=OP.bypass).then_inc(s_v.sem, 1)
                s_v.inc()

        def emit_table(r, strip, use_psw):
            banks = ([self.psW[:, 0:512], self.psW[:, 512:1024],
                      self.psW[:, 1024:1536], self.psW[:, 1536:2048]]
                     if use_psw else
                     [self.psO[:, 0:512], self.psA[:, 0:512],
                      self.psB[:, 0:512], None])

            def emit_group(g4s):
                @self.te
                def _(te, g4s=g4s):
                    te.wait_ge(s_v.sem, s_v.v)
                    for g4 in g4s:
                        mm = te.matmul(banks[g4 % 4] if banks[g4 % 4] is not None
                                       else self.psO[:, 0:512],
                                       self.ones1[0:1, :],
                                       BROW[:, 512 * g4:512 * (g4 + 1)],
                                       start=True, stop=True)
                    mm.then_inc(s_pe.sem, 1)
                s_pe.inc()

                @self.ve
                def _(v, g4s=g4s, r=r, strip=strip):
                    v.wait_ge(s_pe.sem, s_pe.v)
                    for g4 in g4s:
                        ps = (banks[g4 % 4] if banks[g4 % 4] is not None
                              else self.psO[:, 0:512])
                        sl = slice(512 * g4, 512 * (g4 + 1))
                        for bd in rsplit(BAND[r]):
                            v.tensor_scalar(out=strip[bd, sl], in0=ps[bd, :],
                                            scalar1=self.iotap[bd, 0:1],
                                            scalar2=0.0, op0=OP.subtract,
                                            op1=OP.is_equal)
                        if r == 3:
                            v.tensor_scalar(out=strip[0:32, sl],
                                            in0=ps[0:32, :],
                                            scalar1=self.iotap128[0:32, 0:1],
                                            scalar2=0.0, op0=OP.subtract,
                                            op1=OP.is_equal)
                    v.drain()
                    v.sem_inc(s_v.sem, 1)
                s_v.inc()

            if use_psw:
                emit_group([0, 1, 2, 3])
            else:
                emit_group([0, 1, 2])
                emit_group([3])

        for h in range(4, 16):
            hs, hb, ch = h - 4, 64 * (h % 2), h // 2
            hsl = slice(hb, hb + 64)

            @self.te
            def _(te, hb=hb, ch=ch, hsl=hsl):
                te.wait_ge(s_v.sem, s_v.v)
                te.wait_ge(s_w.sem, s_w.v)
                for g4 in range(4):
                    sl = slice(512 * g4, 512 * (g4 + 1))
                    mm = te.matmul(self.psW[:, sl], rot2[hsl, :],
                                   qkf[hsl, ch, sl], start=True, stop=True,
                                   tile_position=(hb, 0))
                mm.then_inc(s_pe.sem, 1)
            s_pe.inc()

            @self.ve
            def _(v):
                v.wait_ge(s_pe.sem, s_pe.v)
                v.tensor_copy(out=rotf[:], in_=self.psW[:]).then_inc(s_v.sem, 1)
            s_v.inc()

            for tt in range(16):
                @self.te
                def _(te, tt=tt):
                    te.wait_ge(s_v.sem, s_v.v)
                    te.wait_ge(s_a.sem, s_a.v)
                    te.transpose(self.psO[:, 0:128],
                                 rotf[:, 128 * tt:128 * (tt + 1)],
                                 self.ident[:]).then_inc(s_pe.sem, 1)
                s_pe.inc()

                @self.ae
                def _(a):
                    a.wait_ge(s_pe.sem, s_pe.v)
                    a.copy(out=self.tmp[:, 0:128], in_=self.psO[:, 0:128]).then_inc(s_a.sem, 1)
                s_a.inc()

                @self.ve
                def _(v, tt=tt):
                    v.wait_ge(s_a.sem, s_a.v)
                    src = self.tmp[:, 0:128].rearrange("p (a b) -> p a b", a=4)
                    v.tensor_copy(out=catv[:, :, 0:32], in_=src)
                    v.drain()
                    v.tensor_scalar(out=catv[:, :, 32:64], in0=src, scalar1=-1.0,
                                    scalar2=None, op0=OP.mult, op1=OP.bypass)
                    v.drain()
                    mx8 = self.tmp[:, 384:416].rearrange("p (a b) -> p a b",
                                                         a=4)
                    ix8 = self.tmp[:, 416:448].rearrange("p (a b) -> p a b",
                                                         a=4)
                    for r in range(4):
                        v.max(out=mx8[:, r, :], in_=catv[:, r, :])
                    v.drain()
                    for r in range(4):
                        v.max_index(out=ix8[:, r, :].bitcast(U32),
                                    in_max=mx8[:, r, :],
                                    in_values=catv[:, r, :])
                    v.drain()
                    v.tensor_copy(out=btokv[:, tt, :],
                                  in_=ix8[:, :, 0:1].bitcast(U32).rearrange(
                                      "p a b -> p (a b)")).then_inc(s_v.sem, 1)
                s_v.inc()

            @self.ve
            def _(v):
                v.wait_ge(s_v.sem, s_v.v)
                v.memset(self.tmp[:, 1728:1792], 0.0)
                v.drain()
                for d in (1, 2, 3):
                    nw = 4 - d
                    q = self.tmp[:, 560:560 + 16 * nw].rearrange(
                        "p (a b) -> p a b", a=16)
                    v.tensor_tensor(out=q, in0=btokv[:, :, 0:nw],
                                    in1=btokv[:, :, d:4], op=OP.subtract)
                    v.drain()
                    v.tensor_scalar(out=q, in0=q, scalar1=float(32 * d),
                                    scalar2=None, op0=OP.is_equal, op1=OP.bypass)
                    v.drain()
                    v.tensor_tensor(out=tiev[:, :, d:4], in0=tiev[:, :, d:4],
                                    in1=q, op=OP.add)
                    v.drain()
                v.sem_inc(s_v.sem, 1)
            s_v.inc()

            for r in range(4):
                emit_brow(r)
                emit_table(r, CSTRIP[r], True)

            @self.ve
            def _(v):
                v.wait_ge(s_v.sem, s_v.v)
                v.tensor_tensor_scan(out=SA[:],
                                     data0=self.cb[:, 1:2].to_broadcast([128, T]),
                                     data1=SA[:], initial=0.0, op0=OP.mult,
                                     op1=OP.add)
                v.drain()
                v.tensor_tensor_scan(out=SB[:],
                                     data0=self.cb[:, 1:2].to_broadcast([128, T]),
                                     data1=SB[:], initial=0.0, op0=OP.mult,
                                     op1=OP.add)
                v.drain()
                v.tensor_copy(out=self.tmp[:, 544:545], in_=SA[:, T - 1:T])
                v.tensor_copy(out=self.tmp[:, 545:546], in_=SB[:, T - 1:T]).then_inc(s_v.sem, 1)
            s_v.inc()

            @self.te
            def _(te):
                te.wait_ge(s_v.sem, s_v.v)
                te.transpose(self.psA[0:1, 0:128], self.tmp[:, 544:545],
                             self.ident[:])
                te.transpose(self.psB[0:1, 0:128], self.tmp[:, 545:546],
                             self.ident[:]).then_inc(s_pe.sem, 1)
            s_pe.inc()

            @self.ve
            def _(v):
                v.wait_ge(s_pe.sem, s_pe.v)
                v.tensor_copy(out=RWA[:], in_=self.psA[0:1, 0:128])
                v.tensor_copy(out=RWB[:], in_=self.psB[0:1, 0:128])
                v.drain()
                v.tensor_copy(out=TOT[:, 0:128], in_=RWA[:])
                v.tensor_copy(out=rows[0:1, T + 128:T + 160], in_=RWB[:, 0:32])
                v.drain()
                v.tensor_tensor(out=TOT[:, 32:128], in0=TOT[:, 32:128],
                                in1=RWB[:, 32:128], op=OP.add)
                v.drain()
                v.tensor_tensor_scan(out=SCN[:],
                                     data0=self.cb[0:1, 1:2].to_broadcast([1, 160]),
                                     data1=TOT[:], initial=0.0, op0=OP.mult,
                                     op1=OP.add)
                v.drain()
                v.memset(STA[:, 0:1], 0.0)
                v.drain()
                v.tensor_copy(out=STA[:, 1:161], in_=SCN[:, 0:160]).then_inc(s_v.sem, 1)
            s_v.inc()

            @self.te
            def _(te):
                te.wait_ge(s_v.sem, s_v.v)
                te.transpose(self.psA[:, 0:1], rows[0:1, T + 672:T + 800],
                             self.ident[0:1, 0:1])
                te.transpose(self.psB[0:32, 0:1], rows[0:1, T + 800:T + 832],
                             self.ident[0:1, 0:1]).then_inc(s_pe.sem, 1)
            s_pe.inc()

            @self.ve
            def _(v):
                v.wait_ge(s_pe.sem, s_pe.v)
                tm = self.tmp
                v.tensor_copy(out=tm[:, 528:529], in_=self.psA[:, 0:1])
                v.tensor_copy(out=tm[0:32, 534:535], in_=self.psB[0:32, 0:1])
                v.drain()
                for (src, hic, loc, rsl) in ((528, 532, 533, slice(0, 128)),
                                             (534, 535, 536, slice(0, 32))):
                    v.tensor_scalar(out=tm[rsl, 529:530], in0=tm[rsl, src:src + 1],
                                    scalar1=2048.0, scalar2=None, op0=OP.is_ge,
                                    op1=OP.bypass)
                    v.tensor_scalar(out=tm[rsl, 530:531], in0=tm[rsl, src:src + 1],
                                    scalar1=4096.0, scalar2=None, op0=OP.is_ge,
                                    op1=OP.bypass)
                    v.tensor_scalar(out=tm[rsl, 531:532], in0=tm[rsl, src:src + 1],
                                    scalar1=6144.0, scalar2=None, op0=OP.is_ge,
                                    op1=OP.bypass)
                    v.drain()
                    v.tensor_tensor(out=tm[rsl, 529:530], in0=tm[rsl, 529:530],
                                    in1=tm[rsl, 530:531], op=OP.add)
                    v.drain()
                    v.tensor_tensor(out=tm[rsl, 529:530], in0=tm[rsl, 529:530],
                                    in1=tm[rsl, 531:532], op=OP.add)
                    v.drain()
                    v.tensor_scalar(out=tm[rsl, hic:hic + 1],
                                    in0=tm[rsl, 529:530], scalar1=2048.0,
                                    scalar2=None, op0=OP.mult, op1=OP.bypass)
                    v.drain()
                    v.tensor_tensor(out=tm[rsl, loc:loc + 1],
                                    in0=tm[rsl, src:src + 1],
                                    in1=tm[rsl, hic:hic + 1], op=OP.subtract)
                    v.drain()
                v.sem_inc(s_v.sem, 1)
            s_v.inc()

            @self.ve
            def _(v):
                v.wait_ge(s_v.sem, s_v.v)
                v.memset(SCR[:], 0.0).then_inc(s_v.sem, 1)
            s_v.inc()

            NPROD = [4, 5, 5, 7]
            self._prev_zero = None
            pidx = 0
            ptot = sum(NPROD)
            for r in range(4):
                emit_brow(r)

                @self.ve
                def _(v):
                    v.wait_ge(s_v.sem, s_v.v)
                    v.memset(MG[:], 0.0).then_inc(s_v.sem, 1)
                s_v.inc()

                emit_table(r, MG, False)

                tm = self.tmp
                bd = BAND[r]

                def sgprod(v, rsl, col):
                    for q in rsplit(rsl):
                        v.tensor_scalar(out=SCR[q, :], in0=MG[q, :],
                                        scalar1=tm[q, col:col + 1], scalar2=None,
                                        op0=OP.mult, op1=OP.bypass)

                def cprod(v, rsl, strip):
                    for q in rsplit(rsl):
                        v.tensor_tensor(out=SCR[q, 1:T], in0=MG[q, 1:T],
                                        in1=strip[q, 0:T - 1], op=OP.mult)

                prods = [(lambda v, rsl=bd: sgprod(v, rsl, 532), bd),
                         (lambda v, rsl=bd: sgprod(v, rsl, 533), bd)]
                if r == 3:
                    prods += [(lambda v: sgprod(v, slice(0, 32), 535),
                               slice(0, 32)),
                              (lambda v: sgprod(v, slice(0, 32), 536),
                               slice(0, 32)),
                              (lambda v: cprod(v, slice(96, 128), SB),
                               slice(96, 128)),
                              (lambda v: cprod(v, slice(0, 32), SB),
                               slice(0, 32)),
                              (lambda v: cprod(v, slice(96, 128), SA),
                               slice(96, 128))]
                else:
                    prods += [(lambda v, rsl=bd, st=CSTRIP[r]: cprod(v, rsl, st),
                               bd)]
                    if r >= 1:
                        rsl = slice(32 * r, 32 * r + 32)
                        prods += [(lambda v, rsl=rsl, st=CSTRIP[r - 1]:
                                   cprod(v, rsl, st), rsl)]
                    if r <= 2:
                        rsl = slice(32 * r + 32, 32 * r + 64)
                        prods += [(lambda v, rsl=rsl, st=CSTRIP[r + 1]:
                                   cprod(v, rsl, st), rsl)]
                assert len(prods) == NPROD[r]

                for (wfn, rsl) in prods:
                    @self.ve
                    def _(v, wfn=wfn, pz=self._prev_zero):
                        v.wait_ge(s_v.sem, s_v.v)
                        v.wait_ge(s_pe.sem, s_pe.v)
                        if pz is not None:
                            for q in rsplit(pz):
                                v.memset(SCR[q, :], 0.0)
                            v.drain()
                        wfn(v)
                        v.drain()
                        v.sem_inc(s_v.sem, 1)
                    s_v.inc()
                    self._prev_zero = rsl

                    @self.te
                    def _(te, r=r, pidx=pidx, ptot=ptot):
                        te.wait_ge(s_v.sem, s_v.v)
                        for g4 in range(4):
                            mm = te.matmul(
                                self.psW[64:68, 512 * g4:512 * (g4 + 1)],
                                self.e16[:, 4 * r:4 * r + 4],
                                SCR[:, 512 * g4:512 * (g4 + 1)],
                                start=(pidx == 0), stop=(pidx == ptot - 1),
                                skip_group_check=True, tile_position=(0, 64))
                        mm.then_inc(s_pe.sem, 1)
                    s_pe.inc()
                    pidx += 1

            for tt in range(16):
                @self.ve
                def _(v, tt=tt):
                    v.wait_ge(s_pe.sem, s_pe.v)
                    v.tensor_copy(out=self.tmp[64:68, 0:128],
                                  in_=self.psW[64:68, 128 * tt:128 * (tt + 1)]).then_inc(s_v.sem, 1)
                s_v.inc()

                @self.te
                def _(te):
                    te.wait_ge(s_v.sem, s_v.v)
                    te.transpose(self.psO[:, 0:4], self.tmp[64:68, 0:128],
                                 self.ident[64:68, 64:68],
                                 tile_position=(64, 0)).then_inc(s_pe.sem, 1)
                s_pe.inc()

                @self.ve
                def _(v, tt=tt):
                    v.wait_ge(s_pe.sem, s_pe.v)
                    tm = self.tmp
                    c = tm[:, 128:132]
                    v.tensor_tensor(out=c, in0=self.psO[:, 0:4],
                                    in1=tiev[:, tt, :], op=OP.add)
                    v.drain()
                    v.tensor_scalar(out=c, in0=c, scalar1=64.0, scalar2=None,
                                    op0=OP.add, op1=OP.bypass)
                    v.drain()
                    v.tensor_copy(out=rkc[:, tt, :], in_=c)
                    v.drain()
                    m = tm[:, 132:136]
                    v.tensor_scalar(out=m, in0=c, scalar1=8192.0, scalar2=None,
                                    op0=OP.is_ge, op1=OP.bypass)
                    v.drain()
                    t1 = tm[:, 136:140]
                    v.tensor_scalar(out=t1, in0=c, scalar1=-8192.0, scalar2=None,
                                    op0=OP.add, op1=OP.bypass)
                    v.drain()
                    v.tensor_tensor(out=t1, in0=t1, in1=m, op=OP.mult)
                    v.drain()
                    t2 = tm[:, 140:144]
                    v.tensor_scalar(out=t2, in0=m, scalar1=-1.0, scalar2=1.0,
                                    op0=OP.mult, op1=OP.add)
                    v.drain()
                    v.tensor_tensor(out=t2, in0=t2, in1=self.dmp4[:],
                                    op=OP.mult)
                    v.drain()
                    v.tensor_tensor(out=t1, in0=t1, in1=t2, op=OP.add)
                    v.drain()
                    v.tensor_copy(out=rk2[:, tt, :], in_=t1).then_inc(s_v.sem, 1)
                s_v.inc()

            if "rk" in self.taps and h == 4:
                s_dbg = self.sem("s_dbgrk")

                @self.sy
                def _(sy):
                    sy.wait_ge(s_v.sem, s_v.v)
                    sy.dma_start(out=dram["dbg_rkc"][:], in_=rkc[:]
                                 ).then_inc(s_dbg.sem, 16)
                    sy.dma_start(out=dram["dbg_bt"][:], in_=self.tmp[:, 1664:1728]
                                 ).then_inc(s_dbg.sem, 16)
                    sy.dma_start(out=dram["dbg_tie"][:], in_=self.tmp[:, 1728:1792]
                                 ).then_inc(s_dbg.sem, 16)
                    sy.dma_start(out=dram["dbg_rows"][:], in_=rows[0:1, :]
                                 ).then_inc(s_dbg.sem, 16)
                    sy.dma_start(out=dram["dbg_sa"][:], in_=SA[:]
                                 ).then_inc(s_dbg.sem, 16)
                    sy.dma_start(out=dram["dbg_sb"][:], in_=SB[:]
                                 ).then_inc(s_dbg.sem, 16)
                    sy.wait_ge(s_dbg.sem, s_dbg.inc(96))
                return

            @self.ve
            def _(v):
                v.wait_ge(s_v.sem, s_v.v)
                v.memset(rowstg[:, 193:256], 0.0).then_inc(s_v.sem, 1)
            s_v.inc()

            for tt in range(16):
                @self.te
                def _(te, tt=tt, hsl=hsl, ch=ch, hb=hb):
                    te.wait_ge(s_v.sem, s_v.v)
                    te.wait_ge(s_a.sem, s_a.v)
                    te.wait_ge(s_sc.sem, s_sc.v)
                    te.transpose(self.psT[:, 0:64],
                                 qkf[hsl, ch, 128 * tt:128 * (tt + 1)],
                                 self.identb[hsl, hsl],
                                 tile_position=(hb, 0)).then_inc(s_pe.sem, 1)
                s_pe.inc()

                @self.ae
                def _(a):
                    a.wait_ge(s_pe.sem, s_pe.v)
                    a.copy(out=rowstg[:, 0:64], in_=self.psT[:, 0:64])
                    a.drain()
                    a.activation(out=self.tmp[:, 256:320], in_=rowstg[:, 0:64],
                                 func=AF.Square, accum_out=self.tmp[:, 320:321])
                    a.drain()
                    a.activation(out=self.tmp[:, 321:322],
                                 in_=self.tmp[:, 320:321], func=AF.Sqrt).then_inc(s_a.sem, 1)
                s_a.inc()

                @self.te
                def _(te, tt=tt, hsl=hsl, ch=ch, hb=hb):
                    te.wait_ge(s_a.sem, s_a.v)
                    te.transpose(self.psT[:, 0:64],
                                 vf[hsl, ch, 128 * tt:128 * (tt + 1)],
                                 self.identb[hsl, hsl],
                                 tile_position=(hb, 0)).then_inc(s_pe.sem, 1)
                s_pe.inc()

                @self.ae
                def _(a):
                    a.wait_ge(s_pe.sem, s_pe.v)
                    a.copy(out=rowstg[:, 128:192], in_=self.psT[:, 0:64]).then_inc(s_a.sem, 1)
                s_a.inc()

                @self.ve
                def _(v, tt=tt):
                    v.wait_ge(s_a.sem, s_a.v)
                    v.reciprocal(out=self.tmp[:, 322:323], in_=self.tmp[:, 321:322])
                    v.drain()
                    v.tensor_scalar(out=self.tmp[:, 322:323],
                                    in0=self.tmp[:, 322:323], scalar1=0.125,
                                    scalar2=None, op0=OP.mult, op1=OP.bypass)
                    v.drain()
                    v.tensor_scalar(out=rowstg[:, 64:128], in0=rowstg[:, 0:64],
                                    scalar1=self.tmp[:, 322:323], scalar2=None,
                                    op0=OP.mult, op1=OP.bypass)
                    v.drain()
                    v.tensor_scalar(out=rowstg[:, 192:193], in0=self.iotap[:],
                                    scalar1=float(128 * tt), scalar2=None,
                                    op0=OP.add, op1=OP.bypass).then_inc(s_v.sem, 1)
                s_v.inc()

                @self.ge
                def _(g, hs=hs, tt=tt):
                    g.wait_ge(s_v.sem, s_v.v)
                    for r in range(4):
                        g.indirect_dma_start(
                            out=dram["s_sort%d" % hs],
                            out_offset=bass.IndirectOffsetOnAxis(
                                ap=rkc[:, tt, r:r + 1], axis=0),
                            in_=rowstg[:], in_offset=None
                            ).then_inc(s_sc.sem, 16)
                    for r in range(4):
                        g.indirect_dma_start(
                            out=dram["s_sort%d" % hs],
                            out_offset=bass.IndirectOffsetOnAxis(
                                ap=rk2[:, tt, r:r + 1], axis=0),
                            in_=rowstg[:], in_offset=None
                            ).then_inc(s_sc.sem, 16)
                s_sc.inc(128)

            @self.sy
            def _(sy, hs=hs):
                sy.wait_ge(s_sc.sem, s_sc.v)
                sy.dma_start(out=knf[:], in_=dram["s_sort%d" % hs][0:8320, 64:192],
                             transpose=True).then_inc(s_kn.sem, 16)
            s_kn.inc(16)
            kn_ld = s_kn.v

            @self.ve
            def _(v):
                v.wait_ge(s_v.sem, s_v.v)
                v.memset(self.tmp[64:128, 512:1664], 0.0)
                v.memset(self.tmp[0:64, 1952:2024], 0.0).then_inc(s_v.sem, 1)
            s_v.inc()

            @self.sy
            def _(sy, hs=hs):
                sy.wait_ge(s_v.sem, s_v.v)
                sy.dma_start(out=dram["s_so%d" % hs][0:64],
                             in_=self.tmp[0:64, 1952:2024]).then_inc(s_so.sem, 16)
                sy.dma_start(out=dram["s_so%d" % hs][8256:8320],
                             in_=self.tmp[0:64, 1952:2024]).then_inc(s_so.sem, 16)
            s_so.inc(32)

            win_used = {}
            wd_marks = {}
            dots_ps = [self.psA, self.psW[:, 0:512]]
            stj_ps = [self.psB, self.psW[:, 512:1024]]
            bqslot = [slice(768, 832), slice(832, 896)]
            stcol = [480, 481]

            def emit_load(k, sv0):
                kp = k % 2

                @self.sy
                def _(sy, hs=hs, k=k, kp=kp, sv0=sv0):
                    if k >= 2:
                        sy.wait_ge(s_pe.sem, win_used[k - 2])
                    else:
                        sy.wait_ge(s_v.sem, sv0)
                    sy.dma_start(out=winb[kp][:],
                                 in_=dram["s_sort%d" % hs][64 * k:64 * k + 128, :]
                                 ).then_inc(s_wd[kp].sem, 16)
                s_wd[kp].inc(16)
                wd_marks[k] = s_wd[kp].v

            def emit_front(k, kn_ld):
                kp = k % 2

                @self.ae
                def _(a, k=k, kp=kp, kn_ld=kn_ld):
                    a.wait_ge(s_wd[kp].sem, wd_marks[k])
                    if k == 0:
                        a.wait_ge(s_kn.sem, kn_ld)
                    wb = winb[kp]
                    a.copy(out=self.tmp[:, stcol[kp]:stcol[kp] + 1],
                           in_=wb[:, 192:193])
                    a.copy(out=tbf[64:128, bqslot[kp]], in_=wb[64:128, 0:64]
                           ).then_inc(s_a.sem, 1)
                s_a.inc()

                @self.te
                def _(te, kp=kp):
                    te.wait_ge(s_a.sem, s_a.v)
                    te.wait_ge(s_v.sem, s_v.v)
                    te.transpose(self.psO[0:1, 0:128],
                                 self.tmp[:, stcol[kp]:stcol[kp] + 1],
                                 self.ident[:]).then_inc(s_pe.sem, 1)
                s_pe.inc()

                @self.ve
                def _(v):
                    v.wait_ge(s_pe.sem, s_pe.v)
                    v.tensor_copy(out=STROW[:], in_=self.psO[0:1, 0:128]
                                  ).then_inc(s_v.sem, 1)
                s_v.inc()

                @self.te
                def _(te, kp=kp):
                    te.wait_ge(s_a.sem, s_a.v)
                    te.wait_ge(s_v.sem, s_v.v)
                    te.transpose(self.psT[0:64, 0:64], tbf[64:128, bqslot[kp]],
                                 self.identb[64:128, 64:128],
                                 tile_position=(64, 0)).then_inc(s_pe.sem, 1)
                s_pe.inc()

                @self.ae
                def _(a, kp=kp):
                    a.wait_ge(s_pe.sem, s_pe.v)
                    a.copy(out=tbf[0:64, bqslot[kp]], in_=self.psT[0:64, 0:64]
                           ).then_inc(s_a.sem, 1)
                s_a.inc()

                @self.te
                def _(te, k=k, kp=kp):
                    te.wait_ge(s_a.sem, s_a.v)
                    te.wait_ge(s_v.sem, s_v.v)
                    te.matmul(stj_ps[kp][64:128, 0:128], self.ones1[0:1, 0:64],
                              STROW[:], start=True, stop=True,
                              tile_position=(0, 64))
                    te.matmul(dots_ps[kp][64:128, 0:128], tbf[0:64, bqslot[kp]],
                              knf[0:64, 64 * k:64 * k + 128], start=True,
                              stop=True, tile_position=(0, 64)
                              ).then_inc(s_pe.sem, 1)
                s_pe.inc()

            sv0 = s_v.v
            emit_load(0, sv0)
            emit_load(1, sv0)
            emit_front(0, kn_ld)

            for c in range(128):
                par = c % 2

                @self.ve
                def _(v, par=par):
                    v.wait_ge(s_pe.sem, s_pe.v)
                    tm = self.tmp
                    sti = tm[64:128, stcol[par]:stcol[par] + 1]
                    stj = stj_ps[par][64:128, 0:128]
                    v.tensor_scalar(out=tm[64:128, 0:128], in0=stj, scalar1=sti,
                                    scalar2=0.0, op0=OP.subtract,
                                    op1=OP.not_equal)
                    v.tensor_scalar(out=tm[64:128, 128:256], in0=stj,
                                    scalar1=sti, scalar2=0.0, op0=OP.subtract,
                                    op1=OP.is_gt)
                    v.drain()
                    d = tm[64:128, 256:384]
                    v.tensor_tensor(out=d, in0=dots_ps[par][64:128, 0:128],
                                    in1=tm[64:128, 0:128], op=OP.mult)
                    v.drain()
                    v.scalar_tensor_tensor(out=d, in0=tm[64:128, 0:128],
                                           scalar=SHV, in1=d, op0=OP.mult,
                                           op1=OP.add)
                    v.drain()
                    v.scalar_tensor_tensor(out=d, in0=tm[64:128, 128:256],
                                           scalar=NEGB, in1=d, op0=OP.mult,
                                           op1=OP.add)
                    v.drain()
                    v.reduce_max(out=rcol[64:128, 0:1], in_=d, axis=AX.X,
                                 negate=True).then_inc(s_v.sem, 1)
                s_v.inc()

                @self.ae
                def _(a):
                    a.wait_ge(s_v.sem, s_v.v)
                    a.activation(out=tbf[64:128, 0:128],
                                 in_=self.tmp[64:128, 256:384], func=AF.Exp,
                                 bias=rcol[64:128, 0:1],
                                 accum_out=rcol[64:128, 1:2])
                    a.drain()
                    a.activation(out=rcol[64:128, 2:3], in_=rcol[64:128, 1:2],
                                 func=AF.Ln).then_inc(s_a.sem, 1)
                s_a.inc()

                @self.te
                def _(te):
                    te.wait_ge(s_a.sem, s_a.v)
                    te.transpose(self.psT[:, 0:64], tbf[64:128, 0:128],
                                 self.identb[64:128, 64:128],
                                 tile_position=(64, 0)).then_inc(s_pe.sem, 1)
                s_pe.inc()

                @self.ve
                def _(v):
                    v.wait_ge(s_pe.sem, s_pe.v)
                    v.tensor_copy(out=tbf[:, 896:960], in_=self.psT[:, 0:64]
                                  ).then_inc(s_v.sem, 1)
                s_v.inc()

                @self.te
                def _(te, par=par):
                    te.wait_ge(s_v.sem, s_v.v)
                    te.matmul(self.psO[64:128, 0:64], tbf[:, 896:960],
                              winb[par][:, 128:192], start=True,
                              stop=True, tile_position=(0, 64)
                              ).then_inc(s_pe.sem, 1)
                s_pe.inc()
                win_used[c] = s_pe.v

                @self.ve
                def _(v, c=c, so_prev=s_so.v):
                    v.wait_ge(s_pe.sem, s_pe.v)
                    if c % 16 == 0 and c > 0:
                        v.wait_ge(s_so.sem, so_prev)
                    ci = c % 16
                    v.reciprocal(out=rcol[64:128, 4:5], in_=rcol[64:128, 1:2])
                    v.tensor_tensor(out=rcol[64:128, 3:4],
                                    in0=rcol[64:128, 2:3],
                                    in1=rcol[64:128, 0:1], op=OP.subtract)
                    v.drain()
                    v.tensor_scalar(out=sostg[:, 72 * ci:72 * ci + 64],
                                    in0=self.psO[64:128, 0:64],
                                    scalar1=rcol[64:128, 4:5], scalar2=None,
                                    op0=OP.mult, op1=OP.bypass)
                    v.tensor_copy(out=sostg[:, 72 * ci + 64:72 * ci + 65],
                                  in_=rcol[64:128, 3:4]).then_inc(s_v.sem, 1)
                s_v.inc()

                if c % 16 == 15:
                    @self.sy
                    def _(sy, hs=hs, c=c):
                        sy.wait_ge(s_v.sem, s_v.v)
                        grp = c // 16
                        sy.dma_start(
                            out=dram["s_so%d" % hs][1024 * grp + 64:
                                                 1024 * (grp + 1) + 64]
                            .rearrange("(c q) f -> q c f", c=16),
                            in_=sostg[:].rearrange("p (c f) -> p c f", c=16)
                            ).then_inc(s_so.sem, 16)
                    s_so.inc(16)

                if c < 127:
                    if c + 2 <= 127:
                        emit_load(c + 2, sv0)
                    emit_front(c + 1, kn_ld)

            for tt in range(16):
                @self.ge
                def _(g, hs=hs, tt=tt, so_done=s_so.v, sv=s_v.v):
                    g.wait_ge(s_so.sem, so_done)
                    g.wait_ge(s_v.sem, sv)
                    for r in range(4):
                        g.indirect_dma_start(
                            out=ugb[:, r, :], out_offset=None,
                            in_=dram["s_so%d" % hs],
                            in_offset=bass.IndirectOffsetOnAxis(
                                ap=rkc[:, tt, r:r + 1], axis=0)
                            ).then_inc(s_ug.sem, 16)
                s_ug.inc(64)

                @self.ve
                def _(v):
                    v.wait_ge(s_ug.sem, s_ug.v)
                    for r in range(4):
                        v.tensor_copy(out=rcol[:, 4 + r:5 + r],
                                      in_=ugb[:, r, 64:65])
                    v.drain()
                    v.reduce_max(out=rcol[:, 0:1], in_=rcol[:, 4:8], axis=AX.X,
                                 negate=True).then_inc(s_v.sem, 1)
                s_v.inc()

                @self.ae
                def _(a):
                    a.wait_ge(s_v.sem, s_v.v)
                    a.activation(out=rcol[:, 4:8], in_=rcol[:, 4:8], func=AF.Exp,
                                 bias=rcol[:, 0:1], accum_out=rcol[:, 1:2]).then_inc(s_a.sem, 1)
                s_a.inc()

                @self.ve
                def _(v):
                    v.wait_ge(s_a.sem, s_a.v)
                    v.reciprocal(out=rcol[:, 2:3], in_=rcol[:, 1:2])
                    v.drain()
                    v.tensor_scalar(out=rcol[:, 4:8], in0=rcol[:, 4:8],
                                    scalar1=rcol[:, 2:3], scalar2=None,
                                    op0=OP.mult, op1=OP.bypass)
                    v.drain()
                    for r in range(4):
                        v.tensor_scalar(out=self.tmp[:, 64 * r:64 * r + 64],
                                        in0=ugb[:, r, 0:64],
                                        scalar1=rcol[:, 4 + r:5 + r],
                                        scalar2=None, op0=OP.mult,
                                        op1=OP.bypass)
                    v.drain()
                    v.tensor_tensor(out=self.tmp[:, 0:64], in0=self.tmp[:, 0:64],
                                    in1=self.tmp[:, 64:128], op=OP.add)
                    v.tensor_tensor(out=self.tmp[:, 128:192],
                                    in0=self.tmp[:, 128:192],
                                    in1=self.tmp[:, 192:256], op=OP.add)
                    v.drain()
                    v.tensor_tensor(out=self.tmp[:, 0:64], in0=self.tmp[:, 0:64],
                                    in1=self.tmp[:, 128:192],
                                    op=OP.add).then_inc(s_v.sem, 1)
                s_v.inc()

                @self.te
                def _(te):
                    te.wait_ge(s_v.sem, s_v.v)
                    te.transpose(self.psO[0:64, 0:128], self.tmp[:, 0:64],
                                 self.ident[:]).then_inc(s_pe.sem, 1)
                s_pe.inc()

                @self.ae
                def _(a, tt=tt, hsl=hsl, ch=ch):
                    a.wait_ge(s_pe.sem, s_pe.v)
                    a.copy(out=obuf[hsl, ch, 128 * tt:128 * (tt + 1)],
                           in_=self.psO[0:64, 0:128]).then_inc(s_a.sem, 1)
                s_a.inc()

    def _l1a_tail(self):
        x, dram = self.x, self.dram
        s_w, s_pe, s_v, s_a, s_ld = (self.s_w, self.s_pe, self.s_v, self.s_a,
                                     self.s_ld)
        obuf, tbf = self.obuf, self.tbf
        wout = self.av(96, [NCH, D], BF16)
        colv = self.av(112, [6, NCH])
        bff1 = self.av(112.25, [32])
        u = self.av(32, [NCH, T])
        x1 = self.av(0, [NCH, T], BF16)

        @self.ge
        def _(g):
            g.wait_ge(s_pe.sem, s_pe.v)
            g.dma_start(out=wout[:], in_=dram["l1a_wout"][:]).then_inc(s_w.sem, 16)
        s_w.inc(16)

        @self.sy
        def _(sy):
            sy.wait_ge(s_v.sem, s_v.v)
            sy.dma_start(out=colv[:], in_=dram["l1a_cols"][:]).then_inc(s_ld.sem, 16)
            sy.dma_start(out=bff1[:], in_=dram["l1a_bff1"][:]).then_inc(s_ld.sem, 16)
        s_ld.inc(32)

        for tg in range(4):
            sl = slice(512 * tg, 512 * (tg + 1))

            @self.te
            def _(te, sl=sl):
                te.wait_ge(s_v.sem, s_v.v)
                te.matmul(self.psA[:], self.ones1[0:1, :], self.R(0, sl),
                          start=True, stop=True)
                te.matmul(self.psB[:], self.ones1[32:33, :], self.R(1, sl),
                          start=True, stop=True).then_inc(s_pe.sem, 1)
            s_pe.inc()

            for o in range(NCH):
                @self.te
                def _(te, sl=sl, o=o):
                    te.wait_ge(s_w.sem, s_w.v)
                    te.wait_ge(s_v.sem, s_v.v)
                    te.wait_ge(self.s_a.sem, self.s_a.v)
                    for c in range(NCH):
                        mm = te.matmul(self.psO[:], wout[:, c, 128 * o:128 * (o + 1)],
                                       obuf[:, c, sl], start=(c == 0),
                                       stop=(c == NCH - 1))
                    mm.then_inc(s_pe.sem, 1)
                s_pe.inc()

                @self.ve
                def _(v, sl=sl, o=o):
                    v.wait_ge(s_pe.sem, s_pe.v)
                    v.wait_ge(s_ld.sem, s_ld.v)
                    t1 = self.tmp[:, 512:1024]
                    v.tensor_tensor(out=t1, in0=x[:, o, sl], in1=self.psA[:],
                                    op=OP.mult)
                    v.drain()
                    v.tensor_tensor(out=t1, in0=t1, in1=self.psB[:], op=OP.add)
                    v.drain()
                    v.tensor_tensor(out=t1, in0=t1, in1=self.psO[:], op=OP.add)
                    v.drain()
                    v.tensor_scalar(out=u[:, o, sl], in0=t1,
                                    scalar1=colv[:, 0, o:o + 1], scalar2=None,
                                    op0=OP.add, op1=OP.bypass).then_inc(s_v.sem, 1)
                s_v.inc()

        def ln_small(src):
            pe_at = {}
            for tg in range(4):
                sl = slice(512 * tg, 512 * (tg + 1))
                for c in range(NCH):
                    slot = c % 4

                    @self.ve
                    def _(v, sl=sl, c=c, slot=slot, tg=tg):
                        v.wait_ge(s_v.sem, s_v.v)
                        if (tg, c - 4) in pe_at:
                            v.wait_ge(s_pe.sem, pe_at[(tg, c - 4)])
                        elif tg > 0 and c < 4:
                            v.wait_ge(s_pe.sem, pe_at[(tg - 1, 4 + c)])
                        v.tensor_copy(out=tbf[:, 512 * slot:512 * (slot + 1)],
                                      in_=src[:, c, sl]).then_inc(s_v.sem, 1)
                    s_v.inc()

                    @self.ae
                    def _(a, sl=sl, c=c, slot=slot):
                        a.wait_ge(s_v.sem, s_v.v)
                        a.activation(out=tbf[:, 2048 + 512 * slot:2560 + 512 * slot],
                                     in_=src[:, c, sl], func=AF.Square).then_inc(s_a.sem, 1)
                    s_a.inc()

                    @self.te
                    def _(te, sl=sl, c=c, slot=slot):
                        te.wait_ge(s_v.sem, s_v.v)
                        te.wait_ge(s_a.sem, s_a.v)
                        te.matmul(self.psW[0:1, sl], self.onesb[:, 0:1],
                                  tbf[:, 512 * slot:512 * (slot + 1)],
                                  start=(c == 0), stop=(c == NCH - 1))
                        mm = te.matmul(self.psW[32:33, sl], self.onesb[:, 1:2],
                                       tbf[:, 2048 + 512 * slot:2560 + 512 * slot],
                                       start=(c == 0), stop=(c == NCH - 1))
                        mm.then_inc(s_pe.sem, 1)
                    s_pe.inc()
                    pe_at[(tg, c)] = s_pe.v
            self.rows_finish(r_mean=3, r_A=0, r_B=1)

        ln_small(u)

        for tg in range(4):
            sl = slice(512 * tg, 512 * (tg + 1))

            @self.te
            def _(te, sl=sl):
                te.wait_ge(s_v.sem, s_v.v)
                te.matmul(self.psA[:], self.ones1[0:1, :], self.R(0, sl),
                          start=True, stop=True)
                te.matmul(self.psB[:], self.ones1[32:33, :], self.R(1, sl),
                          start=True, stop=True).then_inc(s_pe.sem, 1)
            s_pe.inc()

            @self.ve
            def _(v, sl=sl):
                v.wait_ge(s_pe.sem, s_pe.v)
                for c in range(NCH):
                    t1 = self.tmp[:, 512:1024]
                    v.tensor_tensor(out=t1, in0=u[:, c, sl], in1=self.psA[:],
                                    op=OP.mult)
                    v.drain()
                    v.tensor_tensor(out=t1, in0=t1, in1=self.psB[:], op=OP.add)
                    v.drain()
                    v.tensor_scalar(out=x1[:, c, sl], in0=t1,
                                    scalar1=colv[:, 1, c:c + 1],
                                    scalar2=colv[:, 2, c:c + 1],
                                    op0=OP.mult, op1=OP.add)
                    v.drain()
                v.sem_inc(s_v.sem, 1)
            s_v.inc()

        if "x1" in self.taps:
            s_dx1 = self.sem("s_dx1")

            @self.sy
            def _(sy):
                sy.wait_ge(s_v.sem, s_v.v)
                sy.dma_start(out=dram["dbg_xa"][:], in_=x1[:]).then_inc(s_dx1.sem, 16)
                sy.wait_ge(s_dx1.sem, s_dx1.inc(16))

        wf1 = [self.av(96, [NCH, 128], BF16), self.av(98, [NCH, 128], BF16)]
        hstg = [self.av(100, [T], BF16), self.av(104, [T], BF16)]
        s_f1 = [self.sem("s_f1a"), self.sem("s_f1b")]
        s_h1 = [self.sem("s_h1a"), self.sem("s_h1b")]
        f1_used = {}
        gelu_at = {}
        hdma_at = {}

        for og in range(32):
            par = og % 2

            @self.ge
            def _(g, og=og, par=par):
                if og >= 2:
                    g.wait_ge(s_pe.sem, f1_used[og - 2])
                else:
                    g.wait_ge(s_v.sem, s_v.v)
                g.dma_start(out=wf1[par][:],
                            in_=dram["l1a_wff1"][:, :, 128 * og:128 * (og + 1)]
                            ).then_inc(s_f1[par].sem, 16)
            s_f1[par].inc(16)
            f1_ld = s_f1[par].v

            for tg in range(4):
                sl = slice(512 * tg, 512 * (tg + 1))

                @self.te
                def _(te, og=og, par=par, sl=sl, tg=tg, f1_ld=f1_ld):
                    te.wait_ge(s_f1[par].sem, f1_ld)
                    te.wait_ge(s_v.sem, s_v.v)
                    if (og - 1, tg) in gelu_at:
                        te.wait_ge(s_a.sem, gelu_at[(og - 1, tg)])
                    for c in range(NCH):
                        mm = te.matmul(self.psW[:, sl], wf1[par][:, c, :],
                                       x1[:, c, sl], start=(c == 0),
                                       stop=(c == NCH - 1))
                    mm.then_inc(s_pe.sem, 1)
                s_pe.inc()
                f1_used[og] = s_pe.v

                @self.ae
                def _(a, og=og, par=par, sl=sl, tg=tg):
                    a.wait_ge(s_pe.sem, s_pe.v)
                    if tg == 0 and og >= 2:
                        a.wait_ge(s_h1[par].sem, hdma_at[og - 2])
                    a.activation(out=hstg[par][:, sl], in_=self.psW[:, sl],
                                 func=AF.Gelu, bias=bff1[:, og:og + 1]).then_inc(s_a.sem, 1)
                s_a.inc()
                gelu_at[(og, tg)] = s_a.v

            @self.sy
            def _(sy, og=og, par=par):
                sy.wait_ge(s_a.sem, s_a.v)
                sy.dma_start(out=dram["s_h"][og], in_=hstg[par][:]
                             ).then_inc(s_h1[par].sem, 16)
            s_h1[par].inc(16)
            hdma_at[og] = s_h1[par].v

        acc = u
        for tg in range(4):
            sl = slice(512 * tg, 512 * (tg + 1))

            @self.ve
            def _(v, sl=sl):
                v.wait_ge(s_v.sem, s_v.v)
                for c in range(NCH):
                    v.tensor_scalar(out=acc[:, c, sl], in0=x1[:, c, sl],
                                    scalar1=colv[:, 5, c:c + 1], scalar2=None,
                                    op0=OP.add, op1=OP.bypass)
                v.drain()
                v.sem_inc(s_v.sem, 1)
            s_v.inc()

        def fftile(c, half):
            off = 256 * half
            if c < 4:
                return self.psW[:, 512 * c + off:512 * c + off + 256]
            if c < 6:
                return self.psA[:, 256 * (c - 4) + 0:256 * (c - 4) + 256]
            return self.psB[:, 256 * (c - 6) + 0:256 * (c - 6) + 256]

        wf2 = [self.av(96, [D], BF16), self.av(98, [D], BF16)]
        hsl2 = [self.av(100, [256], BF16), self.av(100.5, [256], BF16)]
        s_f2 = [self.sem("s_f2a"), self.sem("s_f2b")]
        s_hr = [self.sem("s_hra"), self.sem("s_hrb")]
        f2_used = {}
        hr_used = {}

        for tg in range(4):
            for half in range(2):
                tsl = slice(512 * tg + 256 * half, 512 * tg + 256 * (half + 1))
                for og in range(32):
                    par = og % 2
                    key = (tg, half, og)

                    @self.ge
                    def _(g, og=og, par=par, key=key):
                        kprev = (key[0], key[1], og - 2) if og >= 2 else None
                        if kprev in f2_used:
                            g.wait_ge(s_pe.sem, f2_used[kprev])
                        elif key[0] == 0 and key[1] == 0 and og < 2:
                            g.wait_ge(s_h1[0].sem, s_h1[0].v)
                            g.wait_ge(s_h1[1].sem, s_h1[1].v)
                            g.wait_ge(s_v.sem, s_v.v)
                        elif og < 2:
                            g.wait_ge(s_pe.sem, f2_used[(key[0], key[1] - 1, 30 + og)
                                      if key[1] else (key[0] - 1, 1, 30 + og)])
                        g.dma_start(out=wf2[par][:],
                                    in_=dram["l1a_wff2"][:, og, :]
                                    ).then_inc(s_f2[par].sem, 16)
                    s_f2[par].inc(16)
                    f2_ld = s_f2[par].v

                    @self.sy
                    def _(sy, og=og, par=par, key=key, tsl=tsl):
                        kprev = (key[0], key[1], og - 2) if og >= 2 else None
                        if kprev in hr_used:
                            sy.wait_ge(s_pe.sem, hr_used[kprev])
                        elif key[0] == 0 and key[1] == 0 and og < 2:
                            sy.wait_ge(s_h1[0].sem, s_h1[0].v)
                            sy.wait_ge(s_h1[1].sem, s_h1[1].v)
                            sy.wait_ge(s_v.sem, s_v.v)
                        elif og < 2:
                            sy.wait_ge(s_pe.sem, hr_used[(key[0], key[1] - 1, 30 + og)
                                       if key[1] else (key[0] - 1, 1, 30 + og)])
                        sy.dma_start(out=hsl2[par][:],
                                     in_=dram["s_h"][og][:, tsl]
                                     ).then_inc(s_hr[par].sem, 16)
                    s_hr[par].inc(16)
                    hr_ld = s_hr[par].v

                    @self.te
                    def _(te, og=og, par=par, half=half, f2_ld=f2_ld,
                          hr_ld=hr_ld):
                        te.wait_ge(s_f2[par].sem, f2_ld)
                        te.wait_ge(s_hr[par].sem, hr_ld)
                        if og == 0:
                            te.wait_ge(s_v.sem, s_v.v)
                        for c in range(NCH):
                            mm = te.matmul(fftile(c, half), wf2[par][:, 128 * c:
                                           128 * (c + 1)], hsl2[par][:],
                                           start=(og == 0), stop=(og == 31),
                                           skip_group_check=True)
                        mm.then_inc(s_pe.sem, 1)
                    s_pe.inc()
                    f2_used[key] = s_pe.v
                    hr_used[key] = s_pe.v

                @self.ve
                def _(v, tg=tg, half=half, tsl=tsl):
                    v.wait_ge(s_pe.sem, s_pe.v)
                    for c in range(NCH):
                        v.tensor_tensor(out=acc[:, c, tsl], in0=acc[:, c, tsl],
                                        in1=fftile(c, half), op=OP.add)
                    v.drain()
                    v.sem_inc(s_v.sem, 1)
                s_v.inc()

        ln_small(acc)
        for tg in range(4):
            sl = slice(512 * tg, 512 * (tg + 1))

            @self.te
            def _(te, sl=sl):
                te.wait_ge(s_v.sem, s_v.v)
                te.matmul(self.psA[:], self.ones1[0:1, :], self.R(0, sl),
                          start=True, stop=True)
                te.matmul(self.psB[:], self.ones1[32:33, :], self.R(1, sl),
                          start=True, stop=True).then_inc(s_pe.sem, 1)
            s_pe.inc()

            @self.ve
            def _(v, sl=sl):
                v.wait_ge(s_pe.sem, s_pe.v)
                for c in range(NCH):
                    t1 = self.tmp[:, 512:1024]
                    v.tensor_tensor(out=t1, in0=acc[:, c, sl], in1=self.psA[:],
                                    op=OP.mult)
                    v.drain()
                    v.tensor_tensor(out=t1, in0=t1, in1=self.psB[:], op=OP.add)
                    v.drain()
                    v.tensor_scalar(out=t1, in0=t1,
                                    scalar1=colv[:, 3, c:c + 1],
                                    scalar2=colv[:, 4, c:c + 1],
                                    op0=OP.mult, op1=OP.add)
                    v.drain()
                    v.tensor_tensor(out=x[:, c, sl], in0=x[:, c, sl], in1=t1,
                                    op=OP.add)
                    v.drain()
                v.sem_inc(s_v.sem, 1)
            s_v.inc()

    def l1_ffn(self):
        x, dram = self.x, self.dram
        s_ld, s_w, s_pe, s_v, s_a = (self.s_ld, self.s_w, self.s_pe, self.s_v,
                                     self.s_a)
        wr = self.av(0, [NCH, D], BF16)
        wk = self.av(16, [NCH, D], BF16)
        wv = self.av(32, [NCH, D], BF16)
        xr_bf = self.av(48, [NCH, 512], BF16)
        xk_bf = self.av(56, [NCH, 512], BF16)
        kact = self.av(64, [NCH, 512], BF16)
        r32 = self.av(72, [NCH, 512], F32)
        tail = self.av(104, [NCH, 2], F32)
        mu = self.av(105, [4, NCH], F32)

        self.ln_rows(xb=self.av(48, [NCH, T], BF16),
                     xsq=self.av(80, [NCH, T], BF16))

        @self.ge
        def _(g):
            g.wait_ge(s_v.sem, s_v.v)
            g.dma_start(out=wr[:], in_=dram["l1f_wr"][:]).then_inc(s_w.sem, 16)
            g.dma_start(out=wk[:], in_=dram["l1f_wk"][:]).then_inc(s_w.sem, 16)
            g.dma_start(out=wv[:], in_=dram["l1f_wv"][:]).then_inc(s_w.sem, 16)
        s_w.inc(48)

        @self.sy
        def _(sy):
            sy.wait_ge(s_v.sem, s_v.v)
            sy.dma_start(out=mu[:, 0, :], in_=dram["l1f_mur"][:]
                         ).then_inc(s_ld.sem, 16)
            sy.dma_start(out=mu[:, 2, :], in_=dram["l1f_muk"][:]
                         ).then_inc(s_ld.sem, 16)
        s_ld.inc(32)

        @self.ve
        def _(v):
            v.wait_ge(s_ld.sem, s_ld.v)
            v.wait_ge(s_v.sem, s_v.v)
            v.tensor_scalar(out=mu[:, 1, :], in0=mu[:, 0, :], scalar1=-1.0,
                            scalar2=1.0, op0=OP.mult, op1=OP.add)
            v.tensor_scalar(out=mu[:, 3, :], in0=mu[:, 2, :], scalar1=-1.0,
                            scalar2=1.0, op0=OP.mult, op1=OP.add)
            v.drain()
            v.memset(tail[:], 0.0).then_inc(s_v.sem, 1)
        s_v.inc()

        for tg in range(4):
            sl = slice(512 * tg, 512 * (tg + 1))

            @self.te
            def _(te, tg=tg, sl=sl):
                te.wait_ge(s_v.sem, s_v.v)
                pa, pb = self.Rp(0), self.Rp(1)
                te.matmul(self.psA[:], self.ones1[pa:pa + 1, :], self.R(0, sl),
                          start=True, stop=True)
                te.matmul(self.psB[:], self.ones1[pb:pb + 1, :], self.R(1, sl),
                          start=True, stop=True).then_inc(s_pe.sem, 1)
            s_pe.inc()

            @self.ve
            def _(v, tg=tg, sl=sl):
                v.wait_ge(s_pe.sem, s_pe.v)
                tpar = tg % 2
                for c in range(NCH):
                    v.tensor_tensor(out=self.tmp[:, 0:512], in0=x[:, c, sl],
                                    in1=self.psA[:], op=OP.mult)
                    v.drain()
                    v.tensor_tensor(out=self.tmp[:, 0:512], in0=self.tmp[:, 0:512],
                                    in1=self.psB[:], op=OP.add)
                    v.drain()
                    v.tensor_copy(out=self.tmp[:, 513:1024],
                                  in_=self.tmp[:, 0:511])
                    v.tensor_copy(out=self.tmp[:, 512:513],
                                  in_=tail[:, c, 0:1])
                    v.drain()
                    v.tensor_copy(out=tail[:, c, 0:1], in_=self.tmp[:, 511:512])
                    v.drain()
                    v.tensor_scalar(out=self.tmp[:, 1024:1536],
                                    in0=self.tmp[:, 0:512],
                                    scalar1=mu[:, 0, c:c + 1], scalar2=None,
                                    op0=OP.mult, op1=OP.bypass)
                    v.tensor_scalar(out=self.tmp[:, 1536:2048],
                                    in0=self.tmp[:, 512:1024],
                                    scalar1=mu[:, 1, c:c + 1], scalar2=None,
                                    op0=OP.mult, op1=OP.bypass)
                    v.drain()
                    v.tensor_tensor(out=xr_bf[:, c, :], in0=self.tmp[:, 1024:1536],
                                    in1=self.tmp[:, 1536:2048], op=OP.add)
                    v.drain()
                    v.tensor_scalar(out=self.tmp[:, 1024:1536],
                                    in0=self.tmp[:, 0:512],
                                    scalar1=mu[:, 2, c:c + 1], scalar2=None,
                                    op0=OP.mult, op1=OP.bypass)
                    v.tensor_scalar(out=self.tmp[:, 1536:2048],
                                    in0=self.tmp[:, 512:1024],
                                    scalar1=mu[:, 3, c:c + 1], scalar2=None,
                                    op0=OP.mult, op1=OP.bypass)
                    v.drain()
                    v.tensor_tensor(out=xk_bf[:, c, :], in0=self.tmp[:, 1024:1536],
                                    in1=self.tmp[:, 1536:2048], op=OP.add)
                    v.drain()
                v.sem_inc(s_v.sem, 1)
            s_v.inc()

            for o in range(NCH):
                @self.te
                def _(te, o=o):
                    te.wait_ge(s_v.sem, s_v.v)
                    te.wait_ge(s_w.sem, s_w.v)
                    if o > 0:
                        te.wait_ge(s_a.sem, s_a.v)
                    for c in range(NCH):
                        te.matmul(self.psA[:], wr[:, c, 128 * o:128 * (o + 1)],
                                  xr_bf[:, c, :], start=(c == 0),
                                  stop=(c == NCH - 1))
                    for c in range(NCH):
                        mm = te.matmul(self.psB[:], wk[:, c, 128 * o:128 * (o + 1)],
                                       xk_bf[:, c, :], start=(c == 0),
                                       stop=(c == NCH - 1))
                    mm.then_inc(s_pe.sem, 1)
                s_pe.inc()

                @self.ae
                def _(a, o=o):
                    a.wait_ge(s_pe.sem, s_pe.v)
                    a.activation(out=r32[:, o, :], in_=self.psA[:], func=AF.Sigmoid).then_inc(s_a.sem, 1)
                s_a.inc()

                @self.ve
                def _(v, o=o):
                    v.wait_ge(s_a.sem, s_a.v)
                    v.tensor_scalar(out=self.tmp[:, 0:512], in0=self.psB[:],
                                    scalar1=0.0, scalar2=None, op0=OP.max,
                                    op1=OP.bypass).then_inc(s_v.sem, 1)
                s_v.inc()

                @self.ae
                def _(a, o=o):
                    a.wait_ge(s_v.sem, s_v.v)
                    a.activation(out=kact[:, o, :], in_=self.tmp[:, 0:512],
                                 func=AF.Square).then_inc(s_a.sem, 1)
                s_a.inc()

            for o in range(NCH):
                @self.te
                def _(te, o=o):
                    te.wait_ge(s_a.sem, s_a.v)
                    if o > 0:
                        te.wait_ge(s_v.sem, s_v.v)
                    for c in range(NCH):
                        mm = te.matmul(self.psA[:], wv[:, c, 128 * o:128 * (o + 1)],
                                       kact[:, c, :], start=(c == 0),
                                       stop=(c == NCH - 1))
                    mm.then_inc(s_pe.sem, 1)
                s_pe.inc()

                @self.ve
                def _(v, o=o, sl=sl):
                    v.wait_ge(s_pe.sem, s_pe.v)
                    v.tensor_tensor(out=self.tmp[:, 0:512], in0=self.psA[:],
                                    in1=r32[:, o, :], op=OP.mult)
                    v.drain()
                    v.tensor_tensor(out=x[:, o, sl], in0=x[:, o, sl],
                                    in1=self.tmp[:, 0:512], op=OP.add).then_inc(s_v.sem, 1)
                s_v.inc()



def _prep(inputs):
    f32 = np.float32
    C = np.ascontiguousarray
    col = lambda b: C(np.asarray(b, f32).reshape(NCH, 128).T)

    wqkv = np.asarray(inputs["L0_attn_w_qkv"], f32)
    wv = C(wqkv[2048:].reshape(8, 128, NCH, 128).transpose(3, 2, 0, 1)
           .reshape(128, NCH, 1024))
    wqk_full = wqkv[:2048].reshape(2, 8, 128, NCH, 128)
    wqk_host = np.zeros((128, 16, NCH, 128), f32)
    for qi in range(2):
        for h in range(8):
            wqk_host[:, 2 * h + qi] = wqk_full[qi, h].transpose(2, 1, 0)
    wout = np.asarray(inputs["L0_attn_w_out"], f32)
    wout_host = C(wout.T.reshape(NCH, 128, D).transpose(1, 0, 2))

    def fm(w):
        w = np.asarray(w, f32)
        return C(w.T.reshape(NCH, 128, -1).transpose(1, 0, 2))

    gate_l = [np.asarray(inputs["L0_ffn_gate_w"], f32)[e] for e in range(8)] +              [np.asarray(inputs["L0_ffn_shared_gate_w"], f32)]
    up_l = [np.asarray(inputs["L0_ffn_up_w"], f32)[e] for e in range(8)] +            [np.asarray(inputs["L0_ffn_shared_up_w"], f32)]
    down_l = [np.asarray(inputs["L0_ffn_down_w"], f32)[e] for e in range(8)] +              [np.asarray(inputs["L0_ffn_shared_down_w"], f32)]
    gate_host = np.stack([g.T.reshape(NCH, 128, 4, 512).transpose(2, 1, 0, 3)
                          for g in gate_l])
    up_host = np.stack([g.T.reshape(NCH, 128, 4, 512).transpose(2, 1, 0, 3)
                        for g in up_l])
    down_host = np.stack([g.T.reshape(4, 4, 128, D).transpose(0, 2, 1, 3)
                          for g in down_l])
    pairs = [(e, e) for e in range(NCH)] +             [(e, f) for e in range(NCH) for f in range(e + 1, NCH)]
    sel_host = np.zeros((128, NCH, 36), f32)
    for j, (a, b) in enumerate(pairs):
        sel_host[:, a, j] = 1.0
        sel_host[:, b, j] = 1.0 if a != b else sel_host[0, b, j]
    for j, (a, b) in enumerate(pairs):
        if a != b:
            sel_host[:, a, j] = 1.0
            sel_host[:, b, j] = 1.0

    rr = np.asarray(inputs["L1_attn_rotations"], f32).reshape(64, 128)
    rot_host = C(np.concatenate([rr, rr], axis=0))
    cols_host = np.stack([col(inputs["L1_attn_" + k]) for k in
                          ("b_out", "ln1_g", "ln1_b", "ln2_g", "ln2_b", "b_ff2")],
                         axis=1)
    cols_host = C(cols_host)
    shared = {
        "l0a_wv": wv,
        "l0a_wqk": C(wqk_host),
        "l0a_gamma": col(inputs["L0_attn_gamma"]),
        "l0a_wout": wout_host,
        "l0f_ctxw": fm(inputs["L0_ffn_ctx_w"]),
        "l0f_ctxb": col(inputs["L0_ffn_ctx_b"]),
        "l0f_wih": fm(inputs["L0_ffn_gru_w_ih"]),
        "l0f_whh": fm(inputs["L0_ffn_gru_w_hh"]),
        "l0f_expr": fm(inputs["L0_ffn_expr_w"]),
        "l0f_gate": C(gate_host),
        "l0f_up": C(up_host),
        "l0f_down": C(down_host),
        "sel8": sel_host,
        "l1f_wr": fm(inputs["L1_ffn_W_r"]),
        "l1f_wk": fm(inputs["L1_ffn_W_k"]),
        "l1f_wv": fm(inputs["L1_ffn_W_v"]),
        "l1f_mur": col(inputs["L1_ffn_mu_r"]),
        "l1f_muk": col(inputs["L1_ffn_mu_k"]),
        "l1a_wqk": fm(inputs["L1_attn_w_qk"]),
        "l1a_wv": fm(inputs["L1_attn_w_v"]),
        "l1a_wout": fm(inputs["L1_attn_w_out"]),
        "l1a_rot": rot_host,
        "l1a_cols": cols_host,
        "l1a_bff1": C(np.asarray(inputs["L1_attn_b_ff1"], f32).reshape(32, 128).T),
        "l1a_wff1": fm(inputs["L1_attn_w_ff1"]),
        "l1a_wff2": C(np.asarray(inputs["L1_attn_w_ff2"], f32).T
                      .reshape(32, 128, D).transpose(1, 0, 2)),
    }
    x = np.asarray(inputs["L0_attn_x"], f32)
    maps = []
    for b in range(8):
        m = dict(shared)
        m["x0"] = C(x[b].T.reshape(NCH, 128, T).transpose(1, 0, 2))
        maps.append(m)
    return maps


def kernel(**inputs):
    nc = build_program()
    maps = _prep(inputs)
    res = run_bass_kernel_spmd(nc, maps, core_ids=list(range(8)))
    out = np.empty((8, T, D), np.float32)
    for b in range(8):
        y = res.results[b]["y"]
        out[b] = y.transpose(2, 1, 0).reshape(T, D)
    return out


# version 06 (final)
# speedup vs baseline: 1.0503x; 1.0209x over previous
import numpy as np
import ml_dtypes
import concourse.bass as bass
import concourse.mybir as mybir
from concourse.bass_utils import run_bass_kernel_spmd
from concourse.alu_op_type import AluOpType as OP
from contextlib import ExitStack

F32 = mybir.dt.float32
BF16 = mybir.dt.bfloat16
F16 = mybir.dt.float16
U32 = mybir.dt.uint32
AF = mybir.ActivationFunctionType
AX = mybir.AxisListType

D = 1024
NCH = 8
T = 2048
NEG = -1.0e38
SC_ATTN0 = 128.0 ** -0.5
ARENA_BYTES = 114 * 1024

DT_BYTES = {F32: 4, BF16: 2, F16: 2, U32: 4}


def build_program(taps=(), gru_T=T, blocks=("l0a", "l0f", "l1a", "l1f")):
    nc = bass.Bass("TRN2", target_bir_lowering=False)
    dram = {}

    def din(name, shape, dt=F32):
        dram[name] = nc.dram_tensor(name, shape, dt, kind="ExternalInput").ap()

    def dout(name, shape, dt=F32):
        dram[name] = nc.dram_tensor(name, shape, dt, kind="ExternalOutput").ap()

    din("x0", [128, NCH, T])
    din("l0a_wv", [128, NCH, 1024])
    din("l0a_wqk", [128, 16, NCH, 128])
    din("l0a_gamma", [128, NCH])
    din("l0a_wout", [128, NCH, D])
    din("l0f_ctxw", [128, NCH, D])
    din("l0f_ctxb", [128, NCH])
    din("l0f_wih", [128, NCH, 3072])
    din("l0f_whh", [128, NCH, 3072])
    din("l0f_expr", [128, NCH, D])
    din("l0f_gate", [9, 4, 128, NCH, 512])
    din("l0f_up", [9, 4, 128, NCH, 512])
    din("l0f_down", [9, 4, 128, 4, D])
    din("sel8", [128, NCH, 36])
    din("l1f_wr", [128, NCH, D])
    din("l1f_wk", [128, NCH, D])
    din("l1f_wv", [128, NCH, D])
    din("l1f_mur", [128, NCH])
    din("l1f_muk", [128, NCH])
    din("l1a_wqk", [128, NCH, D])
    din("l1a_wv", [128, NCH, D])
    din("l1a_wout", [128, NCH, D])
    din("l1a_rot", [128, 128])
    din("l1a_cols", [128, 6, NCH])
    din("l1a_bff1", [128, 32])
    din("l1a_wff1", [128, NCH, 4096])
    din("l1a_wff2", [128, 32, D])
    nsl = gru_T // 128
    dram["s_xw"] = nc.dram_tensor("s_xw", [nsl, 128, 3072], F32, kind="Internal").ap()
    dram["s_hs"] = nc.dram_tensor("s_hs", [nsl, 128, NCH, 128], F32, kind="Internal").ap()
    dram["s_ob"] = nc.dram_tensor("s_ob", [128, NCH, T], BF16, kind="Internal").ap()
    for _hs in range(12):
        dram[f"s_sort{_hs}"] = nc.dram_tensor(f"s_sort{_hs}", [8768, 256], F16,
                                              kind="Internal").ap()
        dram[f"s_so{_hs}"] = nc.dram_tensor(f"s_so{_hs}", [8320, 72], F32,
                                            kind="Internal").ap()
    dram["s_h"] = nc.dram_tensor("s_h", [32, 128, T], BF16, kind="Internal").ap()
    dout("y", [128, NCH, T])
    tapset = set(taps)
    if "l0a" in tapset:
        dout("dbg_l0a", [128, NCH, T])
    if "l0f" in tapset:
        dout("dbg_l0f", [128, NCH, T])
    if "hs" in tapset:
        pass
    if "xa" in tapset or "x1" in tapset:
        dout("dbg_xa", [128, NCH, T], BF16)
    if "obuf" in tapset:
        dout("dbg_obuf", [128, NCH, T], BF16)
    if "rk" in tapset:
        dout("dbg_rkc", [128, 16, 4], U32)
        dout("dbg_bt", [128, 16, 4])
        dout("dbg_tie", [128, 16, 4])
        dout("dbg_rows", [1, 2 * T])
        dout("dbg_sa", [128, T], F16)
        dout("dbg_sb", [128, T], F16)
    if "vtok" in tapset:
        dout("dbg_vtok", [128, 16, D], BF16)

    with ExitStack() as ctx:
        B = Builder(nc, ctx, dram, tapset)
        B.body(gru_T, blocks)
    return nc


class Ctr:
    def __init__(self, sem):
        self.sem = sem
        self.v = 0

    def inc(self, n=1):
        self.v += n
        return self.v


class Builder:
    def __init__(self, nc, ctx, dram, taps):
        self.nc, self.ctx, self.dram, self.taps = nc, ctx, dram, taps
        self.blk = ctx.enter_context(nc.Block())
        self._arn_off = 0

    def sbt(self, name, shape, dt=F32):
        return self.ctx.enter_context(self.nc.sbuf_tensor(name, shape, dt))

    def ps_(self, name, shape, dt=F32):
        return self.ctx.enter_context(self.nc.psum_tensor(name, shape, dt))

    def sem(self, name):
        return Ctr(self.ctx.enter_context(self.nc.semaphore(name)))

    def av(self, off_kb, shape, dt=F32):
        nbytes = int(np.prod(shape)) * DT_BYTES[dt]
        off = int(off_kb * 1024)
        assert off + nbytes <= ARENA_BYTES, (off_kb, shape, dt)
        v = self.arena[:, off:off + nbytes].bitcast(dt)
        if len(shape) == 1:
            return v
        pat = "p (" + " ".join(f"d{i}" for i in range(len(shape))) + ") -> p " + \
              " ".join(f"d{i}" for i in range(len(shape)))
        kw = {f"d{i}": s for i, s in enumerate(shape)}
        return v.rearrange(pat, **kw)

    def te(self, fn):
        self.blk.tensor(fn)

    def ve(self, fn):
        self.blk.vector(fn)

    def ae(self, fn):
        self.blk.scalar(fn)

    def ge(self, fn):
        self.blk.gpsimd(fn)

    def sy(self, fn):
        self.blk.sync(fn)

    ROWP = [0, 32, 64, 0, 32, 64]

    def R(self, i, sl=None):
        p = self.ROWP[i]
        base = (i // 3) * T
        r = self.rows[p:p + 1, base:base + T]
        return r if sl is None else r[:, sl]

    def Rp(self, i):
        return self.ROWP[i]

    def body(self, gru_T, blocks):
        self.x = x = self.sbt("x", [128, NCH, T])
        self.rows = self.sbt("rows", [65, 2 * T])
        self.tmp = self.sbt("tmpf", [128, T])
        self.ident = self.sbt("ident", [128, 128])
        self.identb = self.sbt("identb", [128, 128], BF16)
        self.ones1 = self.sbt("ones1", [65, 128])
        self.onesb = self.sbt("onesb", [128, 2], BF16)
        self.rcol = self.sbt("rcol", [128, 8])
        self.cb = self.sbt("cb", [128, 2])
        self.iota8 = self.sbt("iota8", [128, 8])
        self.rsel = self.sbt("rsel", [8, 1024], BF16)
        self.mask = self.sbt("maskd", [128, 128])
        self.iotap = self.sbt("iotap", [128, 1])
        self.iotap128 = self.sbt("iotap128", [128, 1])
        self.iotap2 = self.sbt("iotap2", [128, 1])
        self.dmp4 = self.sbt("dmp4", [128, 4])
        self.e16 = self.sbt("e16", [128, 16], F16)
        self.arena = self.sbt("arena", [128, ARENA_BYTES], mybir.dt.uint8)

        self.psA = self.ps_("psA", [128, 512])
        self.psB = self.ps_("psB", [128, 512])
        self.psW = self.ps_("psW", [128, 2048])
        self.psT = self.ps_("psT", [128, 128], BF16)
        self.psO = self.ps_("psO", [128, 512])

        self.s_ld = self.sem("s_ld")
        self.s_w = self.sem("s_w")
        self.s_pe = self.sem("s_pe")
        self.s_v = self.sem("s_v")
        self.s_a = self.sem("s_a")
        self.s_g = self.sem("s_g")
        self.s_out = self.sem("s_out")
        self.s_wq = [self.sem("s_wq0"), self.sem("s_wq1")]
        s_ld, s_v, s_g = self.s_ld, self.s_v, self.s_g

        @self.sy
        def _(sy):
            sy.dma_start(out=x[:], in_=self.dram["x0"][:]).then_inc(s_ld.sem, 16)
        s_ld.inc(16)

        @self.ge
        def _(g):
            g.iota(self.ident[:], pattern=[[-1, 128]], base=0, channel_multiplier=1,
                   allow_small_or_imprecise_dtypes=True)
            g.drain()
            g.iota(self.mask[:], pattern=[[1, 128]], base=0, channel_multiplier=-1,
                   allow_small_or_imprecise_dtypes=True)
            g.drain()
            g.iota(self.iota8[:], pattern=[[1, 8]], base=0, channel_multiplier=0,
                   allow_small_or_imprecise_dtypes=True)
            g.drain()
            g.iota(self.rsel[:].rearrange("p (a b) -> p a b", a=8),
                   pattern=[[1, 8], [0, 128]], base=0, channel_multiplier=-1,
                   allow_small_or_imprecise_dtypes=True)
            g.drain()
            g.iota(self.iotap[:], pattern=[[0, 1]], base=0, channel_multiplier=1,
                   allow_small_or_imprecise_dtypes=True).then_inc(s_g.sem, 1)
        s_g.inc()

        @self.ve
        def _(v):
            v.memset(self.ones1[:], 1.0)
            v.memset(self.onesb[:], 1.0)
            v.memset(self.cb[:, 0:1], 1e-5)
            v.memset(self.cb[:, 1:2], 1.0)
            v.wait_ge(s_g.sem, s_g.v)
            v.tensor_scalar(out=self.ident[:], in0=self.ident[:], scalar1=0.0,
                            scalar2=None, op0=OP.is_equal, op1=OP.bypass)
            v.drain()
            v.tensor_copy(out=self.identb[:], in_=self.ident[:])
            v.tensor_scalar(out=self.rsel[:], in0=self.rsel[:], scalar1=0.0,
                            scalar2=None, op0=OP.is_equal, op1=OP.bypass)
            v.tensor_scalar(out=self.mask[:], in0=self.mask[:], scalar1=0.0,
                            scalar2=NEG, op0=OP.is_gt, op1=OP.mult)
            v.drain()
            v.tensor_scalar(out=self.iotap128[:], in0=self.iotap[:], scalar1=128.0,
                            scalar2=None, op0=OP.add, op1=OP.bypass)
            v.tensor_scalar(out=self.iotap2[:], in0=self.iotap[:], scalar1=8256.0,
                            scalar2=None, op0=OP.add, op1=OP.bypass)
            v.drain()
            v.tensor_scalar(out=self.dmp4[:], in0=self.iota8[:, 0:4],
                            scalar1=128.0, scalar2=self.iotap2[:, 0:1],
                            op0=OP.mult, op1=OP.add)
            v.memset(self.e16[:], 0.0)
            v.drain()
            for r in range(4):
                v.memset(self.e16[:, 5 * r:5 * r + 1], 1.0)
            v.drain()
            v.sem_inc(s_v.sem, 1)
        s_v.inc()

        if "l0a" in blocks:
            self.l0_attn()
        if "l0f" in blocks:
            self.l0_ffn(gru_T)
        if "l1a" in blocks:
            self.l1_attn()
        if "l1f" in blocks:
            self.l1_ffn()

        @self.sy
        def _(sy):
            sy.wait_ge(self.s_v.sem, self.s_v.v)
            sy.dma_start(out=self.dram["y"][:], in_=x[:]).then_inc(self.s_out.sem, 16)
            sy.wait_ge(self.s_out.sem, self.s_out.inc(16))

    def ln_rows(self, xb, xsq, r_mean=3, r_A=0, r_B=1, l2mode=False, src=None):
        x = self.x if src is None else src
        s_v, s_a, s_pe = self.s_v, self.s_a, self.s_pe
        psS = self.psW
        R = self.R

        @self.ve
        def _(v):
            v.wait_ge(self.s_ld.sem, self.s_ld.v)
            v.wait_ge(s_v.sem, s_v.v)
            for c in range(NCH):
                v.tensor_copy(out=xb[:, c, :], in_=x[:, c, :])
            v.drain()
            v.sem_inc(s_v.sem, 1)
        s_v.inc()

        @self.ae
        def _(a):
            a.wait_ge(s_v.sem, s_v.v)
            for c in range(NCH):
                a.activation(out=xsq[:, c, :], in_=x[:, c, :], func=AF.Square)
            a.drain()
            a.sem_inc(s_a.sem, 1)
        s_a.inc()

        @self.te
        def _(te):
            te.wait_ge(s_a.sem, s_a.v)
            te.wait_ge(s_pe.sem, s_pe.v)
            for tg in range(4):
                sl = slice(512 * tg, 512 * (tg + 1))
                for c in range(NCH):
                    te.matmul(psS[0:1, sl], self.onesb[:, 0:1], xb[:, c, sl],
                              start=(c == 0), stop=(c == NCH - 1))
                for c in range(NCH):
                    mm = te.matmul(psS[32:33, sl], self.onesb[:, 0:1], xsq[:, c, sl],
                                   start=(c == 0), stop=(c == NCH - 1))
            mm.then_inc(s_pe.sem, 1)
        s_pe.inc()

        self.rows_finish(r_mean, r_A, r_B, l2mode)

    def rows_finish(self, r_mean=3, r_A=0, r_B=1, l2mode=False):
        s_v, s_a, s_pe = self.s_v, self.s_a, self.s_pe
        psS = self.psW
        R = self.R
        rv, rt = 4, 5

        @self.ve
        def _(v):
            v.wait_ge(s_pe.sem, s_pe.v)
            v.tensor_scalar(out=R(r_mean), in0=psS[0:1, :], scalar1=1.0 / D,
                            scalar2=None, op0=OP.mult, op1=OP.bypass)
            v.drain()
            v.tensor_tensor(out=R(rv), in0=R(r_mean), in1=R(r_mean), op=OP.mult)
            v.drain()
            v.scalar_tensor_tensor(out=R(rv), in0=psS[32:33, :], scalar=1.0 / D,
                                   in1=R(rv), op0=OP.mult, op1=OP.subtract)
            v.drain()
            v.tensor_scalar(out=R(rv), in0=R(rv), scalar1=0.0, scalar2=None,
                            op0=OP.max, op1=OP.bypass).then_inc(s_v.sem, 1)
        s_v.inc()

        @self.ae
        def _(a):
            a.wait_ge(s_v.sem, s_v.v)
            a.activation(out=R(rt), in_=R(rv), func=AF.Sqrt,
                         bias=self.cb[32:33, 0:1]).then_inc(s_a.sem, 1)
        s_a.inc()

        @self.ve
        def _(v):
            v.wait_ge(s_a.sem, s_a.v)
            v.reciprocal(out=R(r_A), in_=R(rt)).then_inc(s_v.sem, 1)
        s_v.inc()

        if l2mode:
            t0 = self.tmp[0:1, 0:T]

            @self.ae
            def _(a):
                a.wait_ge(s_v.sem, s_v.v)
                a.activation(out=t0, in_=R(rv), func=AF.Sqrt).then_inc(s_a.sem, 1)
            s_a.inc()

            @self.ve
            def _(v):
                v.wait_ge(s_a.sem, s_a.v)
                v.tensor_tensor(out=t0, in0=t0, in1=R(r_A), op=OP.mult)
                v.drain()
                v.tensor_scalar(out=t0, in0=t0, scalar1=32.0, scalar2=1e-12,
                                op0=OP.mult, op1=OP.max)
                v.drain()
                v.reciprocal(out=t0, in_=t0)
                v.drain()
                v.tensor_tensor(out=R(r_A), in0=R(r_A), in1=t0, op=OP.mult)
                v.drain()
                v.tensor_scalar(out=R(r_A), in0=R(r_A), scalar1=32.0, scalar2=None,
                                op0=OP.mult, op1=OP.bypass).then_inc(s_v.sem, 1)
            s_v.inc()

        @self.ve
        def _(v):
            v.wait_ge(s_v.sem, s_v.v)
            v.tensor_tensor(out=R(r_B), in0=R(r_mean), in1=R(r_A), op=OP.mult)
            v.drain()
            v.tensor_scalar(out=R(r_B), in0=R(r_B), scalar1=-1.0, scalar2=None,
                            op0=OP.mult, op1=OP.bypass).then_inc(s_v.sem, 1)
        s_v.inc()

    def apply_affine(self, dst_bf, r_A, r_B, percol=None):
        x, s_v, s_pe = self.x, self.s_v, self.s_pe
        for tg in range(4):
            sl = slice(512 * tg, 512 * (tg + 1))

            @self.te
            def _(te, tg=tg, sl=sl):
                te.wait_ge(s_v.sem, s_v.v)
                pa, pb = self.Rp(r_A), self.Rp(r_B)
                te.matmul(self.psA[:], self.ones1[pa:pa + 1, :], self.R(r_A, sl),
                          start=True, stop=True)
                te.matmul(self.psB[:], self.ones1[pb:pb + 1, :], self.R(r_B, sl),
                          start=True, stop=True).then_inc(s_pe.sem, 1)
            s_pe.inc()

            @self.ve
            def _(v, tg=tg, sl=sl):
                v.wait_ge(s_pe.sem, s_pe.v)
                for c in range(NCH):
                    v.tensor_tensor(out=self.tmp[:, 0:512], in0=x[:, c, sl],
                                    in1=self.psA[:], op=OP.mult)
                    v.drain()
                    if percol is None:
                        v.tensor_tensor(out=dst_bf[:, c, sl], in0=self.tmp[:, 0:512],
                                        in1=self.psB[:], op=OP.add)
                    else:
                        v.tensor_tensor(out=self.tmp[:, 0:512], in0=self.tmp[:, 0:512],
                                        in1=self.psB[:], op=OP.add)
                        v.drain()
                        v.tensor_scalar(out=dst_bf[:, c, sl], in0=self.tmp[:, 0:512],
                                        scalar1=percol[:, c:c + 1], scalar2=None,
                                        op0=OP.mult, op1=OP.bypass)
                    v.drain()
                v.sem_inc(s_v.sem, 1)
            s_v.inc()

    def l0_attn(self):
        x, dram = self.x, self.dram
        s_ld, s_w, s_pe, s_v, s_a = (self.s_ld, self.s_w, self.s_pe, self.s_v,
                                     self.s_a)
        xa = self.av(0, [NCH, T], BF16)
        vtok = self.av(32, [16, D], BF16)
        wv = self.av(64, [NCH, D], BF16)
        wqk = self.av(64, [2, 2, NCH, 128], BF16)
        qh = self.av(72, [T], BF16)
        kh = self.av(76, [T], BF16)
        pbuf = self.av(80, [T], BF16)
        ptb = self.av(84, [128], BF16)
        gam = self.av(84.5, [NCH], F32)
        obuf = self.av(85, [T], BF16)
        oprj = self.av(89, [NCH, 512], BF16)
        of32 = self.tmp
        rrow = self.R(2)
        rcol = self.rcol

        @self.ge
        def _(g):
            g.dma_start(out=wv[:], in_=dram["l0a_wv"][:]).then_inc(s_w.sem, 16)
            g.dma_start(out=gam[:], in_=dram["l0a_gamma"][:]).then_inc(s_w.sem, 16)
        s_w.inc(32)

        self.ln_rows(xb=self.av(32, [NCH, T], BF16), xsq=xa, l2mode=True)

        @self.ve
        def _(v):
            v.wait_ge(s_w.sem, s_w.v)
            v.drain()
            v.sem_inc(s_v.sem, 1)
        s_v.inc()
        self.apply_affine(xa, 0, 1, percol=gam)

        if "xa" in self.taps:
            s_dx = self.sem("s_dxa")

            @self.sy
            def _(sy):
                sy.wait_ge(s_v.sem, s_v.v)
                sy.dma_start(out=dram["dbg_xa"][:], in_=xa[:]).then_inc(s_dx.sem, 16)
                sy.wait_ge(s_dx.sem, 16)

        for tt in range(16):
            @self.te
            def _(te, tt=tt):
                te.wait_ge(s_v.sem, s_v.v)
                for c in range(NCH):
                    te.matmul(self.psA[:], xa[:, c, 128 * tt:128 * (tt + 1)],
                              wv[:, c, 0:512], start=(c == 0), stop=(c == NCH - 1))
                for c in range(NCH):
                    mm = te.matmul(self.psB[:], xa[:, c, 128 * tt:128 * (tt + 1)],
                                   wv[:, c, 512:1024], start=(c == 0),
                                   stop=(c == NCH - 1))
                mm.then_inc(s_pe.sem, 1)
            s_pe.inc()

            @self.ve
            def _(v, tt=tt):
                v.wait_ge(s_pe.sem, s_pe.v)
                v.tensor_copy(out=vtok[:, tt, 0:512], in_=self.psA[:])
                v.tensor_copy(out=vtok[:, tt, 512:1024], in_=self.psB[:]).then_inc(s_v.sem, 1)
            s_v.inc()
        vtok_done = s_v.v

        for h in range(8):
            par = h % 2

            s_wq = self.s_wq[par]

            @self.ge
            def _(g, h=h, par=par, vtok_done=vtok_done, s_wq=s_wq):
                if h >= 2:
                    g.wait_ge(s_v.sem, self._qk_used[h - 2])
                else:
                    g.wait_ge(s_v.sem, vtok_done)
                g.dma_start(out=wqk[:, par],
                            in_=dram["l0a_wqk"][:, 2 * h:2 * h + 2]
                            ).then_inc(s_wq.sem, 16)
            s_wq.inc(16)
            qk_ld = s_wq.v

            for tg in range(4):
                @self.te
                def _(te, h=h, tg=tg, par=par, qk_ld=qk_ld, s_wq=s_wq):
                    te.wait_ge(s_v.sem, s_v.v)
                    te.wait_ge(s_wq.sem, qk_ld)
                    slq = slice(512 * tg, 512 * (tg + 1))
                    for c in range(NCH):
                        te.matmul(self.psA[:], wqk[:, par, 0, c], xa[:, c, slq],
                                  start=(c == 0), stop=(c == NCH - 1))
                    for c in range(NCH):
                        mm = te.matmul(self.psB[:], wqk[:, par, 1, c], xa[:, c, slq],
                                       start=(c == 0), stop=(c == NCH - 1))
                    mm.then_inc(s_pe.sem, 1)
                s_pe.inc()

                @self.ve
                def _(v, tg=tg):
                    v.wait_ge(s_pe.sem, s_pe.v)
                    v.tensor_copy(out=qh[:, 512 * tg:512 * (tg + 1)], in_=self.psA[:])
                    v.tensor_copy(out=kh[:, 512 * tg:512 * (tg + 1)], in_=self.psB[:]).then_inc(s_v.sem, 1)
                s_v.inc()
            if not hasattr(self, "_qk_used"):
                self._qk_used = {}
            self._qk_used[h] = s_v.v

            for it in range(16):
                jend = 128 * (it + 1)
                ng = (jend + 511) // 512

                @self.te
                def _(te, it=it, jend=jend, ng=ng):
                    te.wait_ge(s_v.sem, s_v.v)
                    te.wait_ge(s_a.sem, s_a.v)
                    for g in range(ng):
                        n = min(512, jend - 512 * g)
                        mm = te.matmul(self.psW[:, 512 * g:512 * g + n],
                                       qh[:, 128 * it:128 * (it + 1)],
                                       kh[:, 512 * g:512 * g + n], start=True,
                                       stop=True)
                    mm.then_inc(s_pe.sem, 1)
                s_pe.inc()

                @self.ve
                def _(v, it=it, jend=jend):
                    v.wait_ge(s_pe.sem, s_pe.v)
                    js = 128 * it
                    v.tensor_tensor(out=self.psW[:, js:js + 128],
                                    in0=self.psW[:, js:js + 128],
                                    in1=self.mask[:], op=OP.add)
                    v.drain()
                    v.reduce_max(out=rcol[:, 0:1], in_=self.psW[:, 0:jend],
                                 axis=AX.X, negate=True)
                    v.drain()
                    v.tensor_scalar(out=rcol[:, 1:2], in0=rcol[:, 0:1],
                                    scalar1=SC_ATTN0, scalar2=None, op0=OP.mult,
                                    op1=OP.bypass).then_inc(s_v.sem, 1)
                s_v.inc()

                @self.ae
                def _(a, it=it, jend=jend):
                    a.wait_ge(s_v.sem, s_v.v)
                    a.activation(out=pbuf[:, 0:jend], in_=self.psW[:, 0:jend],
                                 func=AF.Exp, bias=rcol[:, 1:2], scale=SC_ATTN0,
                                 accum_out=rcol[:, 2:3]).then_inc(s_a.sem, 1)
                s_a.inc()

                @self.ve
                def _(v):
                    v.wait_ge(s_a.sem, s_a.v)
                    v.reciprocal(out=rcol[:, 3:4], in_=rcol[:, 2:3]).then_inc(s_v.sem, 1)
                s_v.inc()

                @self.te
                def _(te, it=it):
                    te.wait_ge(s_v.sem, s_v.v)
                    te.matmul(self.psO[0:1, 0:128], rcol[:, 3:4], self.ident[:],
                              start=True, stop=True).then_inc(s_pe.sem, 1)
                s_pe.inc()

                @self.ve
                def _(v, it=it):
                    v.wait_ge(s_pe.sem, s_pe.v)
                    v.tensor_copy(out=rrow[:, 128 * it:128 * (it + 1)],
                                  in_=self.psO[0:1, 0:128]).then_inc(s_v.sem, 1)
                s_v.inc()

                for jt in range(it + 1):
                    @self.te
                    def _(te, jt=jt):
                        te.wait_ge(s_a.sem, s_a.v)
                        if jt == 0:
                            te.wait_ge(s_v.sem, s_v.v)
                        te.transpose(self.psT[:], pbuf[:, 128 * jt:128 * (jt + 1)],
                                     self.identb[:]).then_inc(s_pe.sem, 1)
                    s_pe.inc()

                    @self.ve
                    def _(v, jt=jt):
                        v.wait_ge(s_pe.sem, s_pe.v)
                        v.tensor_copy(out=ptb[:], in_=self.psT[:]
                                      ).then_inc(s_v.sem, 1)
                    s_v.inc()

                    @self.te
                    def _(te, h=h, it=it, jt=jt):
                        te.wait_ge(s_v.sem, s_v.v)
                        te.matmul(self.psO[:, 0:128],
                                  vtok[:, jt, 128 * h:128 * (h + 1)],
                                  ptb[:], start=(jt == 0), stop=(jt == it)
                                  ).then_inc(s_pe.sem, 1)
                    s_pe.inc()

                @self.ve
                def _(v, it=it):
                    v.wait_ge(s_pe.sem, s_pe.v)
                    v.tensor_copy(out=of32[:, 128 * it:128 * (it + 1)],
                                  in_=self.psO[:, 0:128]).then_inc(s_v.sem, 1)
                s_v.inc()

            for tg in range(4):
                @self.te
                def _(te, tg=tg, h=h):
                    te.wait_ge(s_v.sem, s_v.v)
                    if h > 0 and tg == 0:
                        te.wait_ge(self.s_ld.sem, self._ob_st[h - 1])
                    te.matmul(self.psA[:], self.ones1[64:65, :],
                              rrow[:, 512 * tg:512 * (tg + 1)], start=True,
                              stop=True).then_inc(s_pe.sem, 1)
                s_pe.inc()

                @self.ve
                def _(v, h=h, tg=tg):
                    v.wait_ge(s_pe.sem, s_pe.v)
                    v.tensor_tensor(out=obuf[:, 512 * tg:512 * (tg + 1)],
                                    in0=of32[:, 512 * tg:512 * (tg + 1)],
                                    in1=self.psA[:], op=OP.mult).then_inc(s_v.sem, 1)
                s_v.inc()

            if not hasattr(self, "_ob_st"):
                self._ob_st = {}

            @self.sy
            def _(sy, h=h):
                sy.wait_ge(s_v.sem, s_v.v)
                sy.dma_start(out=dram["s_ob"][:, h, :], in_=obuf[:]
                             ).then_inc(self.s_ld.sem, 16)
            self.s_ld.inc(16)
            self._ob_st[h] = self.s_ld.v

        if "obuf" in self.taps:
            s_do = self.sem("s_dob")

            @self.sy
            def _(sy):
                sy.wait_ge(s_v.sem, s_v.v)
                sy.dma_start(out=dram["dbg_obuf"][:], in_=obuf[:]).then_inc(s_do.sem, 16)
                sy.wait_ge(s_do.sem, 16)

        s_dv = None
        if "vtok" in self.taps:
            s_dv = self.sem("s_dvt")

            @self.sy
            def _(sy, vtok_done=vtok_done):
                sy.wait_ge(s_v.sem, vtok_done)
                sy.dma_start(out=dram["dbg_vtok"][:], in_=vtok[:]).then_inc(s_dv.sem, 16)
                sy.wait_ge(s_dv.sem, 16)

        wout = self.av(32, [NCH, D], BF16)

        @self.ge
        def _(g, s_dv=s_dv):
            g.wait_ge(s_v.sem, s_v.v)
            if s_dv is not None:
                g.wait_ge(s_dv.sem, 16)
            g.dma_start(out=wout[:], in_=dram["l0a_wout"][:]).then_inc(s_w.sem, 16)
        s_w.inc(16)

        s_op = self.sem("s_oprj")
        for tg in range(4):
            sl = slice(512 * tg, 512 * (tg + 1))

            @self.sy
            def _(sy, tg=tg, sl=sl):
                sy.wait_ge(self.s_ld.sem, self._ob_st[7])
                if tg > 0:
                    sy.wait_ge(s_v.sem, self._op_used)
                sy.dma_start(out=oprj[:], in_=dram["s_ob"][:, :, sl]
                             ).then_inc(s_op.sem, 16)
            s_op.inc(16)

            for o in range(NCH):
                @self.te
                def _(te, tg=tg, o=o, sl=sl):
                    te.wait_ge(s_v.sem, s_v.v)
                    te.wait_ge(s_w.sem, s_w.v)
                    te.wait_ge(s_op.sem, s_op.v)
                    pd = self.psA if o % 2 == 0 else self.psB
                    for c in range(NCH):
                        mm = te.matmul(pd[:], wout[:, c, 128 * o:128 * (o + 1)],
                                       oprj[:, c, :], start=(c == 0),
                                       stop=(c == NCH - 1))
                    mm.then_inc(s_pe.sem, 1)
                s_pe.inc()

                @self.ve
                def _(v, tg=tg, o=o, sl=sl):
                    v.wait_ge(s_pe.sem, s_pe.v)
                    pd = self.psA if o % 2 == 0 else self.psB
                    v.tensor_tensor(out=x[:, o, sl], in0=x[:, o, sl], in1=pd[:],
                                    op=OP.add).then_inc(s_v.sem, 1)
                s_v.inc()
                self._op_used = s_v.v

        if "l0a" in self.taps:
            s_dbg = self.sem("s_dbg")

            @self.sy
            def _(sy):
                sy.wait_ge(s_v.sem, s_v.v)
                sy.dma_start(out=dram["dbg_l0a"][:], in_=x[:]).then_inc(s_dbg.sem, 16)
                sy.wait_ge(s_dbg.sem, 16)


    def l0_ffn(self, gru_T):
        x, dram = self.x, self.dram
        s_ld, s_w, s_pe, s_v, s_a, s_g = (self.s_ld, self.s_w, self.s_pe,
                                          self.s_v, self.s_a, self.s_g)
        NSL = gru_T // 128
        hid = self.av(0, [NCH, T], BF16)
        self.ln_rows(xb=self.av(32, [NCH, T], BF16),
                     xsq=self.av(64, [NCH, T], BF16))
        self.apply_affine(hid, 0, 1)

        ctxw = self.av(32, [NCH, D], BF16)
        ctxb = self.av(112.0, [NCH], F32)
        mvec = self.av(112.125, [NCH], BF16)
        h_fp = self.av(112.25, [NCH], F32)
        h_bf = self.av(112.375, [NCH], BF16)

        @self.ge
        def _(g):
            g.wait_ge(s_v.sem, s_v.v)
            g.dma_start(out=ctxw[:], in_=dram["l0f_ctxw"][:]).then_inc(s_w.sem, 16)
        s_w.inc(16)

        @self.sy
        def _(sy):
            sy.wait_ge(s_v.sem, s_v.v)
            sy.dma_start(out=ctxb[:], in_=dram["l0f_ctxb"][:]).then_inc(s_ld.sem, 16)
        s_ld.inc(16)

        @self.ve
        def _(v):
            v.wait_ge(s_v.sem, s_v.v)
            for c in range(NCH):
                v.tensor_reduce(out=self.rcol[:, 0:1], in_=hid[:, c, :], axis=AX.X,
                                op=OP.add)
                v.drain()
                v.tensor_scalar(out=mvec[:, c:c + 1], in0=self.rcol[:, 0:1],
                                scalar1=1.0 / T, scalar2=None, op0=OP.mult,
                                op1=OP.bypass)
                v.drain()
            v.sem_inc(s_v.sem, 1)
        s_v.inc()

        @self.te
        def _(te):
            te.wait_ge(s_v.sem, s_v.v)
            te.wait_ge(s_w.sem, s_w.v)
            for o in range(NCH):
                for c in range(NCH):
                    mm = te.matmul(self.psO[:, o:o + 1],
                                   ctxw[:, c, 128 * o:128 * (o + 1)],
                                   mvec[:, c:c + 1], start=(c == 0),
                                   stop=(c == NCH - 1))
            mm.then_inc(s_pe.sem, 1)
        s_pe.inc()

        @self.ve
        def _(v):
            v.wait_ge(s_pe.sem, s_pe.v)
            v.wait_ge(self.s_ld.sem, self.s_ld.v)
            v.tensor_tensor(out=h_fp[:], in0=self.psO[:, 0:NCH], in1=ctxb[:],
                            op=OP.add)
            v.drain()
            v.tensor_copy(out=h_bf[:], in_=h_fp[:]).then_inc(s_v.sem, 1)
        s_v.inc()

        wih = self.av(32, [NCH, 3072], BF16)
        xst = [self.av(80, [24, 128], F32), self.av(92, [24, 128], F32)]
        s_xs = [self.sem("s_xs0"), self.sem("s_xs1")]

        @self.ge
        def _(g):
            g.wait_ge(s_v.sem, s_v.v)
            g.dma_start(out=wih[:], in_=dram["l0f_wih"][:]).then_inc(s_w.sem, 16)
        s_w.inc(16)
        self._xst_free = {}

        for sl_i in range(NSL):
            par = sl_i % 2
            tsl = slice(128 * sl_i, 128 * (sl_i + 1))
            for ph, (o0, o1) in enumerate([(0, 16), (16, 24)]):
                @self.te
                def _(te, sl_i=sl_i, par=par, tsl=tsl, ph=ph, o0=o0, o1=o1):
                    te.wait_ge(s_w.sem, s_w.v)
                    te.wait_ge(s_v.sem, s_v.v)
                    if ph == 0 and sl_i >= 2:
                        te.wait_ge(s_xs[par].sem, self._xst_free[sl_i - 2])
                    for o in range(o0, o1):
                        for c in range(NCH):
                            mm = te.matmul(
                                self.psW[:, 128 * (o - o0):128 * (o - o0) + 128],
                                wih[:, c, 128 * o:128 * (o + 1)],
                                hid[:, c, tsl], start=(c == 0),
                                stop=(c == NCH - 1))
                    mm.then_inc(s_pe.sem, 1)
                s_pe.inc()

                @self.ve
                def _(v, par=par, o0=o0, o1=o1):
                    v.wait_ge(s_pe.sem, s_pe.v)
                    v.tensor_copy(out=xst[par][:, o0:o1, :],
                                  in_=self.psW[:, 0:128 * (o1 - o0)].rearrange(
                                      "p (a b) -> p a b", a=o1 - o0)).then_inc(s_v.sem, 1)
                s_v.inc()

            @self.sy
            def _(sy, sl_i=sl_i, par=par):
                sy.wait_ge(s_v.sem, s_v.v)
                sy.dma_start(out=dram["s_xw"][sl_i],
                             in_=xst[par][:].rearrange("p a b -> p (a b)")
                             ).then_inc(s_xs[par].sem, 16)
            s_xs[par].inc(16)
            self._xst_free[sl_i] = s_xs[par].v

        whh = self.av(32, [NCH, 3072], BF16)
        xin = [self.av(80, [24, 128], F32), self.av(92, [24, 128], F32)]
        hst = [self.av(104, [NCH, 128], F32), self.av(108, [NCH, 128], F32)]
        trz = self.av(112.5, [16], F32)
        tnn = self.av(112.625, [8], F32)
        grz = self.av(112.75, [16], F32)
        nwt = self.av(112.875, [8], F32)
        dlt = self.av(113.0, [8], F32)
        zbc = self.av(113.125, [8], F32)
        s_xi = [self.sem("s_xi0"), self.sem("s_xi1")]
        s_ho = [self.sem("s_ho0"), self.sem("s_ho1")]
        s_sh = self.sem("s_sh")
        s_srz = self.sem("s_srz")
        s_sn = self.sem("s_sn")
        s_sv1 = self.sem("s_sv1")
        s_sa1 = self.sem("s_sa1")
        s_sv2 = self.sem("s_sv2")
        s_sa2 = self.sem("s_sa2")
        s_hof = self.sem("s_hof")

        @self.ge
        def _(g):
            g.wait_ge(s_xs[0].sem, s_xs[0].v)
            g.wait_ge(s_xs[1].sem, s_xs[1].v)
            g.dma_start(out=whh[:], in_=dram["l0f_whh"][:]).then_inc(s_w.sem, 16)
        s_w.inc(16)
        whh_ld = s_w.v

        xs_done = [s_xs[0].v, s_xs[1].v]

        def emit_store(sy, sl_i):
            hpar = sl_i % 2
            sy.wait_ge(s_hof.sem, sl_i + 1)
            sy.dma_start(out=dram["s_hs"][sl_i],
                         in_=hst[hpar][:].rearrange("p a b -> p (a b)")
                         ).then_inc(s_ho[hpar].sem, 16)
            s_ho[hpar].inc(16)

        @self.sy
        def _(sy):
            for sl_i in range(NSL):
                par = sl_i % 2
                if sl_i < 2:
                    sy.wait_ge(s_xs[par].sem, xs_done[par])
                else:
                    sy.wait_ge(s_sv2.sem, 128 * (sl_i - 1))
                sy.dma_start(out=xin[par][:], in_=dram["s_xw"][sl_i]
                             ).then_inc(s_xi[par].sem, 16)
                if sl_i >= 2:
                    emit_store(sy, sl_i - 2)
            for sl_i in range(max(NSL - 2, 0), NSL):
                emit_store(sy, sl_i)

        @self.ve
        def _(v):
            v.wait_ge(s_v.sem, s_v.v)
            v.sem_inc(s_sh.sem, 1)
        s_sh.inc()

        @self.te
        def _(te):
            te.wait_ge(s_w.sem, whh_ld)
            for t in range(gru_T):
                te.wait_ge(s_sh.sem, t + 1)
                for o in range(24):
                    out = (self.psA[:, o:o + 1] if o < 16
                           else self.psB[:, o - 16:o - 15])
                    for c in range(NCH):
                        mm = te.matmul(out, whh[:, c, 128 * o:128 * (o + 1)],
                                       h_bf[:, c:c + 1], start=(c == 0),
                                       stop=(c == NCH - 1))
                    if o == 15:
                        mm.then_inc(s_srz.sem, 1)
                    elif o == 23:
                        mm.then_inc(s_sn.sem, 1)

        @self.ve
        def _(v):
            for t in range(gru_T):
                sl_i = t // 128
                trel = t % 128
                par = sl_i % 2
                xs = xin[par]
                hpar = sl_i % 2
                v.wait_ge(s_srz.sem, t + 1)
                if trel == 0:
                    v.wait_ge(s_xi[par].sem, 16 * (sl_i // 2 + 1))
                    if sl_i >= 2:
                        v.wait_ge(s_ho[hpar].sem, 16 * (sl_i // 2))
                v.tensor_tensor(out=trz[:], in0=self.psA[:, 0:16],
                                in1=xs[:, 0:16, trel], op=OP.add
                                ).then_inc(s_sv1.sem, 1)
                v.wait_ge(s_sa1.sem, t + 1)
                v.wait_ge(s_sn.sem, t + 1)
                v.tensor_tensor(out=tnn[:], in0=self.psB[:, 0:8], in1=grz[:, 0:8],
                                op=OP.mult)
                v.drain()
                v.tensor_tensor(out=tnn[:], in0=tnn[:], in1=xs[:, 16:24, trel],
                                op=OP.add).then_inc(s_sv2.sem, 1)
                v.tensor_tensor(out=dlt[:], in0=h_fp[:], in1=grz[:, 8:16],
                                op=OP.mult)
                v.tensor_scalar(out=zbc[:], in0=grz[:, 8:16], scalar1=-1.0,
                                scalar2=1.0, op0=OP.mult, op1=OP.add)
                v.drain()
                v.wait_ge(s_sa2.sem, t + 1)
                v.tensor_tensor(out=zbc[:], in0=nwt[:], in1=zbc[:], op=OP.mult)
                v.drain()
                v.tensor_tensor(out=h_fp[:], in0=zbc[:], in1=dlt[:], op=OP.add)
                v.drain()
                cp = v.tensor_copy(out=hst[hpar][:, :, trel], in_=h_fp[:])
                if trel == 127:
                    cp.then_inc(s_hof.sem, 1)
                v.tensor_copy(out=h_bf[:], in_=h_fp[:]).then_inc(s_sh.sem, 1)
        s_sh.inc(gru_T)
        s_srz.inc(gru_T)
        s_sn.inc(gru_T)
        s_sv1.inc(gru_T)
        s_sv2.inc(gru_T)
        s_hof.inc(NSL)

        @self.ae
        def _(a):
            for t in range(gru_T):
                a.wait_ge(s_sv1.sem, t + 1)
                a.activation(out=grz[:], in_=trz[:], func=AF.Sigmoid
                             ).then_inc(s_sa1.sem, 1)
                a.wait_ge(s_sv2.sem, t + 1)
                a.activation(out=nwt[:], in_=tnn[:], func=AF.Tanh
                             ).then_inc(s_sa2.sem, 1)
        s_sa1.inc(gru_T)
        s_sa2.inc(gru_T)

        @self.ve
        def _(v):
            v.wait_ge(s_ho[0].sem, s_ho[0].v)
            v.wait_ge(s_ho[1].sem, s_ho[1].v)
            v.sem_inc(s_v.sem, 1)
        s_v.inc()

        self.routing(NSL)
        self.experts()

    def routing(self, NSL):
        x, dram = self.x, self.dram
        s_ld, s_w, s_pe, s_v, s_a = (self.s_ld, self.s_w, self.s_pe, self.s_v,
                                     self.s_a)
        hid = self.av(0, [NCH, T], BF16)
        exw = self.av(32, [NCH, D], BF16)
        hsf = self.av(48, [NCH, 128], F32)
        hstk = self.av(52, [NCH, 128], F32)
        expf = self.av(56, [NCH, 128], F32)
        extk = self.av(60, [NCH, 128], F32)
        gsq = self.av(64, [80], F32)
        nrm = self.av(65, [16], F32)
        sel = self.av(66, [NCH, 36], F32)
        sco = self.av(68, [16, NCH], F32)
        cwt = self.av(70, [16, NCH], F32)
        pen = self.av(72, [16], F32)
        cwfm = self.rows[0:65, T:T + 1024].bitcast(BF16).rearrange(
            "p (a b) -> p a b", a=16)
        s_hr = self.sem("s_hr")
        self._rt_done = None

        @self.ge
        def _(g):
            g.wait_ge(s_v.sem, s_v.v)
            g.dma_start(out=exw[:], in_=dram["l0f_expr"][:]).then_inc(s_w.sem, 16)
        s_w.inc(16)

        @self.sy
        def _(sy):
            sy.dma_start(out=sel[:], in_=dram["sel8"][:]).then_inc(s_ld.sem, 16)
        s_ld.inc(16)

        for tt in range(16):
            @self.sy
            def _(sy, tt=tt):
                if tt > 0:
                    sy.wait_ge(s_v.sem, self._rt_done)
                sy.dma_start(out=hsf[:].rearrange("p a b -> p (a b)"),
                             in_=dram["s_hs"][tt]).then_inc(s_hr.sem, 16)
            s_hr.inc(16)

            for c in range(NCH):
                @self.te
                def _(te, tt=tt, c=c):
                    if c == 0:
                        te.wait_ge(s_hr.sem, s_hr.v)
                        te.wait_ge(s_v.sem, s_v.v)
                        te.wait_ge(s_w.sem, s_w.v)
                    te.wait_ge(s_a.sem, s_a.v)
                    te.transpose(self.psO[:, 0:128], hsf[:, c, :], self.ident[:]
                                 ).then_inc(s_pe.sem, 1)
                s_pe.inc()

                @self.ae
                def _(a_, c=c):
                    a_.wait_ge(s_pe.sem, s_pe.v)
                    a_.copy(out=hstk[:, c, :], in_=self.psO[:, 0:128])
                    a_.drain()
                    a_.sem_inc(s_a.sem, 1)
                s_a.inc()

            @self.te
            def _(te, tt=tt):
                for o in range(NCH):
                    for c in range(NCH):
                        mm = te.matmul(self.psW[:, 128 * o:128 * (o + 1)],
                                       exw[:, c, 128 * o:128 * (o + 1)],
                                       hid[:, c, 128 * tt:128 * (tt + 1)],
                                       start=(c == 0), stop=(c == NCH - 1))
                mm.then_inc(s_pe.sem, 1)
            s_pe.inc()

            @self.ve
            def _(v):
                v.wait_ge(s_pe.sem, s_pe.v)
                v.tensor_copy(out=expf[:], in_=self.psW[:, 0:1024].rearrange(
                    "p (a b) -> p a b", a=NCH)).then_inc(s_v.sem, 1)
            s_v.inc()

            for c in range(NCH):
                @self.te
                def _(te, c=c):
                    if c == 0:
                        te.wait_ge(s_v.sem, s_v.v)
                    te.wait_ge(s_a.sem, s_a.v)
                    te.transpose(self.psO[:, 0:128], expf[:, c, :], self.ident[:]
                                 ).then_inc(s_pe.sem, 1)
                s_pe.inc()

                @self.ae
                def _(a_, c=c):
                    a_.wait_ge(s_pe.sem, s_pe.v)
                    a_.copy(out=extk[:, c, :], in_=self.psO[:, 0:128])
                    a_.drain()
                    a_.sem_inc(s_a.sem, 1)
                s_a.inc()

            @self.ae
            def _(a, tt=tt):
                a.wait_ge(s_a.sem, s_a.v)
                for e in range(NCH):
                    a.activation(out=self.tmp[:, 0:128], in_=hstk[:, e, :],
                                 func=AF.Square, accum_out=nrm[:, e:e + 1])
                    a.drain()
                for e in range(NCH):
                    a.activation(out=self.tmp[:, 0:128], in_=extk[:, e, :],
                                 func=AF.Square, accum_out=nrm[:, 8 + e:9 + e])
                    a.drain()
                a.activation(out=nrm[:, 0:16], in_=nrm[:, 0:16], func=AF.Sqrt).then_inc(s_a.sem, 1)
            s_a.inc()

            @self.ve
            def _(v, tt=tt):
                v.wait_ge(s_a.sem, s_a.v)
                for e in range(NCH):
                    v.tensor_scalar(out=self.rcol[:, 0:1], in0=nrm[:, e:e + 1],
                                    scalar1=1e-12, scalar2=None, op0=OP.max,
                                    op1=OP.bypass)
                    v.drain()
                    v.reciprocal(out=self.rcol[:, 1:2], in_=self.rcol[:, 0:1])
                    v.drain()
                    v.tensor_scalar(out=hstk[:, e, :], in0=hstk[:, e, :],
                                    scalar1=self.rcol[:, 1:2], scalar2=None,
                                    op0=OP.mult, op1=OP.bypass)
                    v.drain()
                    v.tensor_tensor(out=nrm[:, e:e + 1], in0=nrm[:, e:e + 1],
                                    in1=self.rcol[:, 1:2], op=OP.mult)
                    v.drain()
                v.sem_inc(s_v.sem, 1)
            s_v.inc()

            pairs = [(e, e) for e in range(NCH)] + \
                    [(e, f) for e in range(NCH) for f in range(e + 1, NCH)]

            @self.ve
            def _(v, tt=tt, pairs=pairs):
                v.wait_ge(s_v.sem, s_v.v)
                for j, (e, f) in enumerate(pairs):
                    v.tensor_tensor(out=self.tmp[:, 0:128], in0=hstk[:, e, :],
                                    in1=hstk[:, f, :], op=OP.mult)
                    v.drain()
                    v.tensor_reduce(out=gsq[:, j:j + 1], in_=self.tmp[:, 0:128],
                                    axis=AX.X, op=OP.add)
                    v.drain()
                for e in range(NCH):
                    v.tensor_tensor(out=self.tmp[:, 0:128], in0=extk[:, e, :],
                                    in1=hstk[:, e, :], op=OP.mult)
                    v.drain()
                    v.tensor_reduce(out=gsq[:, 40 + e:41 + e],
                                    in_=self.tmp[:, 0:128], axis=AX.X, op=OP.add)
                    v.drain()
                v.tensor_scalar(out=gsq[:, 0:8], in0=gsq[:, 0:8], scalar1=-1.0,
                                scalar2=None, op0=OP.add, op1=OP.bypass).then_inc(s_v.sem, 1)
            s_v.inc()

            @self.ae
            def _(a, tt=tt):
                a.wait_ge(s_v.sem, s_v.v)
                a.activation(out=gsq[:, 0:36], in_=gsq[:, 0:36], func=AF.Square).then_inc(s_a.sem, 1)
            s_a.inc()

            @self.ve
            def _(v, tt=tt):
                v.wait_ge(s_a.sem, s_a.v)
                v.wait_ge(s_ld.sem, s_ld.v)
                for e in range(NCH):
                    v.tensor_tensor(out=self.tmp[:, 0:36], in0=gsq[:, 0:36],
                                    in1=sel[:, e, :], op=OP.mult)
                    v.drain()
                    v.tensor_reduce(out=gsq[:, 50 + e:51 + e],
                                    in_=self.tmp[:, 0:36], axis=AX.X, op=OP.add)
                    v.drain()
                v.sem_inc(s_v.sem, 1)
            s_v.inc()

            @self.ae
            def _(a, tt=tt):
                a.wait_ge(s_v.sem, s_v.v)
                a.activation(out=gsq[:, 58:66], in_=gsq[:, 50:58], func=AF.Sqrt).then_inc(s_a.sem, 1)
            s_a.inc()

            @self.ve
            def _(v, tt=tt):
                v.wait_ge(s_a.sem, s_a.v)
                v.tensor_scalar(out=self.tmp[:, 0:8], in0=gsq[:, 58:66],
                                scalar1=1e-12, scalar2=None, op0=OP.max,
                                op1=OP.bypass)
                v.drain()
                v.reciprocal(out=self.tmp[:, 8:16], in_=self.tmp[:, 0:8])
                v.drain()
                v.tensor_tensor(out=self.tmp[:, 8:16], in0=self.tmp[:, 8:16],
                                in1=self.tmp[:, 8:16], op=OP.mult)
                v.drain()
                v.tensor_tensor(out=self.tmp[:, 0:8], in0=gsq[:, 50:58],
                                in1=self.tmp[:, 8:16], op=OP.mult)
                v.drain()
                v.tensor_reduce(out=pen[:, tt:tt + 1], in_=self.tmp[:, 0:8],
                                axis=AX.X, op=OP.add)
                v.drain()
                v.tensor_scalar(out=self.tmp[:, 16:24], in0=nrm[:, 8:16],
                                scalar1=1e-8, scalar2=None, op0=OP.max,
                                op1=OP.bypass)
                v.drain()
                v.tensor_scalar(out=self.tmp[:, 24:32], in0=nrm[:, 0:8],
                                scalar1=1e-8, scalar2=None, op0=OP.max,
                                op1=OP.bypass)
                v.drain()
                v.tensor_tensor(out=self.tmp[:, 16:24], in0=self.tmp[:, 16:24],
                                in1=self.tmp[:, 24:32], op=OP.mult)
                v.drain()
                v.reciprocal(out=self.tmp[:, 16:24], in_=self.tmp[:, 16:24])
                v.drain()
                v.tensor_tensor(out=self.tmp[:, 16:24], in0=gsq[:, 40:48],
                                in1=self.tmp[:, 16:24], op=OP.mult)
                v.drain()
                v.tensor_scalar(out=sco[:, tt, :], in0=self.tmp[:, 16:24],
                                scalar1=-1.0, scalar2=1.0, op0=OP.mult, op1=OP.add).then_inc(s_v.sem, 1)
            s_v.inc()
            self._rt_done = s_v.v

        @self.ve
        def _(v):
            v.wait_ge(s_v.sem, s_v.v)
            v.tensor_reduce(out=self.rcol[:, 0:1], in_=pen[:, 0:16], axis=AX.X,
                            op=OP.add).then_inc(s_v.sem, 1)
        s_v.inc()

        @self.te
        def _(te):
            te.wait_ge(s_v.sem, s_v.v)
            te.matmul(self.psO[0:1, 0:1], self.cb[:, 1:2], self.rcol[:, 0:1],
                      start=True, stop=True).then_inc(s_pe.sem, 1)
        s_pe.inc()

        @self.ve
        def _(v):
            v.wait_ge(s_pe.sem, s_pe.v)
            v.tensor_scalar(out=self.rcol[0:1, 2:3], in0=self.psO[0:1, 0:1],
                            scalar1=1.0 / T, scalar2=1.0, op0=OP.mult, op1=OP.add).then_inc(s_v.sem, 1)
        s_v.inc()

        @self.te
        def _(te):
            te.wait_ge(s_v.sem, s_v.v)
            te.matmul(self.psO[:, 0:1], self.ones1[0:1, :], self.rcol[0:1, 2:3],
                      start=True, stop=True).then_inc(s_pe.sem, 1)
        s_pe.inc()

        @self.ve
        def _(v):
            v.wait_ge(s_pe.sem, s_pe.v)
            v.tensor_copy(out=self.rcol[:, 3:4], in_=self.psO[:, 0:1])
            v.drain()
            for tt in range(16):
                v.tensor_scalar(out=sco[:, tt, :], in0=sco[:, tt, :],
                                scalar1=self.rcol[:, 3:4], scalar2=None,
                                op0=OP.mult, op1=OP.bypass)
                v.drain()
            v.sem_inc(s_v.sem, 1)
        s_v.inc()

        for tt in range(16):
            @self.ve
            def _(v, tt=tt):
                v.wait_ge(s_v.sem, s_v.v)
                v.wait_ge(s_a.sem, s_a.v)
                v.max(out=self.tmp[:, 24:32], in_=sco[:, tt, :])
                v.drain()
                v.max_index(out=self.tmp[:, 32:40].bitcast(U32),
                            in_max=self.tmp[:, 24:32], in_values=sco[:, tt, :])
                v.drain()
                v.tensor_copy(out=self.rcol[:, 0:1], in_=self.tmp[:, 24:25])
                v.tensor_copy(out=self.rcol[:, 4:5], in_=self.tmp[:, 25:26])
                v.drain()
                v.tensor_copy(out=self.rcol[:, 2:3],
                              in_=self.tmp[:, 32:33].bitcast(U32))
                v.tensor_copy(out=self.rcol[:, 6:7],
                              in_=self.tmp[:, 33:34].bitcast(U32))
                v.drain()
                v.tensor_tensor(out=self.rcol[:, 7:8], in0=self.rcol[:, 4:5],
                                in1=self.rcol[:, 0:1], op=OP.subtract
                                ).then_inc(s_v.sem, 1)
            s_v.inc()
            v_now = s_v.v

            @self.ae
            def _(a, v_now=v_now):
                a.wait_ge(s_v.sem, v_now)
                a.activation(out=self.rcol[:, 7:8], in_=self.rcol[:, 7:8],
                             func=AF.Sigmoid).then_inc(s_a.sem, 1)
            s_a.inc()

            @self.ve
            def _(v, tt=tt):
                v.wait_ge(s_a.sem, s_a.v)
                v.tensor_scalar(out=self.tmp[:, 0:8], in0=self.iota8[:, 0:8],
                                scalar1=self.rcol[:, 2:3], scalar2=None,
                                op0=OP.is_equal, op1=OP.bypass)
                v.drain()
                v.tensor_scalar(out=self.tmp[:, 8:16], in0=self.iota8[:, 0:8],
                                scalar1=self.rcol[:, 6:7], scalar2=None,
                                op0=OP.is_equal, op1=OP.bypass)
                v.drain()
                v.tensor_scalar(out=self.tmp[:, 16:17], in0=self.rcol[:, 7:8],
                                scalar1=-1.0, scalar2=1.0, op0=OP.mult,
                                op1=OP.add)
                v.drain()
                v.tensor_scalar(out=self.tmp[:, 0:8], in0=self.tmp[:, 0:8],
                                scalar1=self.tmp[:, 16:17], scalar2=None,
                                op0=OP.mult, op1=OP.bypass)
                v.drain()
                v.tensor_scalar(out=self.tmp[:, 8:16], in0=self.tmp[:, 8:16],
                                scalar1=self.rcol[:, 7:8], scalar2=None,
                                op0=OP.mult, op1=OP.bypass)
                v.drain()
                v.tensor_tensor(out=cwt[:, tt, :], in0=self.tmp[:, 0:8],
                                in1=self.tmp[:, 8:16], op=OP.add).then_inc(s_v.sem, 1)
            s_v.inc()

        for tt in range(16):
            @self.te
            def _(te, tt=tt):
                te.wait_ge(s_v.sem, s_v.v)
                te.transpose(self.psO[0:NCH, 0:128], cwt[:, tt, :], self.ident[:]
                             ).then_inc(s_pe.sem, 1)
            s_pe.inc()

            @self.ve
            def _(v, tt=tt):
                v.wait_ge(s_pe.sem, s_pe.v)
                v.tensor_copy(out=cwfm[0:NCH, tt, :], in_=self.psO[0:NCH, 0:128]).then_inc(s_v.sem, 1)
            s_v.inc()
        self.cwfm = cwfm

    def experts(self):
        x, dram = self.x, self.dram
        s_ld, s_w, s_pe, s_v, s_a = (self.s_ld, self.s_w, self.s_pe, self.s_v,
                                     self.s_a)
        hid = self.av(0, [NCH, T], BF16)
        wg = [self.av(32, [NCH, 512], BF16), self.av(40, [NCH, 512], BF16)]
        wu = [self.av(48, [NCH, 512], BF16), self.av(56, [NCH, 512], BF16)]
        wd4 = self.av(64, [4, 4, D], BF16)
        hbuf = self.av(96, [16, 512], BF16)
        silu = self.av(112, [512], BF16)
        cwb8 = self.tmp[:, 1024:1536]
        cwfm = self.cwfm
        s_wx = [self.sem("s_wx0"), self.sem("s_wx1")]
        gup_done = {}
        dend = {}
        v_start = s_v.v

        for q in range(4):
            for e in range(9):
                eg = q * 9 + e
                if e < 8:
                    for ti in range(4):
                        tti = 4 * q + ti

                        @self.te
                        def _(te, e=e, tti=tti):
                            te.wait_ge(s_v.sem, s_v.v)
                            te.matmul(self.psO[:, 0:128],
                                      self.rsel[:, 128 * e:128 * (e + 1)],
                                      cwfm[0:NCH, tti, :],
                                      start=True, stop=True).then_inc(s_pe.sem, 1)
                        s_pe.inc()

                        @self.ve
                        def _(v, ti=ti):
                            v.wait_ge(s_pe.sem, s_pe.v)
                            v.tensor_copy(out=cwb8[:, 128 * ti:128 * (ti + 1)],
                                          in_=self.psO[:, 0:128]).then_inc(s_v.sem, 1)
                        s_v.inc()

                for fs in range(4):
                    par = fs % 2

                    @self.ge
                    def _(g, e=e, fs=fs, eg=eg, par=par):
                        if fs >= 2:
                            g.wait_ge(s_pe.sem, gup_done[(eg, fs - 2)])
                        elif eg > 0:
                            g.wait_ge(s_pe.sem, dend[eg - 1])
                        else:
                            g.wait_ge(s_v.sem, v_start)
                        g.dma_start(out=wg[par][:], in_=dram["l0f_gate"][e, fs]
                                    ).then_inc(s_wx[par].sem, 16)
                        g.dma_start(out=wu[par][:], in_=dram["l0f_up"][e, fs]
                                    ).then_inc(s_wx[par].sem, 16)
                        g.dma_start(out=wd4[:, fs],
                                    in_=dram["l0f_down"][e, fs]
                                    ).then_inc(s_wx[par].sem, 16)
                    s_wx[par].inc(48)
                    wld = s_wx[par].v

                    for tg in range(4):
                        tsl = slice(512 * q + 128 * tg, 512 * q + 128 * (tg + 1))

                        @self.te
                        def _(te, tg=tg, par=par, wld=wld, tsl=tsl):
                            if tg == 0:
                                te.wait_ge(s_wx[par].sem, wld)
                            te.wait_ge(s_v.sem, s_v.v)
                            for ft in range(4):
                                fsl = slice(128 * ft, 128 * (ft + 1))
                                for c in range(NCH):
                                    te.matmul(self.psA[:, fsl],
                                              wg[par][:, c, fsl],
                                              hid[:, c, tsl], start=(c == 0),
                                              stop=(c == NCH - 1))
                            for ft in range(4):
                                fsl = slice(128 * ft, 128 * (ft + 1))
                                for c in range(NCH):
                                    mm = te.matmul(self.psB[:, fsl],
                                                   wu[par][:, c, fsl],
                                                   hid[:, c, tsl],
                                                   start=(c == 0),
                                                   stop=(c == NCH - 1))
                            mm.then_inc(s_pe.sem, 1)
                        s_pe.inc()
                        if tg == 3:
                            gup_done[(eg, fs)] = s_pe.v

                        @self.ae
                        def _(a):
                            a.wait_ge(s_pe.sem, s_pe.v)
                            a.activation(out=silu[:], in_=self.psA[:, 0:512],
                                         func=AF.Silu).then_inc(s_a.sem, 1)
                        s_a.inc()

                        @self.ve
                        def _(v, e=e, fs=fs, tg=tg):
                            v.wait_ge(s_a.sem, s_a.v)
                            v.tensor_tensor(out=self.tmp[:, 0:512],
                                            in0=self.psB[:], in1=silu[:],
                                            op=OP.mult)
                            v.drain()
                            dst = hbuf[:, 4 * fs:4 * fs + 4,
                                       128 * tg:128 * (tg + 1)]
                            if e < 8:
                                cw1 = cwb8[:, 128 * tg:128 * (tg + 1)]
                                for fc in range(4):
                                    v.tensor_tensor(
                                        out=dst[:, fc, :],
                                        in0=self.tmp[:, 128 * fc:128 * (fc + 1)],
                                        in1=cw1, op=OP.mult)
                                v.drain()
                            else:
                                v.tensor_copy(out=dst,
                                              in_=self.tmp[:, 0:512].rearrange(
                                                  "p (a b) -> p a b", a=4))
                                v.drain()
                            v.sem_inc(s_v.sem, 1)
                        s_v.inc()

                for co in range(NCH):
                    bank = 512 * (co % 2)

                    @self.te
                    def _(te, e=e, co=co, bank=bank):
                        te.wait_ge(s_v.sem, s_v.v)
                        kk = 0
                        for fs in range(4):
                            for fc in range(4):
                                mm = te.matmul(
                                    self.psW[:, bank:bank + 512],
                                    wd4[:, fs, fc, 128 * co:128 * (co + 1)],
                                    hbuf[:, 4 * fs + fc, :], start=(kk == 0),
                                    stop=(kk == 15))
                                kk += 1
                        mm.then_inc(s_pe.sem, 1)
                    s_pe.inc()

                    @self.ve
                    def _(v, q=q, co=co, bank=bank):
                        v.wait_ge(s_pe.sem, s_pe.v)
                        qsl = slice(512 * q, 512 * (q + 1))
                        v.tensor_tensor(out=x[:, co, qsl], in0=x[:, co, qsl],
                                        in1=self.psW[:, bank:bank + 512],
                                        op=OP.add).then_inc(s_v.sem, 1)
                    s_v.inc()
                dend[eg] = s_pe.v

        if "l0f" in self.taps:
            s_df = self.sem("s_dl0f")

            @self.sy
            def _(sy):
                sy.wait_ge(s_v.sem, s_v.v)
                sy.dma_start(out=dram["dbg_l0f"][:], in_=x[:]).then_inc(
                    s_df.sem, 16)
                sy.wait_ge(s_df.sem, 16)

    def l1_attn(self):
        x, dram = self.x, self.dram
        s_w, s_pe, s_v = self.s_w, self.s_pe, self.s_v
        self.tf16 = self.tmp[:].bitcast(F16)
        self.tbf = self.tmp[:].bitcast(BF16)
        self.qkf = self.av(32, [NCH, T], BF16)
        self.vf = self.av(64, [NCH, T], BF16)
        xbv = self.av(0, [NCH, T], BF16)
        self.obuf = self.av(0, [NCH, T], BF16)

        self.ln_rows(xb=self.av(32, [NCH, T], BF16),
                     xsq=self.av(64, [NCH, T], BF16))
        self.apply_affine(xbv, 0, 1)

        wbig = self.av(97, [NCH, D], BF16)
        for (wname, dst) in (("l1a_wqk", self.qkf), ("l1a_wv", self.vf)):
            @self.ge
            def _(g, wname=wname):
                g.wait_ge(s_pe.sem, s_pe.v)
                g.dma_start(out=wbig[:], in_=dram[wname][:]).then_inc(s_w.sem, 16)
            s_w.inc(16)

            for tg in range(4):
                sl = slice(512 * tg, 512 * (tg + 1))
                for op_ in range(0, NCH, 2):
                    @self.te
                    def _(te, sl=sl, op_=op_):
                        te.wait_ge(s_v.sem, s_v.v)
                        te.wait_ge(s_w.sem, s_w.v)
                        for c in range(NCH):
                            te.matmul(self.psA[:],
                                      wbig[:, c, 128 * op_:128 * op_ + 128],
                                      xbv[:, c, sl], start=(c == 0),
                                      stop=(c == NCH - 1))
                        for c in range(NCH):
                            mm = te.matmul(self.psB[:],
                                           wbig[:, c, 128 * op_ + 128:128 * op_ + 256],
                                           xbv[:, c, sl], start=(c == 0),
                                           stop=(c == NCH - 1))
                        mm.then_inc(s_pe.sem, 1)
                    s_pe.inc()

                    @self.ve
                    def _(v, sl=sl, op_=op_, dst=dst):
                        v.wait_ge(s_pe.sem, s_pe.v)
                        v.tensor_copy(out=dst[:, op_, sl], in_=self.psA[:])
                        v.tensor_copy(out=dst[:, op_ + 1, sl], in_=self.psB[:]).then_inc(s_v.sem, 1)
                    s_v.inc()

        @self.ve
        def _(v):
            v.wait_ge(s_v.sem, s_v.v)
            v.memset(self.av(0, [NCH * T], BF16), 0.0)
            v.drain()
            v.sem_inc(s_v.sem, 1)
        s_v.inc()

        self._l1a_local()
        self._l1a_lsh()
        if "rk" in self.taps:
            return

        if "obuf" in self.taps:
            s_do = self.sem("s_dob1")

            @self.sy
            def _(sy):
                sy.wait_ge(s_v.sem, s_v.v)
                sy.wait_ge(self.s_a.sem, self.s_a.v)
                sy.dma_start(out=dram["dbg_obuf"][:], in_=self.obuf[:]
                             ).then_inc(s_do.sem, 16)
                sy.wait_ge(s_do.sem, s_do.inc(16))

        self._l1a_tail()

    def _l1a_local(self):
        s_pe, s_v, s_a, s_g = self.s_pe, self.s_v, self.s_a, self.s_g
        qkf, vf, obuf, tbf = self.qkf, self.vf, self.obuf, self.tbf
        NEGB = -1.0e30
        SHV = 50000.0
        NROW = self.R(3)
        knl = self.av(109, [T], BF16)
        vtok = self.av(97, [16, 64], BF16)
        m_neq = self.tmp[:, 1280:1536]
        m_gtn = self.tmp[:, 1536:1792]
        m_padn = self.tmp[:, 1792:2048]
        rcol = self.rcol

        @self.ge
        def _(g):
            g.wait_ge(s_v.sem, s_v.v)
            g.iota(m_neq[:], pattern=[[1, 256]], base=0, channel_multiplier=-1,
                   allow_small_or_imprecise_dtypes=True)
            g.drain()
            g.iota(m_padn[:], pattern=[[1, 256]], base=0, channel_multiplier=0,
                   allow_small_or_imprecise_dtypes=True).then_inc(s_g.sem, 1)
        s_g.inc()

        @self.ve
        def _(v):
            v.wait_ge(s_g.sem, s_g.v)
            v.tensor_scalar(out=m_gtn[:], in0=m_neq[:], scalar1=128.0,
                            scalar2=NEGB, op0=OP.is_gt, op1=OP.mult)
            v.drain()
            v.tensor_scalar(out=m_neq[:], in0=m_neq[:], scalar1=128.0,
                            scalar2=None, op0=OP.not_equal, op1=OP.bypass)
            v.drain()
            v.tensor_scalar(out=m_padn[:], in0=m_padn[:], scalar1=127.5,
                            scalar2=None, op0=OP.is_gt, op1=OP.bypass)
            v.drain()
            v.tensor_scalar(out=m_padn[:], in0=m_padn[:], scalar1=-NEGB,
                            scalar2=NEGB, op0=OP.mult, op1=OP.add).then_inc(s_v.sem, 1)
        s_v.inc()

        for h in range(4):
            hb, ch = 64 * (h % 2), h // 2
            hsl = slice(hb, hb + 64)

            @self.ae
            def _(a, hsl=hsl, ch=ch):
                a.wait_ge(s_v.sem, s_v.v)
                a.wait_ge(s_pe.sem, s_pe.v)
                for tg in range(4):
                    sl = slice(512 * tg, 512 * (tg + 1))
                    a.activation(out=tbf[hsl, sl], in_=qkf[hsl, ch, sl],
                                 func=AF.Square)
                a.drain()
                a.sem_inc(s_a.sem, 1)
            s_a.inc()

            @self.te
            def _(te, hsl=hsl, hb=hb):
                te.wait_ge(s_a.sem, s_a.v)
                for tg in range(4):
                    sl = slice(512 * tg, 512 * (tg + 1))
                    mm = te.matmul(self.psW[0:1, sl], self.onesb[hsl, 0:1],
                                   tbf[hsl, sl], start=True, stop=True,
                                   tile_position=(hb, 0))
                mm.then_inc(s_pe.sem, 1)
            s_pe.inc()

            @self.ae
            def _(a):
                a.wait_ge(s_pe.sem, s_pe.v)
                a.activation(out=NROW, in_=self.psW[0:1, :], func=AF.Sqrt).then_inc(s_a.sem, 1)
            s_a.inc()

            @self.ve
            def _(v):
                v.wait_ge(s_a.sem, s_a.v)
                v.reciprocal(out=NROW, in_=NROW)
                v.drain()
                v.tensor_scalar(out=NROW, in0=NROW, scalar1=0.125, scalar2=None,
                                op0=OP.mult, op1=OP.bypass).then_inc(s_v.sem, 1)
            s_v.inc()

            for tg in range(4):
                sl = slice(512 * tg, 512 * (tg + 1))

                @self.te
                def _(te, sl=sl, hb=hb):
                    te.wait_ge(s_v.sem, s_v.v)
                    te.matmul(self.psO[hb:hb + 64, :], self.ones1[0:1, 0:64],
                              NROW[:, sl], start=True, stop=True,
                              tile_position=(0, hb)).then_inc(s_pe.sem, 1)
                s_pe.inc()

                @self.ve
                def _(v, sl=sl, hsl=hsl, ch=ch):
                    v.wait_ge(s_pe.sem, s_pe.v)
                    v.tensor_tensor(out=knl[hsl, sl], in0=qkf[hsl, ch, sl],
                                    in1=self.psO[hsl, :], op=OP.mult).then_inc(s_v.sem, 1)
                s_v.inc()

            for tt in range(16):
                @self.te
                def _(te, tt=tt, hsl=hsl, ch=ch, hb=hb):
                    te.wait_ge(s_v.sem, s_v.v)
                    te.wait_ge(s_a.sem, s_a.v)
                    te.transpose(self.psT[:, 0:64],
                                 vf[hsl, ch, 128 * tt:128 * (tt + 1)],
                                 self.identb[hsl, hsl],
                                 tile_position=(hb, 0)).then_inc(s_pe.sem, 1)
                s_pe.inc()

                @self.ae
                def _(a, tt=tt):
                    a.wait_ge(s_pe.sem, s_pe.v)
                    a.copy(out=vtok[:, tt, :], in_=self.psT[:, 0:64]).then_inc(s_a.sem, 1)
                s_a.inc()

            for w in range(16):
                wsl = slice(128 * w, 128 * (w + 1))
                ksl = slice(max(0, 128 * (w - 1)), 128 * (w + 1))
                nk = 256 if w > 0 else 128
                msl = slice(0, 256) if w > 0 else slice(128, 256)

                @self.te
                def _(te, wsl=wsl, ksl=ksl, nk=nk, hsl=hsl, ch=ch, hb=hb):
                    te.wait_ge(s_v.sem, s_v.v)
                    te.matmul(self.psA[:, 0:nk], qkf[hsl, ch, wsl],
                              knl[hsl, ksl], start=True, stop=True,
                              tile_position=(hb, 0)).then_inc(s_pe.sem, 1)
                s_pe.inc()

                @self.ve
                def _(v, w=w, nk=nk, msl=msl):
                    v.wait_ge(s_pe.sem, s_pe.v)
                    d = self.tmp[:, 0:nk]
                    v.tensor_tensor(out=d, in0=self.psA[:, 0:nk],
                                    in1=m_neq[:, msl], op=OP.mult)
                    v.drain()
                    v.scalar_tensor_tensor(out=d, in0=m_neq[:, msl], scalar=SHV,
                                           in1=d, op0=OP.mult, op1=OP.add)
                    v.drain()
                    v.tensor_tensor(out=d, in0=d, in1=m_gtn[:, msl], op=OP.add)
                    v.drain()
                    if w == 0:
                        v.tensor_tensor(out=d, in0=d, in1=m_padn[:, msl],
                                        op=OP.add)
                        v.drain()
                    v.reduce_max(out=rcol[:, 0:1], in_=d, axis=AX.X, negate=True).then_inc(s_v.sem, 1)
                s_v.inc()

                @self.ae
                def _(a, nk=nk):
                    a.wait_ge(s_v.sem, s_v.v)
                    a.activation(out=tbf[:, 512:512 + nk], in_=self.tmp[:, 0:nk],
                                 func=AF.Exp, bias=rcol[:, 0:1],
                                 accum_out=rcol[:, 1:2]).then_inc(s_a.sem, 1)
                s_a.inc()

                nh = 2 if w > 0 else 1
                for half in range(nh):
                    @self.te
                    def _(te, half=half, nh=nh):
                        te.wait_ge(s_a.sem, s_a.v)
                        te.transpose(self.psT[:],
                                     tbf[:, 512 + 128 * half:640 + 128 * half],
                                     self.identb[:]).then_inc(s_pe.sem, 1)
                    s_pe.inc()

                    @self.ae
                    def _(a):
                        a.wait_ge(s_pe.sem, s_pe.v)
                        a.copy(out=tbf[:, 768:896], in_=self.psT[:]).then_inc(s_a.sem, 1)
                    s_a.inc()

                    @self.te
                    def _(te, w=w, half=half, nh=nh):
                        te.wait_ge(s_a.sem, s_a.v)
                        vt = w - 1 + half if nh == 2 else w
                        te.matmul(self.psO[:, 0:64], tbf[:, 768:896],
                                  vtok[:, vt, :], start=(half == 0),
                                  stop=(half == nh - 1)).then_inc(s_pe.sem, 1)
                    s_pe.inc()

                @self.ve
                def _(v):
                    v.wait_ge(s_pe.sem, s_pe.v)
                    v.reciprocal(out=rcol[:, 2:3], in_=rcol[:, 1:2])
                    v.drain()
                    v.tensor_scalar(out=self.tmp[:, 256:320],
                                    in0=self.psO[:, 0:64],
                                    scalar1=rcol[:, 2:3], scalar2=None,
                                    op0=OP.mult, op1=OP.bypass).then_inc(s_v.sem, 1)
                s_v.inc()

                @self.te
                def _(te):
                    te.wait_ge(s_v.sem, s_v.v)
                    te.transpose(self.psB[0:64, 0:128],
                                 self.tmp[:, 256:320],
                                 self.ident[:]).then_inc(s_pe.sem, 1)
                s_pe.inc()

                @self.ae
                def _(a, hsl=hsl, ch=ch, wsl=wsl):
                    a.wait_ge(s_pe.sem, s_pe.v)
                    a.copy(out=obuf[hsl, ch, wsl], in_=self.psB[0:64, 0:128]).then_inc(s_a.sem, 1)
                s_a.inc()


    def _l1a_lsh(self):
        s_pe, s_v, s_a, s_w = self.s_pe, self.s_v, self.s_a, self.s_w
        qkf, vf, obuf, tf16, tbf = self.qkf, self.vf, self.obuf, self.tf16, self.tbf
        dram, rows = self.dram, self.rows
        NEGB = -1.0e30
        SHV = 50000.0
        rot2 = self.av(96, [128], BF16)
        SA = self.av(97, [T], F16)
        SB = self.av(101, [T], F16)
        MG = self.av(105, [T], F16)
        SCR = self.av(109, [T], F16)
        rotf = self.av(97, [T])
        rkc = self.av(113.25, [16, 4], U32)
        rk2 = self.av(113.5, [16, 4], U32)
        knf = self.av(97, [8320], F16)
        rowstgs = [tf16[:, 768:1024], tf16[:, 1024:1280]]
        winb = [tf16[:, 3328:3584], tf16[:, 3584:3840]]
        btokv = self.tmp[:, 1664:1728].rearrange("p (a b) -> p a b", a=16)
        tiev = self.tmp[:, 1728:1792].rearrange("p (a b) -> p a b", a=16)
        catv = self.tmp[:, 128:384].rearrange("p (a b) -> p a b", a=4)
        BROW = self.R(3)
        TOT = rows[0:1, T:T + 160]
        RWA = rows[0:1, T + 256:T + 384]
        RWB = rows[0:1, T + 384:T + 512]
        SCN = rows[0:1, T + 512:T + 672]
        STA = rows[0:1, T + 672:T + 833]
        STROW = rows[0:1, T:T + 128]
        sostg = self.tmp[64:128, 512:1664]
        ugb = self.tmp[:, 1664:1952].rearrange("p (r f) -> p r f", r=4)
        rcol = self.rcol
        s_sc = [self.sem("s_sca"), self.sem("s_scb")]
        s_so = self.sem("s_so")
        s_ug = self.sem("s_ug")
        s_kn = self.sem("s_kn")
        s_wd = [self.sem("s_wd0"), self.sem("s_wd1")]

        @self.ge
        def _(g):
            g.wait_ge(s_v.sem, s_v.v)
            g.dma_start(out=rot2[:], in_=dram["l1a_rot"][:]).then_inc(s_w.sem, 16)
        s_w.inc(16)

        BAND = [slice(0, 64), slice(32, 96), slice(64, 128), slice(96, 128)]
        CSTRIP = [SA, SB, SA, SB]

        def rsplit(sl):
            a, b = sl.start, sl.stop
            if a < 64 < b:
                return [slice(a, 64), slice(64, b)]
            return [sl]

        def emit_brow(r):
            for tp in range(8):
                @self.te
                def _(te, r=r, tp=tp):
                    te.wait_ge(s_v.sem, s_v.v)
                    te.transpose(self.psA[0:1, 0:128], btokv[:, 2 * tp, r:r + 1],
                                 self.ident[:])
                    te.transpose(self.psB[0:1, 0:128],
                                 btokv[:, 2 * tp + 1, r:r + 1],
                                 self.ident[:]).then_inc(s_pe.sem, 1)
                s_pe.inc()

                @self.ve
                def _(v, r=r, tp=tp):
                    v.wait_ge(s_pe.sem, s_pe.v)
                    v.tensor_scalar(out=BROW[:, 256 * tp:256 * tp + 128],
                                    in0=self.psA[0:1, 0:128],
                                    scalar1=float(32 * r), scalar2=None,
                                    op0=OP.add, op1=OP.bypass)
                    v.tensor_scalar(out=BROW[:, 256 * tp + 128:256 * tp + 256],
                                    in0=self.psB[0:1, 0:128],
                                    scalar1=float(32 * r), scalar2=None,
                                    op0=OP.add, op1=OP.bypass).then_inc(s_v.sem, 1)
                s_v.inc()

        def emit_table(r, strip, use_psw):
            banks = ([self.psW[:, 0:512], self.psW[:, 512:1024],
                      self.psW[:, 1024:1536], self.psW[:, 1536:2048]]
                     if use_psw else
                     [self.psO[:, 0:512], self.psA[:, 0:512],
                      self.psB[:, 0:512], None])

            def emit_group(g4s):
                @self.te
                def _(te, g4s=g4s):
                    te.wait_ge(s_v.sem, s_v.v)
                    for g4 in g4s:
                        mm = te.matmul(banks[g4 % 4] if banks[g4 % 4] is not None
                                       else self.psO[:, 0:512],
                                       self.ones1[0:1, :],
                                       BROW[:, 512 * g4:512 * (g4 + 1)],
                                       start=True, stop=True)
                    mm.then_inc(s_pe.sem, 1)
                s_pe.inc()

                @self.ve
                def _(v, g4s=g4s, r=r, strip=strip):
                    v.wait_ge(s_pe.sem, s_pe.v)
                    for g4 in g4s:
                        ps = (banks[g4 % 4] if banks[g4 % 4] is not None
                              else self.psO[:, 0:512])
                        sl = slice(512 * g4, 512 * (g4 + 1))
                        for bd in rsplit(BAND[r]):
                            v.tensor_scalar(out=strip[bd, sl], in0=ps[bd, :],
                                            scalar1=self.iotap[bd, 0:1],
                                            scalar2=0.0, op0=OP.subtract,
                                            op1=OP.is_equal)
                        if r == 3:
                            v.tensor_scalar(out=strip[0:32, sl],
                                            in0=ps[0:32, :],
                                            scalar1=self.iotap128[0:32, 0:1],
                                            scalar2=0.0, op0=OP.subtract,
                                            op1=OP.is_equal)
                    v.drain()
                    v.sem_inc(s_v.sem, 1)
                s_v.inc()

            if use_psw:
                emit_group([0, 1, 2, 3])
            else:
                emit_group([0, 1, 2])
                emit_group([3])

        for h in range(4, 16):
            hs, hb, ch = h - 4, 64 * (h % 2), h // 2
            hsl = slice(hb, hb + 64)

            @self.te
            def _(te, hb=hb, ch=ch, hsl=hsl):
                te.wait_ge(s_v.sem, s_v.v)
                te.wait_ge(s_w.sem, s_w.v)
                for g4 in range(4):
                    sl = slice(512 * g4, 512 * (g4 + 1))
                    mm = te.matmul(self.psW[:, sl], rot2[hsl, :],
                                   qkf[hsl, ch, sl], start=True, stop=True,
                                   tile_position=(hb, 0))
                mm.then_inc(s_pe.sem, 1)
            s_pe.inc()

            @self.ve
            def _(v):
                v.wait_ge(s_pe.sem, s_pe.v)
                v.tensor_copy(out=rotf[:], in_=self.psW[:]).then_inc(s_v.sem, 1)
            s_v.inc()

            for tt in range(16):
                @self.te
                def _(te, tt=tt):
                    te.wait_ge(s_v.sem, s_v.v)
                    te.wait_ge(s_a.sem, s_a.v)
                    te.transpose(self.psO[:, 0:128],
                                 rotf[:, 128 * tt:128 * (tt + 1)],
                                 self.ident[:]).then_inc(s_pe.sem, 1)
                s_pe.inc()

                @self.ae
                def _(a):
                    a.wait_ge(s_pe.sem, s_pe.v)
                    a.copy(out=self.tmp[:, 0:128], in_=self.psO[:, 0:128]).then_inc(s_a.sem, 1)
                s_a.inc()

                @self.ve
                def _(v, tt=tt):
                    v.wait_ge(s_a.sem, s_a.v)
                    src = self.tmp[:, 0:128].rearrange("p (a b) -> p a b", a=4)
                    v.tensor_copy(out=catv[:, :, 0:32], in_=src)
                    v.drain()
                    v.tensor_scalar(out=catv[:, :, 32:64], in0=src, scalar1=-1.0,
                                    scalar2=None, op0=OP.mult, op1=OP.bypass)
                    v.drain()
                    mx8 = self.tmp[:, 384:416].rearrange("p (a b) -> p a b",
                                                         a=4)
                    ix8 = self.tmp[:, 416:448].rearrange("p (a b) -> p a b",
                                                         a=4)
                    for r in range(4):
                        v.max(out=mx8[:, r, :], in_=catv[:, r, :])
                    v.drain()
                    for r in range(4):
                        v.max_index(out=ix8[:, r, :].bitcast(U32),
                                    in_max=mx8[:, r, :],
                                    in_values=catv[:, r, :])
                    v.drain()
                    v.tensor_copy(out=btokv[:, tt, :],
                                  in_=ix8[:, :, 0:1].bitcast(U32).rearrange(
                                      "p a b -> p (a b)")).then_inc(s_v.sem, 1)
                s_v.inc()

            @self.ve
            def _(v):
                v.wait_ge(s_v.sem, s_v.v)
                v.memset(self.tmp[:, 1728:1792], 0.0)
                v.drain()
                for d in (1, 2, 3):
                    nw = 4 - d
                    q = self.tmp[:, 560:560 + 16 * nw].rearrange(
                        "p (a b) -> p a b", a=16)
                    v.tensor_tensor(out=q, in0=btokv[:, :, 0:nw],
                                    in1=btokv[:, :, d:4], op=OP.subtract)
                    v.drain()
                    v.tensor_scalar(out=q, in0=q, scalar1=float(32 * d),
                                    scalar2=None, op0=OP.is_equal, op1=OP.bypass)
                    v.drain()
                    v.tensor_tensor(out=tiev[:, :, d:4], in0=tiev[:, :, d:4],
                                    in1=q, op=OP.add)
                    v.drain()
                v.sem_inc(s_v.sem, 1)
            s_v.inc()

            for r in range(4):
                emit_brow(r)
                emit_table(r, CSTRIP[r], True)

            @self.ve
            def _(v):
                v.wait_ge(s_v.sem, s_v.v)
                v.tensor_tensor_scan(out=SA[:],
                                     data0=self.cb[:, 1:2].to_broadcast([128, T]),
                                     data1=SA[:], initial=0.0, op0=OP.mult,
                                     op1=OP.add)
                v.drain()
                v.tensor_tensor_scan(out=SB[:],
                                     data0=self.cb[:, 1:2].to_broadcast([128, T]),
                                     data1=SB[:], initial=0.0, op0=OP.mult,
                                     op1=OP.add)
                v.drain()
                v.tensor_copy(out=self.tmp[:, 544:545], in_=SA[:, T - 1:T])
                v.tensor_copy(out=self.tmp[:, 545:546], in_=SB[:, T - 1:T]).then_inc(s_v.sem, 1)
            s_v.inc()

            @self.te
            def _(te):
                te.wait_ge(s_v.sem, s_v.v)
                te.transpose(self.psA[0:1, 0:128], self.tmp[:, 544:545],
                             self.ident[:])
                te.transpose(self.psB[0:1, 0:128], self.tmp[:, 545:546],
                             self.ident[:]).then_inc(s_pe.sem, 1)
            s_pe.inc()

            @self.ve
            def _(v):
                v.wait_ge(s_pe.sem, s_pe.v)
                v.tensor_copy(out=RWA[:], in_=self.psA[0:1, 0:128])
                v.tensor_copy(out=RWB[:], in_=self.psB[0:1, 0:128])
                v.drain()
                v.tensor_copy(out=TOT[:, 0:128], in_=RWA[:])
                v.tensor_copy(out=rows[0:1, T + 128:T + 160], in_=RWB[:, 0:32])
                v.drain()
                v.tensor_tensor(out=TOT[:, 32:128], in0=TOT[:, 32:128],
                                in1=RWB[:, 32:128], op=OP.add)
                v.drain()
                v.tensor_tensor_scan(out=SCN[:],
                                     data0=self.cb[0:1, 1:2].to_broadcast([1, 160]),
                                     data1=TOT[:], initial=0.0, op0=OP.mult,
                                     op1=OP.add)
                v.drain()
                v.memset(STA[:, 0:1], 0.0)
                v.drain()
                v.tensor_copy(out=STA[:, 1:161], in_=SCN[:, 0:160]).then_inc(s_v.sem, 1)
            s_v.inc()

            @self.te
            def _(te):
                te.wait_ge(s_v.sem, s_v.v)
                te.transpose(self.psA[:, 0:1], rows[0:1, T + 672:T + 800],
                             self.ident[0:1, 0:1])
                te.transpose(self.psB[0:32, 0:1], rows[0:1, T + 800:T + 832],
                             self.ident[0:1, 0:1]).then_inc(s_pe.sem, 1)
            s_pe.inc()

            @self.ve
            def _(v):
                v.wait_ge(s_pe.sem, s_pe.v)
                tm = self.tmp
                v.tensor_copy(out=tm[:, 528:529], in_=self.psA[:, 0:1])
                v.tensor_copy(out=tm[0:32, 534:535], in_=self.psB[0:32, 0:1])
                v.drain()
                for (src, hic, loc, rsl) in ((528, 532, 533, slice(0, 128)),
                                             (534, 535, 536, slice(0, 32))):
                    v.tensor_scalar(out=tm[rsl, 529:530], in0=tm[rsl, src:src + 1],
                                    scalar1=2048.0, scalar2=None, op0=OP.is_ge,
                                    op1=OP.bypass)
                    v.tensor_scalar(out=tm[rsl, 530:531], in0=tm[rsl, src:src + 1],
                                    scalar1=4096.0, scalar2=None, op0=OP.is_ge,
                                    op1=OP.bypass)
                    v.tensor_scalar(out=tm[rsl, 531:532], in0=tm[rsl, src:src + 1],
                                    scalar1=6144.0, scalar2=None, op0=OP.is_ge,
                                    op1=OP.bypass)
                    v.drain()
                    v.tensor_tensor(out=tm[rsl, 529:530], in0=tm[rsl, 529:530],
                                    in1=tm[rsl, 530:531], op=OP.add)
                    v.drain()
                    v.tensor_tensor(out=tm[rsl, 529:530], in0=tm[rsl, 529:530],
                                    in1=tm[rsl, 531:532], op=OP.add)
                    v.drain()
                    v.tensor_scalar(out=tm[rsl, hic:hic + 1],
                                    in0=tm[rsl, 529:530], scalar1=2048.0,
                                    scalar2=None, op0=OP.mult, op1=OP.bypass)
                    v.drain()
                    v.tensor_tensor(out=tm[rsl, loc:loc + 1],
                                    in0=tm[rsl, src:src + 1],
                                    in1=tm[rsl, hic:hic + 1], op=OP.subtract)
                    v.drain()
                v.sem_inc(s_v.sem, 1)
            s_v.inc()

            @self.ve
            def _(v):
                v.wait_ge(s_v.sem, s_v.v)
                v.memset(SCR[:], 0.0).then_inc(s_v.sem, 1)
            s_v.inc()

            NPROD = [4, 5, 5, 7]
            self._prev_zero = None
            pidx = 0
            ptot = sum(NPROD)
            for r in range(4):
                emit_brow(r)

                @self.ve
                def _(v):
                    v.wait_ge(s_v.sem, s_v.v)
                    v.memset(MG[:], 0.0).then_inc(s_v.sem, 1)
                s_v.inc()

                emit_table(r, MG, False)

                tm = self.tmp
                bd = BAND[r]

                def sgprod(v, rsl, col):
                    for q in rsplit(rsl):
                        v.tensor_scalar(out=SCR[q, :], in0=MG[q, :],
                                        scalar1=tm[q, col:col + 1], scalar2=None,
                                        op0=OP.mult, op1=OP.bypass)

                def cprod(v, rsl, strip):
                    for q in rsplit(rsl):
                        v.tensor_tensor(out=SCR[q, 1:T], in0=MG[q, 1:T],
                                        in1=strip[q, 0:T - 1], op=OP.mult)

                prods = [(lambda v, rsl=bd: sgprod(v, rsl, 532), bd),
                         (lambda v, rsl=bd: sgprod(v, rsl, 533), bd)]
                if r == 3:
                    prods += [(lambda v: sgprod(v, slice(0, 32), 535),
                               slice(0, 32)),
                              (lambda v: sgprod(v, slice(0, 32), 536),
                               slice(0, 32)),
                              (lambda v: cprod(v, slice(96, 128), SB),
                               slice(96, 128)),
                              (lambda v: cprod(v, slice(0, 32), SB),
                               slice(0, 32)),
                              (lambda v: cprod(v, slice(96, 128), SA),
                               slice(96, 128))]
                else:
                    prods += [(lambda v, rsl=bd, st=CSTRIP[r]: cprod(v, rsl, st),
                               bd)]
                    if r >= 1:
                        rsl = slice(32 * r, 32 * r + 32)
                        prods += [(lambda v, rsl=rsl, st=CSTRIP[r - 1]:
                                   cprod(v, rsl, st), rsl)]
                    if r <= 2:
                        rsl = slice(32 * r + 32, 32 * r + 64)
                        prods += [(lambda v, rsl=rsl, st=CSTRIP[r + 1]:
                                   cprod(v, rsl, st), rsl)]
                assert len(prods) == NPROD[r]

                for (wfn, rsl) in prods:
                    @self.ve
                    def _(v, wfn=wfn, pz=self._prev_zero):
                        v.wait_ge(s_v.sem, s_v.v)
                        v.wait_ge(s_pe.sem, s_pe.v)
                        if pz is not None:
                            for q in rsplit(pz):
                                v.memset(SCR[q, :], 0.0)
                            v.drain()
                        wfn(v)
                        v.drain()
                        v.sem_inc(s_v.sem, 1)
                    s_v.inc()
                    self._prev_zero = rsl

                    @self.te
                    def _(te, r=r, pidx=pidx, ptot=ptot):
                        te.wait_ge(s_v.sem, s_v.v)
                        for g4 in range(4):
                            mm = te.matmul(
                                self.psW[64:68, 512 * g4:512 * (g4 + 1)],
                                self.e16[:, 4 * r:4 * r + 4],
                                SCR[:, 512 * g4:512 * (g4 + 1)],
                                start=(pidx == 0), stop=(pidx == ptot - 1),
                                skip_group_check=True, tile_position=(0, 64))
                        mm.then_inc(s_pe.sem, 1)
                    s_pe.inc()
                    pidx += 1

            for tt in range(16):
                @self.ve
                def _(v, tt=tt):
                    v.wait_ge(s_pe.sem, s_pe.v)
                    v.tensor_copy(out=self.tmp[64:68, 0:128],
                                  in_=self.psW[64:68, 128 * tt:128 * (tt + 1)]).then_inc(s_v.sem, 1)
                s_v.inc()

                @self.te
                def _(te):
                    te.wait_ge(s_v.sem, s_v.v)
                    te.transpose(self.psO[:, 0:4], self.tmp[64:68, 0:128],
                                 self.ident[64:68, 64:68],
                                 tile_position=(64, 0)).then_inc(s_pe.sem, 1)
                s_pe.inc()

                @self.ve
                def _(v, tt=tt):
                    v.wait_ge(s_pe.sem, s_pe.v)
                    tm = self.tmp
                    c = tm[:, 128:132]
                    v.tensor_tensor(out=c, in0=self.psO[:, 0:4],
                                    in1=tiev[:, tt, :], op=OP.add)
                    v.drain()
                    v.tensor_scalar(out=c, in0=c, scalar1=64.0, scalar2=None,
                                    op0=OP.add, op1=OP.bypass)
                    v.drain()
                    v.tensor_copy(out=rkc[:, tt, :], in_=c)
                    v.drain()
                    m = tm[:, 132:136]
                    v.tensor_scalar(out=m, in0=c, scalar1=8192.0, scalar2=None,
                                    op0=OP.is_ge, op1=OP.bypass)
                    v.drain()
                    t1 = tm[:, 136:140]
                    v.tensor_scalar(out=t1, in0=c, scalar1=-8192.0, scalar2=None,
                                    op0=OP.add, op1=OP.bypass)
                    v.drain()
                    v.tensor_tensor(out=t1, in0=t1, in1=m, op=OP.mult)
                    v.drain()
                    t2 = tm[:, 140:144]
                    v.tensor_scalar(out=t2, in0=m, scalar1=-1.0, scalar2=1.0,
                                    op0=OP.mult, op1=OP.add)
                    v.drain()
                    v.tensor_tensor(out=t2, in0=t2, in1=self.dmp4[:],
                                    op=OP.mult)
                    v.drain()
                    v.tensor_tensor(out=t1, in0=t1, in1=t2, op=OP.add)
                    v.drain()
                    v.tensor_copy(out=rk2[:, tt, :], in_=t1).then_inc(s_v.sem, 1)
                s_v.inc()

            if "rk" in self.taps and h == 4:
                s_dbg = self.sem("s_dbgrk")

                @self.sy
                def _(sy):
                    sy.wait_ge(s_v.sem, s_v.v)
                    sy.dma_start(out=dram["dbg_rkc"][:], in_=rkc[:]
                                 ).then_inc(s_dbg.sem, 16)
                    sy.dma_start(out=dram["dbg_bt"][:], in_=self.tmp[:, 1664:1728]
                                 ).then_inc(s_dbg.sem, 16)
                    sy.dma_start(out=dram["dbg_tie"][:], in_=self.tmp[:, 1728:1792]
                                 ).then_inc(s_dbg.sem, 16)
                    sy.dma_start(out=dram["dbg_rows"][:], in_=rows[0:1, :]
                                 ).then_inc(s_dbg.sem, 16)
                    sy.dma_start(out=dram["dbg_sa"][:], in_=SA[:]
                                 ).then_inc(s_dbg.sem, 16)
                    sy.dma_start(out=dram["dbg_sb"][:], in_=SB[:]
                                 ).then_inc(s_dbg.sem, 16)
                    sy.wait_ge(s_dbg.sem, s_dbg.inc(96))
                return

            @self.ve
            def _(v):
                v.wait_ge(s_v.sem, s_v.v)
                v.memset(rowstgs[0][:, 193:256], 0.0)
                v.memset(rowstgs[1][:, 193:256], 0.0).then_inc(s_v.sem, 1)
            s_v.inc()

            sc_marks = {}
            for tt in range(16):
                rowstg = rowstgs[tt % 2]

                @self.te
                def _(te, tt=tt, hsl=hsl, ch=ch, hb=hb,
                      sc0=s_sc[tt % 2].v):
                    te.wait_ge(s_v.sem, s_v.v)
                    te.wait_ge(s_a.sem, s_a.v)
                    te.wait_ge(s_sc[tt % 2].sem, sc_marks.get(tt - 2, sc0))
                    te.transpose(self.psT[:, 0:64],
                                 qkf[hsl, ch, 128 * tt:128 * (tt + 1)],
                                 self.identb[hsl, hsl],
                                 tile_position=(hb, 0)).then_inc(s_pe.sem, 1)
                s_pe.inc()

                @self.ae
                def _(a, rowstg=rowstg):
                    a.wait_ge(s_pe.sem, s_pe.v)
                    a.copy(out=rowstg[:, 0:64], in_=self.psT[:, 0:64])
                    a.drain()
                    a.activation(out=self.tmp[:, 256:320], in_=rowstg[:, 0:64],
                                 func=AF.Square, accum_out=self.tmp[:, 320:321])
                    a.drain()
                    a.activation(out=self.tmp[:, 321:322],
                                 in_=self.tmp[:, 320:321], func=AF.Sqrt).then_inc(s_a.sem, 1)
                s_a.inc()

                @self.te
                def _(te, tt=tt, hsl=hsl, ch=ch, hb=hb):
                    te.wait_ge(s_a.sem, s_a.v)
                    te.transpose(self.psT[:, 0:64],
                                 vf[hsl, ch, 128 * tt:128 * (tt + 1)],
                                 self.identb[hsl, hsl],
                                 tile_position=(hb, 0)).then_inc(s_pe.sem, 1)
                s_pe.inc()

                @self.ae
                def _(a, rowstg=rowstg):
                    a.wait_ge(s_pe.sem, s_pe.v)
                    a.copy(out=rowstg[:, 128:192], in_=self.psT[:, 0:64]).then_inc(s_a.sem, 1)
                s_a.inc()

                @self.ve
                def _(v, tt=tt, rowstg=rowstg):
                    v.wait_ge(s_a.sem, s_a.v)
                    v.reciprocal(out=self.tmp[:, 322:323], in_=self.tmp[:, 321:322])
                    v.drain()
                    v.tensor_scalar(out=self.tmp[:, 322:323],
                                    in0=self.tmp[:, 322:323], scalar1=0.125,
                                    scalar2=None, op0=OP.mult, op1=OP.bypass)
                    v.drain()
                    v.tensor_scalar(out=rowstg[:, 64:128], in0=rowstg[:, 0:64],
                                    scalar1=self.tmp[:, 322:323], scalar2=None,
                                    op0=OP.mult, op1=OP.bypass)
                    v.drain()
                    v.tensor_scalar(out=rowstg[:, 192:193], in0=self.iotap[:],
                                    scalar1=float(128 * tt), scalar2=None,
                                    op0=OP.add, op1=OP.bypass).then_inc(s_v.sem, 1)
                s_v.inc()

                @self.ge
                def _(g, hs=hs, tt=tt, rowstg=rowstg,
                      scp=s_sc[1 - (tt % 2)].v):
                    g.wait_ge(s_v.sem, s_v.v)
                    g.wait_ge(s_sc[1 - (tt % 2)].sem, scp)
                    for r in range(4):
                        g.indirect_dma_start(
                            out=dram["s_sort%d" % hs],
                            out_offset=bass.IndirectOffsetOnAxis(
                                ap=rkc[:, tt, r:r + 1], axis=0),
                            in_=rowstg[:], in_offset=None
                            ).then_inc(s_sc[tt % 2].sem, 16)
                    for r in range(4):
                        g.indirect_dma_start(
                            out=dram["s_sort%d" % hs],
                            out_offset=bass.IndirectOffsetOnAxis(
                                ap=rk2[:, tt, r:r + 1], axis=0),
                            in_=rowstg[:], in_offset=None
                            ).then_inc(s_sc[tt % 2].sem, 16)
                s_sc[tt % 2].inc(128)
                sc_marks[tt] = s_sc[tt % 2].v

            @self.sy
            def _(sy, hs=hs):
                sy.wait_ge(s_sc[0].sem, s_sc[0].v)
                sy.wait_ge(s_sc[1].sem, s_sc[1].v)
                sy.dma_start(out=knf[:], in_=dram["s_sort%d" % hs][0:8320, 64:192],
                             transpose=True).then_inc(s_kn.sem, 16)
            s_kn.inc(16)
            kn_ld = s_kn.v

            @self.ve
            def _(v, sc0=s_sc[0].v, sc1=s_sc[1].v):
                v.wait_ge(s_v.sem, s_v.v)
                v.wait_ge(s_sc[0].sem, sc0)
                v.wait_ge(s_sc[1].sem, sc1)
                v.memset(self.tmp[64:128, 512:1664], 0.0)
                v.memset(self.tmp[0:64, 1952:2024], 0.0).then_inc(s_v.sem, 1)
            s_v.inc()

            @self.sy
            def _(sy, hs=hs):
                sy.wait_ge(s_v.sem, s_v.v)
                sy.dma_start(out=dram["s_so%d" % hs][0:64],
                             in_=self.tmp[0:64, 1952:2024]).then_inc(s_so.sem, 16)
                sy.dma_start(out=dram["s_so%d" % hs][8256:8320],
                             in_=self.tmp[0:64, 1952:2024]).then_inc(s_so.sem, 16)
            s_so.inc(32)

            win_used = {}
            wd_marks = {}
            dots_ps = [self.psA, self.psW[:, 0:512]]
            stj_ps = [self.psB, self.psW[:, 512:1024]]
            bqslot = [slice(768, 832), slice(832, 896)]
            stcol = [480, 481]

            def emit_load(k, sv0):
                kp = k % 2

                @self.sy
                def _(sy, hs=hs, k=k, kp=kp, sv0=sv0):
                    if k >= 2:
                        sy.wait_ge(s_pe.sem, win_used[k - 2])
                    else:
                        sy.wait_ge(s_v.sem, sv0)
                    sy.dma_start(out=winb[kp][:],
                                 in_=dram["s_sort%d" % hs][64 * k:64 * k + 128, :]
                                 ).then_inc(s_wd[kp].sem, 16)
                s_wd[kp].inc(16)
                wd_marks[k] = s_wd[kp].v

            def emit_front(k, kn_ld):
                kp = k % 2

                @self.ae
                def _(a, k=k, kp=kp, kn_ld=kn_ld):
                    a.wait_ge(s_wd[kp].sem, wd_marks[k])
                    if k == 0:
                        a.wait_ge(s_kn.sem, kn_ld)
                    wb = winb[kp]
                    a.copy(out=self.tmp[:, stcol[kp]:stcol[kp] + 1],
                           in_=wb[:, 192:193])
                    a.copy(out=tbf[64:128, bqslot[kp]], in_=wb[64:128, 0:64]
                           ).then_inc(s_a.sem, 1)
                s_a.inc()

                @self.te
                def _(te, kp=kp):
                    te.wait_ge(s_a.sem, s_a.v)
                    te.wait_ge(s_v.sem, s_v.v)
                    te.transpose(self.psO[0:1, 0:128],
                                 self.tmp[:, stcol[kp]:stcol[kp] + 1],
                                 self.ident[:]).then_inc(s_pe.sem, 1)
                s_pe.inc()

                @self.ve
                def _(v):
                    v.wait_ge(s_pe.sem, s_pe.v)
                    v.tensor_copy(out=STROW[:], in_=self.psO[0:1, 0:128]
                                  ).then_inc(s_v.sem, 1)
                s_v.inc()

                @self.te
                def _(te, kp=kp):
                    te.wait_ge(s_a.sem, s_a.v)
                    te.wait_ge(s_v.sem, s_v.v)
                    te.transpose(self.psT[0:64, 0:64], tbf[64:128, bqslot[kp]],
                                 self.identb[64:128, 64:128],
                                 tile_position=(64, 0)).then_inc(s_pe.sem, 1)
                s_pe.inc()

                @self.ae
                def _(a, kp=kp):
                    a.wait_ge(s_pe.sem, s_pe.v)
                    a.copy(out=tbf[0:64, bqslot[kp]], in_=self.psT[0:64, 0:64]
                           ).then_inc(s_a.sem, 1)
                s_a.inc()

                @self.te
                def _(te, k=k, kp=kp):
                    te.wait_ge(s_a.sem, s_a.v)
                    te.wait_ge(s_v.sem, s_v.v)
                    te.matmul(stj_ps[kp][64:128, 0:128], self.ones1[0:1, 0:64],
                              STROW[:], start=True, stop=True,
                              tile_position=(0, 64))
                    te.matmul(dots_ps[kp][64:128, 0:128], tbf[0:64, bqslot[kp]],
                              knf[0:64, 64 * k:64 * k + 128], start=True,
                              stop=True, tile_position=(0, 64)
                              ).then_inc(s_pe.sem, 1)
                s_pe.inc()

            sv0 = s_v.v
            emit_load(0, sv0)
            emit_load(1, sv0)
            emit_front(0, kn_ld)

            for c in range(128):
                par = c % 2

                @self.ve
                def _(v, par=par):
                    v.wait_ge(s_pe.sem, s_pe.v)
                    tm = self.tmp
                    sti = tm[64:128, stcol[par]:stcol[par] + 1]
                    stj = stj_ps[par][64:128, 0:128]
                    v.tensor_scalar(out=tm[64:128, 0:128], in0=stj, scalar1=sti,
                                    scalar2=0.0, op0=OP.subtract,
                                    op1=OP.not_equal)
                    v.tensor_scalar(out=tm[64:128, 128:256], in0=stj,
                                    scalar1=sti, scalar2=0.0, op0=OP.subtract,
                                    op1=OP.is_gt)
                    v.drain()
                    d = tm[64:128, 256:384]
                    v.tensor_tensor(out=d, in0=dots_ps[par][64:128, 0:128],
                                    in1=tm[64:128, 0:128], op=OP.mult)
                    v.drain()
                    v.scalar_tensor_tensor(out=d, in0=tm[64:128, 0:128],
                                           scalar=SHV, in1=d, op0=OP.mult,
                                           op1=OP.add)
                    v.drain()
                    v.scalar_tensor_tensor(out=d, in0=tm[64:128, 128:256],
                                           scalar=NEGB, in1=d, op0=OP.mult,
                                           op1=OP.add)
                    v.drain()
                    v.reduce_max(out=rcol[64:128, 0:1], in_=d, axis=AX.X,
                                 negate=True).then_inc(s_v.sem, 1)
                s_v.inc()

                @self.ae
                def _(a):
                    a.wait_ge(s_v.sem, s_v.v)
                    a.activation(out=tbf[64:128, 0:128],
                                 in_=self.tmp[64:128, 256:384], func=AF.Exp,
                                 bias=rcol[64:128, 0:1],
                                 accum_out=rcol[64:128, 1:2])
                    a.drain()
                    a.activation(out=rcol[64:128, 2:3], in_=rcol[64:128, 1:2],
                                 func=AF.Ln).then_inc(s_a.sem, 1)
                s_a.inc()

                @self.te
                def _(te):
                    te.wait_ge(s_a.sem, s_a.v)
                    te.transpose(self.psT[:, 0:64], tbf[64:128, 0:128],
                                 self.identb[64:128, 64:128],
                                 tile_position=(64, 0)).then_inc(s_pe.sem, 1)
                s_pe.inc()

                @self.ve
                def _(v):
                    v.wait_ge(s_pe.sem, s_pe.v)
                    v.tensor_copy(out=tbf[:, 896:960], in_=self.psT[:, 0:64]
                                  ).then_inc(s_v.sem, 1)
                s_v.inc()

                @self.te
                def _(te, par=par):
                    te.wait_ge(s_v.sem, s_v.v)
                    te.matmul(self.psO[64:128, 0:64], tbf[:, 896:960],
                              winb[par][:, 128:192], start=True,
                              stop=True, tile_position=(0, 64)
                              ).then_inc(s_pe.sem, 1)
                s_pe.inc()
                win_used[c] = s_pe.v

                @self.ve
                def _(v, c=c, so_prev=s_so.v):
                    v.wait_ge(s_pe.sem, s_pe.v)
                    if c % 16 == 0 and c > 0:
                        v.wait_ge(s_so.sem, so_prev)
                    ci = c % 16
                    v.reciprocal(out=rcol[64:128, 4:5], in_=rcol[64:128, 1:2])
                    v.tensor_tensor(out=rcol[64:128, 3:4],
                                    in0=rcol[64:128, 2:3],
                                    in1=rcol[64:128, 0:1], op=OP.subtract)
                    v.drain()
                    v.tensor_scalar(out=sostg[:, 72 * ci:72 * ci + 64],
                                    in0=self.psO[64:128, 0:64],
                                    scalar1=rcol[64:128, 4:5], scalar2=None,
                                    op0=OP.mult, op1=OP.bypass)
                    v.tensor_copy(out=sostg[:, 72 * ci + 64:72 * ci + 65],
                                  in_=rcol[64:128, 3:4]).then_inc(s_v.sem, 1)
                s_v.inc()

                if c % 16 == 15:
                    @self.sy
                    def _(sy, hs=hs, c=c):
                        sy.wait_ge(s_v.sem, s_v.v)
                        grp = c // 16
                        sy.dma_start(
                            out=dram["s_so%d" % hs][1024 * grp + 64:
                                                 1024 * (grp + 1) + 64]
                            .rearrange("(c q) f -> q c f", c=16),
                            in_=sostg[:].rearrange("p (c f) -> p c f", c=16)
                            ).then_inc(s_so.sem, 16)
                    s_so.inc(16)

                if c < 127:
                    if c + 2 <= 127:
                        emit_load(c + 2, sv0)
                    emit_front(c + 1, kn_ld)

            for tt in range(16):
                @self.ge
                def _(g, hs=hs, tt=tt, so_done=s_so.v, sv=s_v.v):
                    g.wait_ge(s_so.sem, so_done)
                    g.wait_ge(s_v.sem, sv)
                    for r in range(4):
                        g.indirect_dma_start(
                            out=ugb[:, r, :], out_offset=None,
                            in_=dram["s_so%d" % hs],
                            in_offset=bass.IndirectOffsetOnAxis(
                                ap=rkc[:, tt, r:r + 1], axis=0)
                            ).then_inc(s_ug.sem, 16)
                s_ug.inc(64)

                @self.ve
                def _(v):
                    v.wait_ge(s_ug.sem, s_ug.v)
                    for r in range(4):
                        v.tensor_copy(out=rcol[:, 4 + r:5 + r],
                                      in_=ugb[:, r, 64:65])
                    v.drain()
                    v.reduce_max(out=rcol[:, 0:1], in_=rcol[:, 4:8], axis=AX.X,
                                 negate=True).then_inc(s_v.sem, 1)
                s_v.inc()

                @self.ae
                def _(a):
                    a.wait_ge(s_v.sem, s_v.v)
                    a.activation(out=rcol[:, 4:8], in_=rcol[:, 4:8], func=AF.Exp,
                                 bias=rcol[:, 0:1], accum_out=rcol[:, 1:2]).then_inc(s_a.sem, 1)
                s_a.inc()

                @self.ve
                def _(v):
                    v.wait_ge(s_a.sem, s_a.v)
                    v.reciprocal(out=rcol[:, 2:3], in_=rcol[:, 1:2])
                    v.drain()
                    v.tensor_scalar(out=rcol[:, 4:8], in0=rcol[:, 4:8],
                                    scalar1=rcol[:, 2:3], scalar2=None,
                                    op0=OP.mult, op1=OP.bypass)
                    v.drain()
                    for r in range(4):
                        v.tensor_scalar(out=self.tmp[:, 64 * r:64 * r + 64],
                                        in0=ugb[:, r, 0:64],
                                        scalar1=rcol[:, 4 + r:5 + r],
                                        scalar2=None, op0=OP.mult,
                                        op1=OP.bypass)
                    v.drain()
                    v.tensor_tensor(out=self.tmp[:, 0:64], in0=self.tmp[:, 0:64],
                                    in1=self.tmp[:, 64:128], op=OP.add)
                    v.tensor_tensor(out=self.tmp[:, 128:192],
                                    in0=self.tmp[:, 128:192],
                                    in1=self.tmp[:, 192:256], op=OP.add)
                    v.drain()
                    v.tensor_tensor(out=self.tmp[:, 0:64], in0=self.tmp[:, 0:64],
                                    in1=self.tmp[:, 128:192],
                                    op=OP.add).then_inc(s_v.sem, 1)
                s_v.inc()

                @self.te
                def _(te):
                    te.wait_ge(s_v.sem, s_v.v)
                    te.transpose(self.psO[0:64, 0:128], self.tmp[:, 0:64],
                                 self.ident[:]).then_inc(s_pe.sem, 1)
                s_pe.inc()

                @self.ae
                def _(a, tt=tt, hsl=hsl, ch=ch):
                    a.wait_ge(s_pe.sem, s_pe.v)
                    a.copy(out=obuf[hsl, ch, 128 * tt:128 * (tt + 1)],
                           in_=self.psO[0:64, 0:128]).then_inc(s_a.sem, 1)
                s_a.inc()

    def _l1a_tail(self):
        x, dram = self.x, self.dram
        s_w, s_pe, s_v, s_a, s_ld = (self.s_w, self.s_pe, self.s_v, self.s_a,
                                     self.s_ld)
        obuf, tbf = self.obuf, self.tbf
        wout = self.av(96, [NCH, D], BF16)
        colv = self.av(112, [6, NCH])
        bff1 = self.av(112.25, [32])
        u = self.av(32, [NCH, T])
        x1 = self.av(0, [NCH, T], BF16)

        @self.ge
        def _(g):
            g.wait_ge(s_pe.sem, s_pe.v)
            g.dma_start(out=wout[:], in_=dram["l1a_wout"][:]).then_inc(s_w.sem, 16)
        s_w.inc(16)

        @self.sy
        def _(sy):
            sy.wait_ge(s_v.sem, s_v.v)
            sy.dma_start(out=colv[:], in_=dram["l1a_cols"][:]).then_inc(s_ld.sem, 16)
            sy.dma_start(out=bff1[:], in_=dram["l1a_bff1"][:]).then_inc(s_ld.sem, 16)
        s_ld.inc(32)

        for tg in range(4):
            sl = slice(512 * tg, 512 * (tg + 1))

            @self.te
            def _(te, sl=sl):
                te.wait_ge(s_v.sem, s_v.v)
                te.matmul(self.psA[:], self.ones1[0:1, :], self.R(0, sl),
                          start=True, stop=True)
                te.matmul(self.psB[:], self.ones1[32:33, :], self.R(1, sl),
                          start=True, stop=True).then_inc(s_pe.sem, 1)
            s_pe.inc()

            for o in range(NCH):
                @self.te
                def _(te, sl=sl, o=o):
                    te.wait_ge(s_w.sem, s_w.v)
                    te.wait_ge(s_v.sem, s_v.v)
                    te.wait_ge(self.s_a.sem, self.s_a.v)
                    for c in range(NCH):
                        mm = te.matmul(self.psO[:], wout[:, c, 128 * o:128 * (o + 1)],
                                       obuf[:, c, sl], start=(c == 0),
                                       stop=(c == NCH - 1))
                    mm.then_inc(s_pe.sem, 1)
                s_pe.inc()

                @self.ve
                def _(v, sl=sl, o=o):
                    v.wait_ge(s_pe.sem, s_pe.v)
                    v.wait_ge(s_ld.sem, s_ld.v)
                    t1 = self.tmp[:, 512:1024]
                    v.tensor_tensor(out=t1, in0=x[:, o, sl], in1=self.psA[:],
                                    op=OP.mult)
                    v.drain()
                    v.tensor_tensor(out=t1, in0=t1, in1=self.psB[:], op=OP.add)
                    v.drain()
                    v.tensor_tensor(out=t1, in0=t1, in1=self.psO[:], op=OP.add)
                    v.drain()
                    v.tensor_scalar(out=u[:, o, sl], in0=t1,
                                    scalar1=colv[:, 0, o:o + 1], scalar2=None,
                                    op0=OP.add, op1=OP.bypass).then_inc(s_v.sem, 1)
                s_v.inc()

        def ln_small(src):
            pe_at = {}
            for tg in range(4):
                sl = slice(512 * tg, 512 * (tg + 1))
                for c in range(NCH):
                    slot = c % 4

                    @self.ve
                    def _(v, sl=sl, c=c, slot=slot, tg=tg):
                        v.wait_ge(s_v.sem, s_v.v)
                        if (tg, c - 4) in pe_at:
                            v.wait_ge(s_pe.sem, pe_at[(tg, c - 4)])
                        elif tg > 0 and c < 4:
                            v.wait_ge(s_pe.sem, pe_at[(tg - 1, 4 + c)])
                        v.tensor_copy(out=tbf[:, 512 * slot:512 * (slot + 1)],
                                      in_=src[:, c, sl]).then_inc(s_v.sem, 1)
                    s_v.inc()

                    @self.ae
                    def _(a, sl=sl, c=c, slot=slot):
                        a.wait_ge(s_v.sem, s_v.v)
                        a.activation(out=tbf[:, 2048 + 512 * slot:2560 + 512 * slot],
                                     in_=src[:, c, sl], func=AF.Square).then_inc(s_a.sem, 1)
                    s_a.inc()

                    @self.te
                    def _(te, sl=sl, c=c, slot=slot):
                        te.wait_ge(s_v.sem, s_v.v)
                        te.wait_ge(s_a.sem, s_a.v)
                        te.matmul(self.psW[0:1, sl], self.onesb[:, 0:1],
                                  tbf[:, 512 * slot:512 * (slot + 1)],
                                  start=(c == 0), stop=(c == NCH - 1))
                        mm = te.matmul(self.psW[32:33, sl], self.onesb[:, 1:2],
                                       tbf[:, 2048 + 512 * slot:2560 + 512 * slot],
                                       start=(c == 0), stop=(c == NCH - 1))
                        mm.then_inc(s_pe.sem, 1)
                    s_pe.inc()
                    pe_at[(tg, c)] = s_pe.v
            self.rows_finish(r_mean=3, r_A=0, r_B=1)

        ln_small(u)

        for tg in range(4):
            sl = slice(512 * tg, 512 * (tg + 1))

            @self.te
            def _(te, sl=sl):
                te.wait_ge(s_v.sem, s_v.v)
                te.matmul(self.psA[:], self.ones1[0:1, :], self.R(0, sl),
                          start=True, stop=True)
                te.matmul(self.psB[:], self.ones1[32:33, :], self.R(1, sl),
                          start=True, stop=True).then_inc(s_pe.sem, 1)
            s_pe.inc()

            @self.ve
            def _(v, sl=sl):
                v.wait_ge(s_pe.sem, s_pe.v)
                for c in range(NCH):
                    t1 = self.tmp[:, 512:1024]
                    v.tensor_tensor(out=t1, in0=u[:, c, sl], in1=self.psA[:],
                                    op=OP.mult)
                    v.drain()
                    v.tensor_tensor(out=t1, in0=t1, in1=self.psB[:], op=OP.add)
                    v.drain()
                    v.tensor_scalar(out=x1[:, c, sl], in0=t1,
                                    scalar1=colv[:, 1, c:c + 1],
                                    scalar2=colv[:, 2, c:c + 1],
                                    op0=OP.mult, op1=OP.add)
                    v.drain()
                v.sem_inc(s_v.sem, 1)
            s_v.inc()

        if "x1" in self.taps:
            s_dx1 = self.sem("s_dx1")

            @self.sy
            def _(sy):
                sy.wait_ge(s_v.sem, s_v.v)
                sy.dma_start(out=dram["dbg_xa"][:], in_=x1[:]).then_inc(s_dx1.sem, 16)
                sy.wait_ge(s_dx1.sem, s_dx1.inc(16))

        wf1 = [self.av(96, [NCH, 128], BF16), self.av(98, [NCH, 128], BF16)]
        hstg = [self.av(100, [T], BF16), self.av(104, [T], BF16)]
        s_f1 = [self.sem("s_f1a"), self.sem("s_f1b")]
        s_h1 = [self.sem("s_h1a"), self.sem("s_h1b")]
        f1_used = {}
        gelu_at = {}
        hdma_at = {}

        for og in range(32):
            par = og % 2

            @self.ge
            def _(g, og=og, par=par):
                if og >= 2:
                    g.wait_ge(s_pe.sem, f1_used[og - 2])
                else:
                    g.wait_ge(s_v.sem, s_v.v)
                g.dma_start(out=wf1[par][:],
                            in_=dram["l1a_wff1"][:, :, 128 * og:128 * (og + 1)]
                            ).then_inc(s_f1[par].sem, 16)
            s_f1[par].inc(16)
            f1_ld = s_f1[par].v

            for tg in range(4):
                sl = slice(512 * tg, 512 * (tg + 1))

                @self.te
                def _(te, og=og, par=par, sl=sl, tg=tg, f1_ld=f1_ld):
                    te.wait_ge(s_f1[par].sem, f1_ld)
                    te.wait_ge(s_v.sem, s_v.v)
                    if (og - 1, tg) in gelu_at:
                        te.wait_ge(s_a.sem, gelu_at[(og - 1, tg)])
                    for c in range(NCH):
                        mm = te.matmul(self.psW[:, sl], wf1[par][:, c, :],
                                       x1[:, c, sl], start=(c == 0),
                                       stop=(c == NCH - 1))
                    mm.then_inc(s_pe.sem, 1)
                s_pe.inc()
                f1_used[og] = s_pe.v

                @self.ae
                def _(a, og=og, par=par, sl=sl, tg=tg):
                    a.wait_ge(s_pe.sem, s_pe.v)
                    if tg == 0 and og >= 2:
                        a.wait_ge(s_h1[par].sem, hdma_at[og - 2])
                    a.activation(out=hstg[par][:, sl], in_=self.psW[:, sl],
                                 func=AF.Gelu, bias=bff1[:, og:og + 1]).then_inc(s_a.sem, 1)
                s_a.inc()
                gelu_at[(og, tg)] = s_a.v

            @self.sy
            def _(sy, og=og, par=par):
                sy.wait_ge(s_a.sem, s_a.v)
                sy.dma_start(out=dram["s_h"][og], in_=hstg[par][:]
                             ).then_inc(s_h1[par].sem, 16)
            s_h1[par].inc(16)
            hdma_at[og] = s_h1[par].v

        acc = u
        for tg in range(4):
            sl = slice(512 * tg, 512 * (tg + 1))

            @self.ve
            def _(v, sl=sl):
                v.wait_ge(s_v.sem, s_v.v)
                for c in range(NCH):
                    v.tensor_scalar(out=acc[:, c, sl], in0=x1[:, c, sl],
                                    scalar1=colv[:, 5, c:c + 1], scalar2=None,
                                    op0=OP.add, op1=OP.bypass)
                v.drain()
                v.sem_inc(s_v.sem, 1)
            s_v.inc()

        def fftile(c, half):
            off = 256 * half
            if c < 4:
                return self.psW[:, 512 * c + off:512 * c + off + 256]
            if c < 6:
                return self.psA[:, 256 * (c - 4) + 0:256 * (c - 4) + 256]
            return self.psB[:, 256 * (c - 6) + 0:256 * (c - 6) + 256]

        wf2 = [self.av(96, [D], BF16), self.av(98, [D], BF16)]
        hsl2 = [self.av(100, [256], BF16), self.av(100.5, [256], BF16)]
        s_f2 = [self.sem("s_f2a"), self.sem("s_f2b")]
        s_hr = [self.sem("s_hra"), self.sem("s_hrb")]
        f2_used = {}
        hr_used = {}

        for tg in range(4):
            for half in range(2):
                tsl = slice(512 * tg + 256 * half, 512 * tg + 256 * (half + 1))
                for og in range(32):
                    par = og % 2
                    key = (tg, half, og)

                    @self.ge
                    def _(g, og=og, par=par, key=key):
                        kprev = (key[0], key[1], og - 2) if og >= 2 else None
                        if kprev in f2_used:
                            g.wait_ge(s_pe.sem, f2_used[kprev])
                        elif key[0] == 0 and key[1] == 0 and og < 2:
                            g.wait_ge(s_h1[0].sem, s_h1[0].v)
                            g.wait_ge(s_h1[1].sem, s_h1[1].v)
                            g.wait_ge(s_v.sem, s_v.v)
                        elif og < 2:
                            g.wait_ge(s_pe.sem, f2_used[(key[0], key[1] - 1, 30 + og)
                                      if key[1] else (key[0] - 1, 1, 30 + og)])
                        g.dma_start(out=wf2[par][:],
                                    in_=dram["l1a_wff2"][:, og, :]
                                    ).then_inc(s_f2[par].sem, 16)
                    s_f2[par].inc(16)
                    f2_ld = s_f2[par].v

                    @self.sy
                    def _(sy, og=og, par=par, key=key, tsl=tsl):
                        kprev = (key[0], key[1], og - 2) if og >= 2 else None
                        if kprev in hr_used:
                            sy.wait_ge(s_pe.sem, hr_used[kprev])
                        elif key[0] == 0 and key[1] == 0 and og < 2:
                            sy.wait_ge(s_h1[0].sem, s_h1[0].v)
                            sy.wait_ge(s_h1[1].sem, s_h1[1].v)
                            sy.wait_ge(s_v.sem, s_v.v)
                        elif og < 2:
                            sy.wait_ge(s_pe.sem, hr_used[(key[0], key[1] - 1, 30 + og)
                                       if key[1] else (key[0] - 1, 1, 30 + og)])
                        sy.dma_start(out=hsl2[par][:],
                                     in_=dram["s_h"][og][:, tsl]
                                     ).then_inc(s_hr[par].sem, 16)
                    s_hr[par].inc(16)
                    hr_ld = s_hr[par].v

                    @self.te
                    def _(te, og=og, par=par, half=half, f2_ld=f2_ld,
                          hr_ld=hr_ld):
                        te.wait_ge(s_f2[par].sem, f2_ld)
                        te.wait_ge(s_hr[par].sem, hr_ld)
                        if og == 0:
                            te.wait_ge(s_v.sem, s_v.v)
                        for c in range(NCH):
                            mm = te.matmul(fftile(c, half), wf2[par][:, 128 * c:
                                           128 * (c + 1)], hsl2[par][:],
                                           start=(og == 0), stop=(og == 31),
                                           skip_group_check=True)
                        mm.then_inc(s_pe.sem, 1)
                    s_pe.inc()
                    f2_used[key] = s_pe.v
                    hr_used[key] = s_pe.v

                @self.ve
                def _(v, tg=tg, half=half, tsl=tsl):
                    v.wait_ge(s_pe.sem, s_pe.v)
                    for c in range(NCH):
                        v.tensor_tensor(out=acc[:, c, tsl], in0=acc[:, c, tsl],
                                        in1=fftile(c, half), op=OP.add)
                    v.drain()
                    v.sem_inc(s_v.sem, 1)
                s_v.inc()

        ln_small(acc)
        for tg in range(4):
            sl = slice(512 * tg, 512 * (tg + 1))

            @self.te
            def _(te, sl=sl):
                te.wait_ge(s_v.sem, s_v.v)
                te.matmul(self.psA[:], self.ones1[0:1, :], self.R(0, sl),
                          start=True, stop=True)
                te.matmul(self.psB[:], self.ones1[32:33, :], self.R(1, sl),
                          start=True, stop=True).then_inc(s_pe.sem, 1)
            s_pe.inc()

            @self.ve
            def _(v, sl=sl):
                v.wait_ge(s_pe.sem, s_pe.v)
                for c in range(NCH):
                    t1 = self.tmp[:, 512:1024]
                    v.tensor_tensor(out=t1, in0=acc[:, c, sl], in1=self.psA[:],
                                    op=OP.mult)
                    v.drain()
                    v.tensor_tensor(out=t1, in0=t1, in1=self.psB[:], op=OP.add)
                    v.drain()
                    v.tensor_scalar(out=t1, in0=t1,
                                    scalar1=colv[:, 3, c:c + 1],
                                    scalar2=colv[:, 4, c:c + 1],
                                    op0=OP.mult, op1=OP.add)
                    v.drain()
                    v.tensor_tensor(out=x[:, c, sl], in0=x[:, c, sl], in1=t1,
                                    op=OP.add)
                    v.drain()
                v.sem_inc(s_v.sem, 1)
            s_v.inc()

    def l1_ffn(self):
        x, dram = self.x, self.dram
        s_ld, s_w, s_pe, s_v, s_a = (self.s_ld, self.s_w, self.s_pe, self.s_v,
                                     self.s_a)
        wr = self.av(0, [NCH, D], BF16)
        wk = self.av(16, [NCH, D], BF16)
        wv = self.av(32, [NCH, D], BF16)
        xr_bf = self.av(48, [NCH, 512], BF16)
        xk_bf = self.av(56, [NCH, 512], BF16)
        kact = self.av(64, [NCH, 512], BF16)
        r32 = self.av(72, [NCH, 512], F32)
        tail = self.av(104, [NCH, 2], F32)
        mu = self.av(105, [4, NCH], F32)

        self.ln_rows(xb=self.av(48, [NCH, T], BF16),
                     xsq=self.av(80, [NCH, T], BF16))

        @self.ge
        def _(g):
            g.wait_ge(s_v.sem, s_v.v)
            g.dma_start(out=wr[:], in_=dram["l1f_wr"][:]).then_inc(s_w.sem, 16)
            g.dma_start(out=wk[:], in_=dram["l1f_wk"][:]).then_inc(s_w.sem, 16)
            g.dma_start(out=wv[:], in_=dram["l1f_wv"][:]).then_inc(s_w.sem, 16)
        s_w.inc(48)

        @self.sy
        def _(sy):
            sy.wait_ge(s_v.sem, s_v.v)
            sy.dma_start(out=mu[:, 0, :], in_=dram["l1f_mur"][:]
                         ).then_inc(s_ld.sem, 16)
            sy.dma_start(out=mu[:, 2, :], in_=dram["l1f_muk"][:]
                         ).then_inc(s_ld.sem, 16)
        s_ld.inc(32)

        @self.ve
        def _(v):
            v.wait_ge(s_ld.sem, s_ld.v)
            v.wait_ge(s_v.sem, s_v.v)
            v.tensor_scalar(out=mu[:, 1, :], in0=mu[:, 0, :], scalar1=-1.0,
                            scalar2=1.0, op0=OP.mult, op1=OP.add)
            v.tensor_scalar(out=mu[:, 3, :], in0=mu[:, 2, :], scalar1=-1.0,
                            scalar2=1.0, op0=OP.mult, op1=OP.add)
            v.drain()
            v.memset(tail[:], 0.0).then_inc(s_v.sem, 1)
        s_v.inc()

        for tg in range(4):
            sl = slice(512 * tg, 512 * (tg + 1))

            @self.te
            def _(te, tg=tg, sl=sl):
                te.wait_ge(s_v.sem, s_v.v)
                pa, pb = self.Rp(0), self.Rp(1)
                te.matmul(self.psA[:], self.ones1[pa:pa + 1, :], self.R(0, sl),
                          start=True, stop=True)
                te.matmul(self.psB[:], self.ones1[pb:pb + 1, :], self.R(1, sl),
                          start=True, stop=True).then_inc(s_pe.sem, 1)
            s_pe.inc()

            @self.ve
            def _(v, tg=tg, sl=sl):
                v.wait_ge(s_pe.sem, s_pe.v)
                tpar = tg % 2
                for c in range(NCH):
                    v.tensor_tensor(out=self.tmp[:, 0:512], in0=x[:, c, sl],
                                    in1=self.psA[:], op=OP.mult)
                    v.drain()
                    v.tensor_tensor(out=self.tmp[:, 0:512], in0=self.tmp[:, 0:512],
                                    in1=self.psB[:], op=OP.add)
                    v.drain()
                    v.tensor_copy(out=self.tmp[:, 513:1024],
                                  in_=self.tmp[:, 0:511])
                    v.tensor_copy(out=self.tmp[:, 512:513],
                                  in_=tail[:, c, 0:1])
                    v.drain()
                    v.tensor_copy(out=tail[:, c, 0:1], in_=self.tmp[:, 511:512])
                    v.drain()
                    v.tensor_scalar(out=self.tmp[:, 1024:1536],
                                    in0=self.tmp[:, 0:512],
                                    scalar1=mu[:, 0, c:c + 1], scalar2=None,
                                    op0=OP.mult, op1=OP.bypass)
                    v.tensor_scalar(out=self.tmp[:, 1536:2048],
                                    in0=self.tmp[:, 512:1024],
                                    scalar1=mu[:, 1, c:c + 1], scalar2=None,
                                    op0=OP.mult, op1=OP.bypass)
                    v.drain()
                    v.tensor_tensor(out=xr_bf[:, c, :], in0=self.tmp[:, 1024:1536],
                                    in1=self.tmp[:, 1536:2048], op=OP.add)
                    v.drain()
                    v.tensor_scalar(out=self.tmp[:, 1024:1536],
                                    in0=self.tmp[:, 0:512],
                                    scalar1=mu[:, 2, c:c + 1], scalar2=None,
                                    op0=OP.mult, op1=OP.bypass)
                    v.tensor_scalar(out=self.tmp[:, 1536:2048],
                                    in0=self.tmp[:, 512:1024],
                                    scalar1=mu[:, 3, c:c + 1], scalar2=None,
                                    op0=OP.mult, op1=OP.bypass)
                    v.drain()
                    v.tensor_tensor(out=xk_bf[:, c, :], in0=self.tmp[:, 1024:1536],
                                    in1=self.tmp[:, 1536:2048], op=OP.add)
                    v.drain()
                v.sem_inc(s_v.sem, 1)
            s_v.inc()

            for o in range(NCH):
                @self.te
                def _(te, o=o):
                    te.wait_ge(s_v.sem, s_v.v)
                    te.wait_ge(s_w.sem, s_w.v)
                    if o > 0:
                        te.wait_ge(s_a.sem, s_a.v)
                    for c in range(NCH):
                        te.matmul(self.psA[:], wr[:, c, 128 * o:128 * (o + 1)],
                                  xr_bf[:, c, :], start=(c == 0),
                                  stop=(c == NCH - 1))
                    for c in range(NCH):
                        mm = te.matmul(self.psB[:], wk[:, c, 128 * o:128 * (o + 1)],
                                       xk_bf[:, c, :], start=(c == 0),
                                       stop=(c == NCH - 1))
                    mm.then_inc(s_pe.sem, 1)
                s_pe.inc()

                @self.ae
                def _(a, o=o):
                    a.wait_ge(s_pe.sem, s_pe.v)
                    a.activation(out=r32[:, o, :], in_=self.psA[:], func=AF.Sigmoid).then_inc(s_a.sem, 1)
                s_a.inc()

                @self.ve
                def _(v, o=o):
                    v.wait_ge(s_a.sem, s_a.v)
                    v.tensor_scalar(out=self.tmp[:, 0:512], in0=self.psB[:],
                                    scalar1=0.0, scalar2=None, op0=OP.max,
                                    op1=OP.bypass).then_inc(s_v.sem, 1)
                s_v.inc()

                @self.ae
                def _(a, o=o):
                    a.wait_ge(s_v.sem, s_v.v)
                    a.activation(out=kact[:, o, :], in_=self.tmp[:, 0:512],
                                 func=AF.Square).then_inc(s_a.sem, 1)
                s_a.inc()

            for o in range(NCH):
                @self.te
                def _(te, o=o):
                    te.wait_ge(s_a.sem, s_a.v)
                    if o > 0:
                        te.wait_ge(s_v.sem, s_v.v)
                    for c in range(NCH):
                        mm = te.matmul(self.psA[:], wv[:, c, 128 * o:128 * (o + 1)],
                                       kact[:, c, :], start=(c == 0),
                                       stop=(c == NCH - 1))
                    mm.then_inc(s_pe.sem, 1)
                s_pe.inc()

                @self.ve
                def _(v, o=o, sl=sl):
                    v.wait_ge(s_pe.sem, s_pe.v)
                    v.tensor_tensor(out=self.tmp[:, 0:512], in0=self.psA[:],
                                    in1=r32[:, o, :], op=OP.mult)
                    v.drain()
                    v.tensor_tensor(out=x[:, o, sl], in0=x[:, o, sl],
                                    in1=self.tmp[:, 0:512], op=OP.add).then_inc(s_v.sem, 1)
                s_v.inc()



def _prep(inputs):
    f32 = np.float32
    C = np.ascontiguousarray
    col = lambda b: C(np.asarray(b, f32).reshape(NCH, 128).T)

    wqkv = np.asarray(inputs["L0_attn_w_qkv"], f32)
    wv = C(wqkv[2048:].reshape(8, 128, NCH, 128).transpose(3, 2, 0, 1)
           .reshape(128, NCH, 1024))
    wqk_full = wqkv[:2048].reshape(2, 8, 128, NCH, 128)
    wqk_host = np.zeros((128, 16, NCH, 128), f32)
    for qi in range(2):
        for h in range(8):
            wqk_host[:, 2 * h + qi] = wqk_full[qi, h].transpose(2, 1, 0)
    wout = np.asarray(inputs["L0_attn_w_out"], f32)
    wout_host = C(wout.T.reshape(NCH, 128, D).transpose(1, 0, 2))

    def fm(w):
        w = np.asarray(w, f32)
        return C(w.T.reshape(NCH, 128, -1).transpose(1, 0, 2))

    gate_l = [np.asarray(inputs["L0_ffn_gate_w"], f32)[e] for e in range(8)] +              [np.asarray(inputs["L0_ffn_shared_gate_w"], f32)]
    up_l = [np.asarray(inputs["L0_ffn_up_w"], f32)[e] for e in range(8)] +            [np.asarray(inputs["L0_ffn_shared_up_w"], f32)]
    down_l = [np.asarray(inputs["L0_ffn_down_w"], f32)[e] for e in range(8)] +              [np.asarray(inputs["L0_ffn_shared_down_w"], f32)]
    gate_host = np.stack([g.T.reshape(NCH, 128, 4, 512).transpose(2, 1, 0, 3)
                          for g in gate_l])
    up_host = np.stack([g.T.reshape(NCH, 128, 4, 512).transpose(2, 1, 0, 3)
                        for g in up_l])
    down_host = np.stack([g.T.reshape(4, 4, 128, D).transpose(0, 2, 1, 3)
                          for g in down_l])
    pairs = [(e, e) for e in range(NCH)] +             [(e, f) for e in range(NCH) for f in range(e + 1, NCH)]
    sel_host = np.zeros((128, NCH, 36), f32)
    for j, (a, b) in enumerate(pairs):
        sel_host[:, a, j] = 1.0
        sel_host[:, b, j] = 1.0 if a != b else sel_host[0, b, j]
    for j, (a, b) in enumerate(pairs):
        if a != b:
            sel_host[:, a, j] = 1.0
            sel_host[:, b, j] = 1.0

    rr = np.asarray(inputs["L1_attn_rotations"], f32).reshape(64, 128)
    rot_host = C(np.concatenate([rr, rr], axis=0))
    cols_host = np.stack([col(inputs["L1_attn_" + k]) for k in
                          ("b_out", "ln1_g", "ln1_b", "ln2_g", "ln2_b", "b_ff2")],
                         axis=1)
    cols_host = C(cols_host)
    shared = {
        "l0a_wv": wv,
        "l0a_wqk": C(wqk_host),
        "l0a_gamma": col(inputs["L0_attn_gamma"]),
        "l0a_wout": wout_host,
        "l0f_ctxw": fm(inputs["L0_ffn_ctx_w"]),
        "l0f_ctxb": col(inputs["L0_ffn_ctx_b"]),
        "l0f_wih": fm(inputs["L0_ffn_gru_w_ih"]),
        "l0f_whh": fm(inputs["L0_ffn_gru_w_hh"]),
        "l0f_expr": fm(inputs["L0_ffn_expr_w"]),
        "l0f_gate": C(gate_host),
        "l0f_up": C(up_host),
        "l0f_down": C(down_host),
        "sel8": sel_host,
        "l1f_wr": fm(inputs["L1_ffn_W_r"]),
        "l1f_wk": fm(inputs["L1_ffn_W_k"]),
        "l1f_wv": fm(inputs["L1_ffn_W_v"]),
        "l1f_mur": col(inputs["L1_ffn_mu_r"]),
        "l1f_muk": col(inputs["L1_ffn_mu_k"]),
        "l1a_wqk": fm(inputs["L1_attn_w_qk"]),
        "l1a_wv": fm(inputs["L1_attn_w_v"]),
        "l1a_wout": fm(inputs["L1_attn_w_out"]),
        "l1a_rot": rot_host,
        "l1a_cols": cols_host,
        "l1a_bff1": C(np.asarray(inputs["L1_attn_b_ff1"], f32).reshape(32, 128).T),
        "l1a_wff1": fm(inputs["L1_attn_w_ff1"]),
        "l1a_wff2": C(np.asarray(inputs["L1_attn_w_ff2"], f32).T
                      .reshape(32, 128, D).transpose(1, 0, 2)),
    }
    x = np.asarray(inputs["L0_attn_x"], f32)
    maps = []
    for b in range(8):
        m = dict(shared)
        m["x0"] = C(x[b].T.reshape(NCH, 128, T).transpose(1, 0, 2))
        maps.append(m)
    return maps


def kernel(**inputs):
    nc = build_program()
    maps = _prep(inputs)
    res = run_bass_kernel_spmd(nc, maps, core_ids=list(range(8)))
    out = np.empty((8, T, D), np.float32)
    for b in range(8):
        y = res.results[b]["y"]
        out[b] = y.transpose(2, 1, 0).reshape(T, D)
    return out
